# version 41
# baseline (speedup 1.0000x reference)
import numpy as np
import ml_dtypes
import concourse.bass as bass
import concourse.mybir as mybir
from concourse.bass import IndirectOffsetOnAxis
from concourse.bass_utils import run_bass_kernel_spmd
from contextlib import ExitStack

F32 = mybir.dt.float32
BF16 = mybir.dt.bfloat16
U32 = mybir.dt.uint32
I32 = mybir.dt.int32
AF = mybir.ActivationFunctionType
ALU = mybir.AluOpType
AX = mybir.AxisListType

NCORE = 8
B = 2
TSEQ = 2048
T4 = B * TSEQ
NT = 32
D = 1024
E = 8
FF = 4096
PD = 256
CAP = 5121
NC41 = 41
RPAD = NC41 * 128
CHT = 384
NCH = 14
BIG = float(2 ** 30)
C23 = 12582912.0
EPS = 1e-5
DEBUG = False
ATT_BATCHES = None


class Sem:
    def __init__(self, ctx, nc, name):
        self.h = ctx.enter_context(nc.semaphore(name))
        self.n = 0

    def inc(self, inst, k=1):
        inst.then_inc(self.h, k)
        self.n += k

    def wait(self, eng):
        eng.wait_ge(self.h, self.n)

    def wait_at(self, eng, v):
        if v > 0:
            eng.wait_ge(self.h, v)


def build():
    nc = bass.Bass("TRN2", target_bir_lowering=False)
    x_in = nc.dram_tensor("x", [T4, D], F32, kind="ExternalInput").ap()
    qk_in = nc.dram_tensor("qk", [D, 2 * PD], F32, kind="ExternalInput").ap()
    wg_in = nc.dram_tensor("wg", [D, E], F32, kind="ExternalInput").ap()
    w1_in = nc.dram_tensor("w1", [D, FF], F32, kind="ExternalInput").ap()
    b1_in = nc.dram_tensor("b1", [128, 32], F32, kind="ExternalInput").ap()
    w2_in = nc.dram_tensor("w2", [FF, D], F32, kind="ExternalInput").ap()
    b2_in = nc.dram_tensor("b2", [1, D], F32, kind="ExternalInput").ap()
    premask_in = nc.dram_tensor("premask", [E, 1], F32, kind="ExternalInput").ap()
    myc_in = nc.dram_tensor("myc", [128, 1], F32, kind="ExternalInput").ap()
    ident_in = nc.dram_tensor("ident", [128, 128], BF16, kind="ExternalInput").ap()
    xo = nc.dram_tensor("xo", [T4, D], F32, kind="ExternalOutput").ap()
    if DEBUG:
        dbg_x1 = nc.dram_tensor("dbg_x1", [T4, D], F32, kind="ExternalOutput").ap()
        dbg_x2 = nc.dram_tensor("dbg_x2", [T4, D], F32, kind="ExternalOutput").ap()
        dbg_xn2 = nc.dram_tensor("dbg_xn2", [T4, D], BF16, kind="ExternalOutput").ap()
        dbg_xn3 = nc.dram_tensor("dbg_xn3", [T4, D], BF16, kind="ExternalOutput").ap()
        dbg_y = nc.dram_tensor("dbg_y", [RPAD, D], BF16, kind="ExternalOutput").ap()
        dbg_r = nc.dram_tensor("dbg_r", [128, 4 * NT], F32, kind="ExternalOutput").ap()
        dbg_g = nc.dram_tensor("dbg_g", [128, NC41], U32, kind="ExternalOutput").ap()
        dbg_z = nc.dram_tensor("dbg_z", [2, 128, 16], F32, kind="ExternalOutput").ap()
        dbg_kv = nc.dram_tensor("dbg_kv", [2, 128, 2048], BF16, kind="ExternalOutput").ap()
        dbg_qe = nc.dram_tensor("dbg_qe", [2, 128, 16 * PD], BF16, kind="ExternalOutput").ap()
        dbg_nm = nc.dram_tensor("dbg_nm", [2, 128, 1024], F32, kind="ExternalOutput").ap()
        dbg_t0 = nc.dram_tensor("dbg_t0", [3, 128, 1024], BF16, kind="ExternalOutput").ap()

    ag_xn_in = nc.dram_tensor("ag_xn_in", [T4, D], BF16)
    ag_xn_out = nc.dram_tensor("ag_xn_out", [NCORE * T4, D], BF16, addr_space="Shared")
    cnt_dram = nc.dram_tensor("cnt_dram", [1, E], F32)
    cnt_ag = nc.dram_tensor("cnt_ag", [NCORE, E], F32, addr_space="Shared")
    S_dram = nc.dram_tensor("S_dram", [E * RPAD, 1], I32)
    S_init = nc.dram_tensor("S_init", [E * RPAD, 1], I32)
    S_ag = nc.dram_tensor("S_ag", [NCORE * E * RPAD, 1], I32, addr_space="Shared")
    y_dA = nc.dram_tensor("y_dA", [3840, D], BF16)
    y_dB = nc.dram_tensor("y_dB", [1408, D], BF16)
    y_agA = nc.dram_tensor("y_agA", [NCORE * 3840, D], BF16, addr_space="Shared")
    y_agB = nc.dram_tensor("y_agB", [NCORE * 1408, D], BF16, addr_space="Shared")
    x1_dram = nc.dram_tensor("x1_dram", [T4, D], F32)
    x2_dram = nc.dram_tensor("x2_dram", [T4, D], F32)
    xn2_dram = nc.dram_tensor("xn2_dram", [T4, D], BF16)

    x_til = x_in.rearrange("(t p) d -> p t d", p=128)
    x1_til = x1_dram.ap().rearrange("(t p) d -> p t d", p=128)
    x2_til = x2_dram.ap().rearrange("(t p) d -> p t d", p=128)
    xo_til = xo.rearrange("(t p) d -> p t d", p=128)
    agxn_til = ag_xn_in.ap().rearrange("(t p) d -> p t d", p=128)
    xn2_til = xn2_dram.ap().rearrange("(t p) d -> p t d", p=128)
    y_tilA = y_dA.ap().rearrange("(c p) d -> p c d", p=128)
    y_tilB = y_dB.ap().rearrange("(c p) d -> p c d", p=128)
    S_rows = S_ag.ap().rearrange("(row c) one -> row (c one)", c=NC41)

    with ExitStack() as ctx:
        en = ctx.enter_context
        w1_sb = en(nc.sbuf_tensor("w1_sb", [128, 8, FF], BF16))
        w2_sb = en(nc.sbuf_tensor("w2_sb", [128, 32, D], BF16))
        wg_sb = en(nc.sbuf_tensor("wg_sb", [128, 8, E], BF16))
        b1_sb = en(nc.sbuf_tensor("b1_sb", [128, 32], F32))
        b2b = en(nc.sbuf_tensor("b2b", [128, D], F32))
        idn = en(nc.sbuf_tensor("idn", [128, 128], BF16))
        ones18 = en(nc.sbuf_tensor("ones18", [1, E], BF16))
        ones128b = en(nc.sbuf_tensor("ones128b", [128, 1], BF16))
        ohtm = en(nc.sbuf_tensor("ohtm", [128, 2, E], BF16))
        idc = en(nc.sbuf_tensor("idc", [128, 1], F32))
        ones8c = en(nc.sbuf_tensor("ones8c", [E, 1], F32))
        iota8c = en(nc.sbuf_tensor("iota8c", [E, 1], F32))
        iotaef = en(nc.sbuf_tensor("iotaef", [128, E], I32))
        locid = en(nc.sbuf_tensor("locid", [128, NT], I32))
        iota128 = en(nc.sbuf_tensor("iota128", [128, 1], I32))
        premask = en(nc.sbuf_tensor("premask_sb", [E, 1], F32))
        myc = en(nc.sbuf_tensor("myc_sb", [128, 1], F32))
        midxu = en(nc.sbuf_tensor("midxu", [128, E], U32))
        mbase = en(nc.sbuf_tensor("mbase", [128, 1], F32))
        xt = en(nc.sbuf_tensor("xt", [128, 2, D], F32))
        xnb = en(nc.sbuf_tensor("xnb", [128, 2, 2, D], BF16))
        xnf = en(nc.sbuf_tensor("xnf", [128, 2, 2, 8, 128], BF16, is_transpose_dest=True))
        bns = en(nc.sbuf_tensor("bns", [128, 2, 12], F32))
        mv = en(nc.sbuf_tensor("mv", [128, 2, 2], F32))
        rsd = en(nc.sbuf_tensor("rsd", [128, 2, 1], F32))
        nbi = en(nc.sbuf_tensor("nbi", [128, 2, 1], F32))
        g_u = en(nc.sbuf_tensor("g_u", [128, NC41], U32))
        gretu = en(nc.sbuf_tensor("gretu", [128, NT], U32))
        gretu2 = en(nc.sbuf_tensor("gretu2", [128, NT], U32))
        wsc = en(nc.sbuf_tensor("wsc", [128, NT], F32))
        cnt8 = en(nc.sbuf_tensor("cnt8", [E, 1], F32))
        cntall = en(nc.sbuf_tensor("cntall", [E, E], F32))
        base8 = en(nc.sbuf_tensor("base8", [E, 1], F32))
        mx8 = en(nc.sbuf_tensor("mx8", [128, 8], F32))
        ix8 = en(nc.sbuf_tensor("ix8", [128, 8], U32))
        idxf = en(nc.sbuf_tensor("idxf", [128, NT], BF16))
        idxw = en(nc.sbuf_tensor("idxw", [128, NT], F32))
        sh8 = en(nc.sbuf_tensor("sh8", [128, 2, E], F32))
        she = en(nc.sbuf_tensor("she", [128, 2, E], F32))
        esum = en(nc.sbuf_tensor("esum", [128, NT], F32))
        comb = en(nc.sbuf_tensor("comb", [128, NT], F32))
        rsel = en(nc.sbuf_tensor("rsel", [128, NT], F32))
        vld = en(nc.sbuf_tensor("vld", [128, NT], F32))
        scc = en(nc.sbuf_tensor("scc", [128, NT], F32))
        spp = en(nc.sbuf_tensor("spp", [128, NT], F32))
        offf = en(nc.sbuf_tensor("offf", [128, NT], F32))
        off2 = en(nc.sbuf_tensor("off2", [128, NT], F32))
        offu = en(nc.sbuf_tensor("offu", [128, NT], U32))
        gretf = en(nc.sbuf_tensor("gretf", [128, NT], F32))

        s_ld = Sem(ctx, nc, "s_ld")
        s_ld0 = Sem(ctx, nc, "s_ld0")
        s_wc = Sem(ctx, nc, "s_wc")
        s_tr = Sem(ctx, nc, "s_tr")
        s_sst = Sem(ctx, nc, "s_sst")
        s_gd = Sem(ctx, nc, "s_gd")
        s_gc = Sem(ctx, nc, "s_gc")
        s_v = Sem(ctx, nc, "s_v")
        s_a = Sem(ctx, nc, "s_a")
        s_t = Sem(ctx, nc, "s_t")
        s_fill = Sem(ctx, nc, "s_fill")
        s_m2 = Sem(ctx, nc, "s_m2")
        s_cc = [Sem(ctx, nc, f"s_cc{i}") for i in range(8)]
        s_ccy = [Sem(ctx, nc, f"s_ccy{i}") for i in range(6)]

        block = en(nc.Block())
        st = {"xnst": 0}

        c0 = ctx.enter_context(ExitStack())
        b2r = c0.enter_context(nc.sbuf_tensor("b2r", [1, D], F32))
        neg1 = c0.enter_context(nc.sbuf_tensor("neg1", [128, E * NC41], I32))
        ones1 = c0.enter_context(nc.sbuf_tensor("ones1", [1, 128], F32))

        @block.sync
        def _(sync):
            s_ld0.inc(sync.dma_start(out=b1_sb[:, :], in_=b1_in[:, :]), 16)
            s_ld0.inc(sync.dma_start(out=b2r[:, :], in_=b2_in[:, :]), 16)
            s_ld0.inc(sync.dma_start(out=idn[:, :], in_=ident_in[:, :]), 16)
            s_ld0.inc(sync.dma_start(out=premask[:, :], in_=premask_in[:, :]), 16)
            s_ld0.inc(sync.dma_start(out=myc[:, :], in_=myc_in[:, :]), 16)

        @block.gpsimd
        def _(g):
            g.iota(iotaef[:, :], pattern=[[1, E]], base=0, channel_multiplier=0)
            g.iota(iota8c[:, :], pattern=[[1, 1]], base=0, channel_multiplier=1,
                   allow_small_or_imprecise_dtypes=True)
            g.iota(locid[:, :], pattern=[[NCORE * 512, NT // 4], [128, 4]], base=0,
                   channel_multiplier=1)
            g.iota(iota128[:, :], pattern=[[1, 1]], base=0, channel_multiplier=1)
            g.memset(neg1[:, :], -1)
            g.memset(ones1[:, :], 1.0)
            g.memset(ones18[:, :], 1.0)
            g.memset(ones8c[:, :], 1.0)
            g.memset(ones128b[:, :], 1.0)
            st["rb_slot"] = g.to_reg(E * RPAD - 1)
            st["rb_srow"] = g.to_reg(NCORE * E * 128 - 1)
            st["rb_xn"] = g.to_reg(NCORE * T4 - 1)
            st["rb_yA"] = g.to_reg(NCORE * 3840 - 1)
            st["rb_yB"] = g.to_reg(NCORE * 1408 - 1)
            s_gc.inc(g.drain())
            s_wc.inc(g.dma_start(out=wg_sb[:, :, :], in_=wg_in.rearrange("(dc p) e -> p dc e", p=128)), 16)
            s_wc.inc(g.dma_start(out=w1_sb[:, :, :], in_=w1_in.rearrange("(dc p) f -> p dc f", p=128)), 16)
            s_wc.inc(g.dma_start(out=w2_sb[:, :, :], in_=w2_in.rearrange("(fc p) d -> p fc d", p=128)), 16)
            s_gd.inc(g.dma_start(out=S_dram.ap().rearrange("(e p c) one -> p e (c one)", e=E, p=128),
                                 in_=neg1[:, :].rearrange("p (e c) -> p e c", e=E)), 16)
            s_gd.inc(g.dma_start(out=S_init.ap().rearrange("(e p c) one -> p e (c one)", e=E, p=128),
                                 in_=neg1[:, :].rearrange("p (e c) -> p e c", e=E)), 16)

        @block.vector
        def _(v):
            s_ld0.wait(v)
            s_gc.wait_at(v, 1)
            v.scalar_tensor_tensor(mbase[:, :], myc[:, :], 128.0, iota128[:, :], ALU.mult, ALU.add)
            v.drain()
            for r in range(NCORE):
                v.tensor_scalar(offf[:, r:r + 1], mbase[:, :], float(r * 1024), None, ALU.add)
            v.drain()
            s_v.inc(v.tensor_copy(midxu[:, :], offf[:, 0:E]))

        ps_tmp = c0.enter_context(nc.psum_tensor("ps_b2", [128, 512], F32))
        for dh in range(2):
            @block.tensor
            def _(t_, dh=dh):
                if dh == 0:
                    s_ld0.wait(t_)
                    s_v.wait(t_)
                else:
                    s_v.wait(t_)
                s_t.inc(t_.matmul(ps_tmp[:, :], ones1[:, :], b2r[:, dh * 512:(dh + 1) * 512],
                                  start=True, stop=True))

            @block.vector
            def _(v, dh=dh):
                s_t.wait(v)
                s_v.inc(v.tensor_copy(b2b[:, dh * 512:(dh + 1) * 512], ps_tmp[:, :]))
        c0.close()

        def emit_routing(call):
            with ExitStack() as rc:
                er = rc.enter_context
                sfx = f"_r{call}"
                idxrow = er(nc.sbuf_tensor("idxrow" + sfx, [1, T4], BF16))
                ohT = er(nc.sbuf_tensor("ohT" + sfx, [E, T4], BF16))
                rT = er(nc.sbuf_tensor("rT" + sfx, [E, T4], F32))
                prodc = er(nc.sbuf_tensor("prodc" + sfx, [E, 2, 512], F32))
                Ssb = er(nc.sbuf_tensor("Ssb" + sfx, [128, E, NC41], I32))
                mrg_m = er(nc.sbuf_tensor("mrg_m" + sfx, [128, NC41], F32))
                mrg_t = er(nc.sbuf_tensor("mrg_t" + sfx, [128, NC41], F32))
                gacc = er(nc.sbuf_tensor("gacc" + sfx, [128, NC41], F32))
                ps_lg = er(nc.psum_tensor("ps_lg" + sfx, [128, 2, E], F32))
                ps_row = er(nc.psum_tensor("ps_row" + sfx, [1, 2, 512], BF16))
                ps_bc = er(nc.psum_tensor("ps_bc" + sfx, [E, 2, 512], F32))
                ps_b8 = er(nc.psum_tensor("ps_b8" + sfx, [E, 1], F32))
                ps_cnt = er(nc.psum_tensor("ps_cnt" + sfx, [E, 1], F32))
                ps_rs = er(nc.psum_tensor("ps_rs" + sfx, [128, NT], F32))

                xnst0 = s_sst.n
                cc_cnt = s_cc[call * 4 + 0]
                cc_xn = s_cc[call * 4 + 1]
                if call == 0:
                    @block.gpsimd
                    def _(g):
                        for k in range(8):
                            g.wait_ge(s_sst.h, xnst0 + 32 * (2 * k + 2))
                            cc_xn.inc(g.collective_compute(
                                "AllGather", ALU.bypass, replica_groups=[list(range(NCORE))],
                                ins=[ag_xn_in.ap()[512 * k:512 * (k + 1), :].opt()],
                                outs=[ag_xn_out.ap()[NCORE * 512 * k:NCORE * 512 * (k + 1), :].opt()]))

                    for gp in range(NT // 2):
                        t0, t1 = 2 * gp, 2 * gp + 1

                        @block.sync
                        def _(sync, gp=gp, t0=t0, t1=t1, call=call):
                            if gp == 0:
                                if call == 1:
                                    s_t.wait(sync)
                                    s_v.wait(sync)
                                    s_a.wait(sync)
                                    s_sst.wait(sync)
                            if call == 0:
                                if gp >= 1:
                                    s_v.wait_at(sync, st[f"lnap_{call}_{gp-1}"])
                                s_ld.inc(sync.dma_start(out=xt[:, 0, :], in_=x_til[:, t0, :]), 16)
                                s_ld.inc(sync.dma_start(out=xt[:, 1, :], in_=x_til[:, t1, :]), 16)
                            else:
                                if gp >= 1:
                                    s_tr.wait_at(sync, st[f"trpp_{call}_{gp-1}"])
                                s_ld.inc(sync.dma_start(out=xnb[:, 0, 0, :], in_=agxn_til[:, t0, :]), 16)
                                s_ld.inc(sync.dma_start(out=xnb[:, 0, 1, :], in_=agxn_til[:, t1, :]), 16)

                        if call == 0:
                            @block.vector
                            def _(v, gp=gp, t0=t0, t1=t1):
                                s_ld.wait(v)
                                if gp >= 1:
                                    s_tr.wait_at(v, st[f"trpp_0_{gp-1}"])
                                    s_sst.wait_at(v, xnst0 + 32 * gp)
                                v.bn_stats(bns[:, 0, 0:6], xt[:, 0, 0:512])
                                v.bn_stats(bns[:, 0, 6:12], xt[:, 0, 512:1024])
                                v.bn_stats(bns[:, 1, 0:6], xt[:, 1, 0:512])
                                v.bn_stats(bns[:, 1, 6:12], xt[:, 1, 512:1024])
                                v.drain()
                                v.bn_aggr(mv[:, 0, :], bns[:, 0, :])
                                v.bn_aggr(mv[:, 1, :], bns[:, 1, :])
                                v.drain()
                                s_v.inc(v.tensor_scalar(rsd[:, :, :], mv[:, :, 1:2], EPS, None, ALU.add))

                            @block.scalar
                            def _(sc, gp=gp):
                                s_v.wait(sc)
                                sc.activation(out=rsd[:, :, :], in_=rsd[:, :, :], func=AF.Ln)
                                sc.drain()
                                s_a.inc(sc.activation(out=rsd[:, :, :], in_=rsd[:, :, :], func=AF.Exp, scale=-0.5))

                            @block.vector
                            def _(v, gp=gp):
                                s_a.wait(v)
                                v.tensor_scalar(xnb[:, 0, 0, :], xt[:, 0, :], mv[:, 0, 0:1],
                                                rsd[:, 0, :], ALU.subtract, ALU.mult)
                                s_v.inc(v.tensor_scalar(xnb[:, 0, 1, :], xt[:, 1, :], mv[:, 1, 0:1],
                                                        rsd[:, 1, :], ALU.subtract, ALU.mult))
                                st[f"lnap_0_{gp}"] = s_v.n

                            @block.scalar
                            def _(sc, gp=gp, t0=t0, t1=t1):
                                s_v.wait(sc)
                                s_sst.inc(sc.dma_start(out=agxn_til[:, t0, :], in_=xnb[:, 0, 0, :]), 16)
                                s_sst.inc(sc.dma_start(out=agxn_til[:, t1, :], in_=xnb[:, 0, 1, :]), 16)

                        @block.sync
                        def _(sync, gp=gp, call=call):
                            if call == 0:
                                s_v.wait(sync)
                            else:
                                s_ld.wait(sync)
                            if gp >= 1:
                                s_t.wait_at(sync, st[f"rtp_{call}_{gp-1}"])
                            s_tr.inc(sync.dma_start(out=xnf[:, 0, 0, :, :], in_=xnb[:, 0, 0, :], transpose=True), 16)
                            s_tr.inc(sync.dma_start(out=xnf[:, 0, 1, :, :], in_=xnb[:, 0, 1, :], transpose=True), 16)
                            st[f"trpp_{call}_{gp}"] = s_tr.n

                        @block.tensor
                        def _(t_, gp=gp, call=call):
                            s_tr.wait(t_)
                            if gp == 0:
                                s_wc.wait_at(t_, 16)
                            if gp >= 1:
                                s_v.wait_at(t_, st[f"shp_{call}_{gp-1}"])
                            inst = None
                            for par in range(2):
                                for dc in range(8):
                                    inst = t_.matmul(ps_lg[:, par, :], xnf[:, 0, par, dc, :], wg_sb[:, dc, :],
                                                     start=(dc == 0), stop=(dc == 7))
                            s_t.inc(inst)
                            st[f"rtp_{call}_{gp}"] = s_t.n

                        @block.vector
                        def _(v, gp=gp, t0=t0, t1=t1, call=call):
                            s_t.wait(v)
                            if gp >= 1:
                                s_a.wait_at(v, st[f"expp_{call}_{gp-1}"])
                                s_t.wait_at(v, st[f"cntp_{call}_{gp-1}"])
                            for par, tt in ((0, t0), (1, t1)):
                                v.max(mx8[:, :], ps_lg[:, par, :])
                                v.drain()
                                v.max_index(ix8[:, :], mx8[:, :], ps_lg[:, par, :])
                                v.drain()
                                v.tensor_copy(idxf[:, tt:tt + 1], ix8[:, 0:1])
                                v.tensor_copy(idc[:, :], ix8[:, 0:1])
                                v.drain()
                                v.tensor_scalar(ohtm[:, par, :], iotaef[:, :], idc[:, :], None, ALU.is_equal)
                                inst = v.tensor_scalar(sh8[:, par, :], ps_lg[:, par, :], mx8[:, 0:1], None, ALU.subtract)
                                v.drain()
                            s_v.inc(inst)
                            st[f"shp_{call}_{gp}"] = s_v.n

                        @block.scalar
                        def _(sc, gp=gp, t0=t0, t1=t1, call=call):
                            s_v.wait(sc)
                            sc.activation(out=she[:, 0, :], in_=sh8[:, 0, :], func=AF.Exp,
                                          accum_out=esum[:, t0:t0 + 1])
                            s_a.inc(sc.activation(out=she[:, 1, :], in_=sh8[:, 1, :], func=AF.Exp,
                                                  accum_out=esum[:, t1:t1 + 1]))
                            st[f"expp_{call}_{gp}"] = s_a.n

                        @block.tensor
                        def _(t_, gp=gp, call=call):
                            s_v.wait(t_)
                            inst = None
                            for par in range(2):
                                inst = t_.matmul(ps_cnt[:, :], ohtm[:, par, :], ones128b[:, :],
                                                 start=(gp == 0 and par == 0),
                                                 stop=(gp == NT // 2 - 1 and par == 1))
                            s_t.inc(inst)
                            st[f"cntp_{call}_{gp}"] = s_t.n
                            if gp == NT // 2 - 1:
                                st[f"cntmm_{call}"] = s_t.n

                    @block.vector
                    def _(v, call=call):
                        s_t.wait(v)
                        s_v.inc(v.tensor_copy(cnt8[:, :], ps_cnt[:, :]))
                        st[f"cntev_{call}"] = s_v.n
                        s_a.wait(v)
                        v.reciprocal(comb[:, :], esum[:, :])
                        v.drain()
                        s_v.inc(v.tensor_copy(idxw[:, :], idxf[:, :]))

                    @block.gpsimd
                    def _(g, call=call):
                        g.wait_ge(s_v.h, st[f"cntev_{call}"])
                        s_gd.inc(g.dma_start(out=cnt_dram.ap(), in_=cnt8[:, :]), 16)
                        s_gd.wait(g)
                        cc_cnt.inc(g.collective_compute(
                            "AllGather", ALU.bypass, replica_groups=[list(range(NCORE))],
                            ins=[cnt_dram.ap().opt()], outs=[cnt_ag.ap().opt()]))

                    for q in range(8):
                        @block.tensor
                        def _(t_, q=q):
                            s_v.wait(t_)
                            inst = None
                            for j in range(4):
                                tl = q * 4 + j
                                inst = t_.transpose(ps_row[:, q % 2, j * 128:(j + 1) * 128],
                                                    idxf[:, tl:tl + 1], idn[:, :])
                            s_t.inc(inst)

                        @block.vector
                        def _(v, q=q):
                            s_t.wait(v)
                            if q == 0:
                                s_gd.wait(v)
                            s_v.inc(v.tensor_copy(idxrow[:, q * 512:(q + 1) * 512], ps_row[:, q % 2, :]))

                    for q in range(8):
                        @block.tensor
                        def _(t_, q=q):
                            s_v.wait(t_)
                            s_t.inc(t_.matmul(ps_bc[:, q % 2, :], ones18[:, :], idxrow[:, q * 512:(q + 1) * 512],
                                              start=True, stop=True))

                        @block.vector
                        def _(v, q=q):
                            s_t.wait(v)
                            s_v.inc(v.tensor_scalar(ohT[:, q * 512:(q + 1) * 512], ps_bc[:, q % 2, :],
                                                    iota8c[:, :], None, ALU.is_equal))


                @block.sync
                def _(sync):
                    cc_cnt.wait_at(sync, 1)
                    s_ld.inc(sync.dma_start(out=cntall[:, :], in_=cnt_ag.ap()[:, :]), 16)

                @block.tensor
                def _(t_):
                    s_ld.wait(t_)
                    s_t.inc(t_.matmul(ps_b8[:, :], cntall[:, :], premask[:, :], start=True, stop=True))

                @block.vector
                def _(v):
                    s_t.wait(v)
                    v.tensor_copy(base8[:, :], ps_b8[:, :])
                    v.drain()
                    s_v.inc(v.tensor_tensor_scan(rT[:, :], ohT[:, :], ohT[:, :], base8[:, :],
                                                 ALU.add, ALU.max))

                for q in range(8):
                    @block.vector
                    def _(v, q=q):
                        if q >= 2:
                            s_t.wait_at(v, st[f"rsmm_{call}_{q-2}"])
                        v.drain()
                        s_v.inc(v.tensor_tensor(prodc[:, q % 2, :], ohT[:, q * 512:(q + 1) * 512],
                                                rT[:, q * 512:(q + 1) * 512], ALU.mult))

                    @block.tensor
                    def _(t_, q=q, call=call):
                        s_v.wait(t_)
                        inst = None
                        for j in range(4):
                            tl = q * 4 + j
                            inst = t_.matmul(ps_rs[:, tl:tl + 1], prodc[:, q % 2, j * 128:(j + 1) * 128],
                                             ones8c[:, :], start=True, stop=True)
                        s_t.inc(inst)
                        st[f"rsmm_{call}_{q}"] = s_t.n

                @block.vector
                def _(v):
                    s_t.wait(v)
                    v.tensor_copy(rsel[:, :], ps_rs[:, :])
                    v.drain()
                    v.tensor_scalar(vld[:, :], rsel[:, :], float(CAP) + 0.5, None, ALU.is_le)
                    v.drain()
                    v.tensor_tensor(wsc[:, :], comb[:, :], vld[:, :], ALU.mult)
                    v.drain()
                    v.tensor_scalar(wsc[:, :], wsc[:, :], 0.5, None, ALU.mult)
                    v.tensor_scalar(off2[:, :], rsel[:, :], -1.0, 5120.0, ALU.add, ALU.min)
                    v.drain()
                    v.tensor_scalar(offf[:, :], off2[:, :], 3839.5, None, ALU.is_ge)
                    v.tensor_scalar(scc[:, :], off2[:, :], 4607.5, None, ALU.is_ge)
                    v.scalar_tensor_tensor(gretf[:, :], idxw[:, :], 3840.0, off2[:, :], ALU.mult, ALU.add)
                    v.tensor_scalar(spp[:, :], idxw[:, :], -128.0, 5376.0, ALU.mult, ALU.add)
                    v.drain()
                    v.scalar_tensor_tensor(gretf[:, :], offf[:, :], BIG, gretf[:, :], ALU.mult, ALU.add)
                    v.scalar_tensor_tensor(off2[:, :], idxw[:, :], 768.0, off2[:, :], ALU.mult, ALU.add)
                    v.tensor_tensor(scc[:, :], scc[:, :], spp[:, :], ALU.mult)
                    v.drain()
                    s_v.inc(v.tensor_copy(gretu[:, :], gretf[:, :]))
                    v.tensor_scalar(off2[:, :], off2[:, :], -3840.0, None, ALU.add)
                    v.tensor_scalar(offf[:, :], offf[:, :], -1.0, 1.0, ALU.mult, ALU.add)
                    v.drain()
                    v.tensor_tensor(off2[:, :], off2[:, :], scc[:, :], ALU.add)
                    v.drain()
                    v.scalar_tensor_tensor(off2[:, :], offf[:, :], BIG, off2[:, :], ALU.mult, ALU.add)
                    v.drain()
                    s_v.inc(v.tensor_copy(gretu2[:, :], off2[:, :]))
                    v.tensor_scalar(spp[:, :], rsel[:, :], -1.0, None, ALU.add)
                    v.drain()
                    v.tensor_scalar(scc[:, :], spp[:, :], 1.0 / 128.0, C23, ALU.mult, ALU.add)
                    v.drain()
                    v.tensor_scalar(scc[:, :], scc[:, :], -C23, None, ALU.add)
                    v.drain()
                    v.scalar_tensor_tensor(spp[:, :], scc[:, :], -128.0, spp[:, :], ALU.mult, ALU.add)
                    v.drain()
                    v.tensor_scalar(offf[:, :], spp[:, :], 0.0, None, ALU.is_lt)
                    v.drain()
                    v.tensor_tensor(scc[:, :], scc[:, :], offf[:, :], ALU.subtract)
                    v.scalar_tensor_tensor(spp[:, :], offf[:, :], 128.0, spp[:, :], ALU.mult, ALU.add)
                    v.drain()
                    v.scalar_tensor_tensor(off2[:, :], spp[:, :], 41.0, scc[:, :], ALU.mult, ALU.add)
                    v.drain()
                    v.scalar_tensor_tensor(offf[:, :], idxw[:, :], float(RPAD), off2[:, :], ALU.mult, ALU.add)
                    v.drain()
                    v.tensor_scalar(off2[:, :], vld[:, :], -BIG, BIG, ALU.mult, ALU.add)
                    v.drain()
                    v.tensor_tensor(offf[:, :], offf[:, :], off2[:, :], ALU.add)
                    v.drain()
                    s_v.inc(v.tensor_copy(offu[:, :], offf[:, :]))

                cc_S = s_cc[call * 4 + 2]

                @block.gpsimd
                def _(g, call=call):
                    s_v.wait(g)
                    s_gd.wait(g)
                    for t in range(NT):
                        s_gd.inc(g.indirect_dma_start(
                            out=S_dram.ap(),
                            out_offset=IndirectOffsetOnAxis(offu[:, t:t + 1], 0),
                            in_=locid[:, t:t + 1], in_offset=None,
                            bounds_check=st["rb_slot"], oob_is_err=False,
                        ), 16)
                    g.wait_ge(s_gd.h, s_gd.n)
                    cc_S.inc(g.collective_compute(
                        "AllGather", ALU.bypass, replica_groups=[list(range(NCORE))],
                        ins=[S_dram.ap().opt()], outs=[S_ag.ap().opt()]))
                    cc_S.wait_at(g, 1)
                    for r in range(NCORE):
                        s_gd.inc(g.indirect_dma_start(
                            out=Ssb[:, r, :], out_offset=None,
                            in_=S_rows, in_offset=IndirectOffsetOnAxis(midxu[:, r:r + 1], 0),
                            bounds_check=st["rb_srow"], oob_is_err=False,
                        ), 16)

                @block.vector
                def _(v):
                    s_gd.wait(v)
                    for r in range(NCORE):
                        v.tensor_scalar(mrg_m[:, :], Ssb[:, r, :], -0.5, None, ALU.is_ge)
                        v.tensor_scalar(mrg_t[:, :], Ssb[:, r, :], float(r * 512), None, ALU.add)
                        v.drain()
                        if r == 0:
                            v.tensor_tensor(gacc[:, :], mrg_t[:, :], mrg_m[:, :], ALU.mult)
                        else:
                            v.tensor_tensor(mrg_t[:, :], mrg_t[:, :], mrg_m[:, :], ALU.mult)
                            v.drain()
                            v.tensor_tensor(gacc[:, :], gacc[:, :], mrg_t[:, :], ALU.add)
                        v.drain()
                    s_v.inc(v.tensor_copy(g_u[:, :], gacc[:, :]))
                    st[f"routing_done_{call}"] = s_v.n

        def emit_expert(call):
            with ExitStack() as fc:
                ef = fc.enter_context
                sfx = f"_f{call}"
                xg = ef(nc.sbuf_tensor("xg" + sfx, [128, 2, 3, D], BF16))
                xgf = ef(nc.sbuf_tensor("xgf" + sfx, [128, 2, 3, 8, 128], BF16, is_transpose_dest=True))
                h_sb = ef(nc.sbuf_tensor("h_sb" + sfx, [128, 16, CHT], BF16))
                y_sb = ef(nc.sbuf_tensor("y_sb" + sfx, [128, 3, D], BF16))
                ps_ha = [ef(nc.psum_tensor(f"ps_h{i}" + sfx, [128, CHT], F32)) for i in range(2)]
                ps_y = [[ef(nc.psum_tensor(f"ps_y{stt}{dh}" + sfx, [128, 512], F32)) for dh in range(2)]
                        for stt in range(3)]

                fill0 = s_fill.n
                tr0 = s_tr.n
                m20 = s_m2.n
                ev0 = s_v.n
                st0 = s_sst.n
                st[f"yst_{call}_base"] = st0
                routing_done = st[f"routing_done_{call}"]
                cc_xn = s_cc[call * 4 + 1]
                cc_y = s_cc[call * 4 + 3]

                ntile = NC41
                chunks = [(ch, min(3, ntile - ch * 3)) for ch in range(NCH)]

                for ch, nk in chunks:
                    par = ch % 2

                    @block.gpsimd
                    def _(g, ch=ch, nk=nk, par=par):
                        if ch == 0:
                            s_v.wait_at(g, routing_done)
                            cc_xn.wait_at(g, 8)
                        if ch >= 2:
                            s_tr.wait_at(g, st[f"xtr_{call}_{ch-2}"])
                        for k in range(nk):
                            s_gd.inc(g.indirect_dma_start(
                                out=xg[:, par, k, :], out_offset=None,
                                in_=ag_xn_out.ap(), in_offset=IndirectOffsetOnAxis(
                                    g_u[:, ch * 3 + k:ch * 3 + k + 1], 0),
                                bounds_check=st["rb_xn"], oob_is_err=False,
                            ), 16)

                    @block.sync
                    def _(sync, ch=ch, nk=nk, par=par):
                        s_gd.wait(sync)
                        if ch >= 2:
                            s_fill.wait_at(sync, fill0 + 32 * (ch - 1))
                        inst = None
                        for k in range(nk):
                            s_tr.inc(sync.dma_start(out=xgf[:, par, k, :, :], in_=xg[:, par, k, :],
                                                    transpose=True), 16)
                        st[f"xtr_{call}_{ch}"] = s_tr.n

                    if ch == 11:
                        @block.gpsimd
                        def _(g, call=call):
                            g.wait_ge(s_sst.h, st[f"yst_{call}_base"] + 10 * 16)
                            s_ccy[call * 3].inc(g.collective_compute(
                                "AllGather", ALU.bypass, replica_groups=[list(range(NCORE))],
                                ins=[y_dA.ap().opt()], outs=[y_agA.ap().opt()]))
                    if ch == 13:
                        @block.gpsimd
                        def _(g, call=call):
                            g.wait_ge(s_sst.h, st[f"yst_{call}_base"] + 12 * 16)
                            s_ccy[call * 3 + 1].inc(g.collective_compute(
                                "AllGather", ALU.bypass, replica_groups=[list(range(NCORE))],
                                ins=[y_dB.ap()[0:768, :].opt()],
                                outs=[y_agB.ap()[0:NCORE * 768, :].opt()]))

                    for fh in range(2):
                        fillv = []
                        abase = s_a.n

                        for ft in range(16):
                            fg = fh * 16 + ft

                            @block.tensor
                            def _(t_, ch=ch, nk=nk, par=par, fh=fh, ft=ft, fg=fg, abase=abase):
                                if fh == 0 and ft == 0:
                                    s_tr.wait(t_)
                                    if ch == 0:
                                        s_wc.wait(t_)
                                if ft >= 2:
                                    s_a.wait_at(t_, abase + ft - 1)
                                elif s_a.n > 0:
                                    s_a.wait_at(t_, s_a.n)
                                inst = None
                                for dc in range(8):
                                    inst = t_.matmul(ps_ha[ft % 2][:, 0:nk * 128],
                                                     w1_sb[:, dc, fg * 128:(fg + 1) * 128],
                                                     xgf[:, par, 0:nk, dc, :],
                                                     start=(dc == 0), stop=(dc == 7))
                                s_fill.inc(inst)
                                fillv.append(s_fill.n)

                        @block.scalar
                        def _(sc, ch=ch, nk=nk, fh=fh, fillv=fillv):
                            for ft in range(16):
                                fg = fh * 16 + ft
                                s_fill.wait_at(sc, fillv[ft])
                                s_a.inc(sc.activation(out=h_sb[:, ft, 0:nk * 128], in_=ps_ha[ft % 2][:, 0:nk * 128],
                                                      func=AF.Gelu, bias=b1_sb[:, fg:fg + 1], scale=1.0))

                        @block.tensor
                        def _(t_, ch=ch, nk=nk, fh=fh):
                            s_a.wait(t_)
                            if fh == 0:
                                s_v.wait_at(t_, ev0 + ch * 6)
                            for stt in range(3 if nk == 3 else nk):
                                for dh in range(2):
                                    inst = None
                                    for fc in range(16):
                                        inst = t_.matmul(ps_y[stt][dh][:, :],
                                                         h_sb[:, fc, stt * 128:(stt + 1) * 128],
                                                         w2_sb[:, fh * 16 + fc, dh * 512:(dh + 1) * 512],
                                                         start=(fh == 0 and fc == 0),
                                                         stop=(fh == 1 and fc == 15))
                                    if fh == 1:
                                        s_m2.inc(inst)

                    @block.vector
                    def _(v, ch=ch, nk=nk, par=par):
                        s_m2.wait(v)
                        s_sst.wait_at(v, st0 + ch * 16)
                        inst = None
                        for stt in range(nk):
                            for dh in range(2):
                                inst = v.tensor_tensor(y_sb[:, stt, dh * 512:(dh + 1) * 512],
                                                       ps_y[stt][dh][:, :],
                                                       b2b[:, dh * 512:(dh + 1) * 512], ALU.add)
                        s_v.inc(inst, 6)

                    @block.scalar
                    def _(sc, ch=ch, nk=nk):
                        s_v.wait(sc)
                        if ch < 10:
                            s_sst.inc(sc.dma_start(out=y_tilA[:, ch * 3:ch * 3 + nk, :],
                                                   in_=y_sb[:, 0:nk, :]), 16)
                        else:
                            s_sst.inc(sc.dma_start(out=y_tilB[:, (ch - 10) * 3:(ch - 10) * 3 + nk, :],
                                                   in_=y_sb[:, 0:nk, :]), 16)



                @block.gpsimd
                def _(g):
                    s_sst.wait(g)
                    s_ccy[call * 3 + 1].inc(g.collective_compute(
                        "AllGather", ALU.bypass, replica_groups=[list(range(NCORE))],
                        ins=[y_dB.ap()[768:1408, :].opt()],
                        outs=[y_agB.ap()[NCORE * 768:NCORE * 1408, :].opt()]))

        def emit_return(call):
            rc = ctx.enter_context(ExitStack())
            ygat = rc.enter_context(nc.sbuf_tensor(f"ygat{call}", [128, 2, 2, D], BF16))
            xtr = rc.enter_context(nc.sbuf_tensor(f"xtr{call}", [128, 2, 2, D], F32))
            xnwr = rc.enter_context(nc.sbuf_tensor(f"xnwr{call}", [128, 2, 2, D], F32))
            if call == 0:
                xnbr = rc.enter_context(nc.sbuf_tensor(f"xnbr{call}", [128, 2, 2, D], BF16))
            for gp in range(NT // 2):
                t0, t1 = 2 * gp, 2 * gp + 1

                @block.gpsimd
                def _(g, gp=gp, t0=t0, t1=t1, call=call):
                    if gp == 0:
                        s_ccy[call * 3].wait_at(g, 1)
                        s_ccy[call * 3 + 1].wait_at(g, 2)
                    if gp >= 2:
                        s_v.wait_at(g, st[f"resp_{call}_{gp-2}"])
                    sl = gp % 2
                    for par, tt in ((0, t0), (1, t1)):
                        s_gd.inc(g.indirect_dma_start(
                            out=ygat[:, sl, par, :], out_offset=None,
                            in_=y_agA.ap(), in_offset=IndirectOffsetOnAxis(gretu[:, tt:tt + 1], 0),
                            bounds_check=st["rb_yA"], oob_is_err=False,
                        ), 16)
                        s_gd.inc(g.indirect_dma_start(
                            out=ygat[:, sl, par, :], out_offset=None,
                            in_=y_agB.ap(), in_offset=IndirectOffsetOnAxis(gretu2[:, tt:tt + 1], 0),
                            bounds_check=st["rb_yB"], oob_is_err=False,
                        ), 16)

                @block.sync
                def _(sync, gp=gp, t0=t0, t1=t1, call=call):
                    if gp < 2:
                        s_ccy[call * 3 + 1].wait_at(sync, 2)
                    else:
                        s_v.wait_at(sync, st[f"resp_{call}_{gp-2}"])
                    sl = gp % 2
                    src = x_til if call == 0 else x2_til
                    s_ld.inc(sync.dma_start(out=xtr[:, sl, 0, :], in_=src[:, t0, :]), 16)
                    s_ld.inc(sync.dma_start(out=xtr[:, sl, 1, :], in_=src[:, t1, :]), 16)

                @block.vector
                def _(v, gp=gp, t0=t0, t1=t1, call=call):
                    s_gd.wait(v)
                    s_ld.wait(v)
                    if gp >= 2:
                        s_sst.wait_at(v, st[f"xstp_{call}_{gp-2}"])
                    sl = gp % 2
                    v.scalar_tensor_tensor(xnwr[:, sl, 0, :], ygat[:, sl, 0, :], wsc[:, t0:t0 + 1],
                                           xtr[:, sl, 0, :], ALU.mult, ALU.add)
                    inst = v.scalar_tensor_tensor(xnwr[:, sl, 1, :], ygat[:, sl, 1, :], wsc[:, t1:t1 + 1],
                                                  xtr[:, sl, 1, :], ALU.mult, ALU.add)
                    s_v.inc(inst)
                    st[f"resp_{call}_{gp}"] = s_v.n
                    if call == 0:
                        v.drain()
                        v.bn_stats(bns[:, 0, 0:6], xnwr[:, sl, 0, 0:512])
                        v.bn_stats(bns[:, 0, 6:12], xnwr[:, sl, 0, 512:1024])
                        v.bn_stats(bns[:, 1, 0:6], xnwr[:, sl, 1, 0:512])
                        v.bn_stats(bns[:, 1, 6:12], xnwr[:, sl, 1, 512:1024])
                        v.drain()
                        v.bn_aggr(mv[:, 0, :], bns[:, 0, :])
                        v.bn_aggr(mv[:, 1, :], bns[:, 1, :])
                        v.drain()
                        s_v.inc(v.tensor_scalar(rsd[:, :, :], mv[:, :, 1:2], EPS, None, ALU.add))

                if call == 0:
                    @block.scalar
                    def _(sc, gp=gp):
                        s_v.wait(sc)
                        sc.activation(out=rsd[:, :, :], in_=rsd[:, :, :], func=AF.Ln)
                        sc.drain()
                        s_a.inc(sc.activation(out=rsd[:, :, :], in_=rsd[:, :, :], func=AF.Exp, scale=-0.5))

                    @block.vector
                    def _(v, gp=gp):
                        s_a.wait(v)
                        if gp >= 2:
                            s_sst.wait_at(v, st[f"xn2p_{gp-2}"])
                        sl = gp % 2
                        v.tensor_scalar(xnbr[:, sl, 0, :], xnwr[:, sl, 0, :], mv[:, 0, 0:1],
                                        rsd[:, 0, :], ALU.subtract, ALU.mult)
                        s_v.inc(v.tensor_scalar(xnbr[:, sl, 1, :], xnwr[:, sl, 1, :], mv[:, 1, 0:1],
                                                rsd[:, 1, :], ALU.subtract, ALU.mult))

                    @block.scalar
                    def _(sc, gp=gp, t0=t0, t1=t1):
                        s_v.wait(sc)
                        sl = gp % 2
                        s_sst.inc(sc.dma_start(out=x1_til[:, t0, :], in_=xnwr[:, sl, 0, :]), 16)
                        s_sst.inc(sc.dma_start(out=x1_til[:, t1, :], in_=xnwr[:, sl, 1, :]), 16)
                        st[f"xstp_0_{gp}"] = s_sst.n
                        s_sst.inc(sc.dma_start(out=xn2_til[:, t0, :], in_=xnbr[:, sl, 0, :]), 16)
                        s_sst.inc(sc.dma_start(out=xn2_til[:, t1, :], in_=xnbr[:, sl, 1, :]), 16)
                        st[f"xn2p_{gp}"] = s_sst.n
                else:
                    @block.scalar
                    def _(sc, gp=gp, t0=t0, t1=t1):
                        s_v.wait(sc)
                        sl = gp % 2
                        s_sst.inc(sc.dma_start(out=xo_til[:, t0, :], in_=xnwr[:, sl, 0, :]), 16)
                        s_sst.inc(sc.dma_start(out=xo_til[:, t1, :], in_=xnwr[:, sl, 1, :]), 16)
                        st[f"xstp_1_{gp}"] = s_sst.n
            rc.close()

        def emit_attention():
            with ExitStack() as ac:
                ea = ac.enter_context
                qkp_sb = ea(nc.sbuf_tensor("qkp_sb", [128, 8, 2 * PD], BF16))
                xnf2 = ea(nc.sbuf_tensor("xnf2", [128, 2, 2, 8, 128], BF16, is_transpose_dest=True))
                lga = ea(nc.sbuf_tensor("lga", [128, 16, 16], F32))
                aw = ea(nc.sbuf_tensor("aw", [128, 2, 2, D], F32))

                def emit_router_mm(b, g):
                    lgb = ps_kv[1][0]

                    @block.tensor
                    def _(t_, g=g, b=b):
                        s_tr.wait_at(t_, st[f"a2trp_{b}_{g}"])
                        if g == 0:
                            s_v.wait_at(t_, st[f"kvz_{b}"])
                        prev = st.get(f"a2sh_{b}_{g-1}", st.get(f"a2sh_{b-1}_7", 0))
                        s_v.wait_at(t_, prev)
                        inst = None
                        for par in range(2):
                            for dc in range(8):
                                inst = t_.matmul(lgb[:, par * 8:(par + 1) * 8],
                                                 xnf2[:, g % 2, par, dc, :], wg_sb[:, dc, :],
                                                 start=(dc == 0), stop=(dc == 7))
                        s_t.inc(inst)
                        st[f"a2mm_{b}_{g}"] = s_t.n

                    @block.vector
                    def _(v, g=g, b=b):
                        s_t.wait_at(v, st[f"a2mm_{b}_{g}"])
                        s_v.inc(v.tensor_copy(lga[:, b * 8 + g, :], lgb[:, 0:16]))
                        st[f"a2sh_{b}_{g}"] = s_v.n
                qe_all = ea(nc.sbuf_tensor("qe_all", [128, 4, PD], BF16))
                qT = ea(nc.sbuf_tensor("qT", [128, 16, 2, 128], BF16, is_transpose_dest=True))
                ke = ea(nc.sbuf_tensor("ke", [128, 2, PD], BF16))
                kv_sb = ea(nc.sbuf_tensor("kv_sb", [128, 2, D], BF16))
                zsum = ea(nc.sbuf_tensor("zsum", [128, 2, 16], F32))
                zinv = ea(nc.sbuf_tensor("zinv", [128, 2, 16], F32))
                ps_qk = [ea(nc.psum_tensor(f"ps_qk{i}", [128, 512], F32)) for i in range(2)]
                ps_kv = [[ea(nc.psum_tensor(f"ps_kv{eh}{dh}", [128, 512], F32)) for dh in range(2)]
                         for eh in range(2)]
                ps_nm = [ea(nc.psum_tensor(f"ps_nm{i}", [128, 512], F32)) for i in range(2)]

                @block.gpsimd
                def _(g):
                    s_sst.wait_at(g, st["xn2p_15"])
                    s_wc.inc(g.dma_start(out=qkp_sb[:, :, :],
                                         in_=qk_in.rearrange("(dc p) e -> p dc e", p=128)), 16)

                qkb = [ps_qk, ps_nm]

                def emit_kv(t_, gp, b):
                    s_a.wait_at(t_, st[f"expkp_{b}_{gp}"])
                    if b > 0 and gp == 0:
                        s_v.wait_at(t_, st[f"aresp_{b-1}_7"])
                        s_v.wait_at(t_, st[f"a2sh_{b-1}_7"])
                    sl = gp % 2
                    inst = None
                    for par in range(2):
                        for eh in range(2):
                            for dh in range(2):
                                inst = t_.matmul(ps_kv[eh][dh][:, :],
                                                 ke[:, par, eh * 128:(eh + 1) * 128],
                                                 xnb[:, sl, par, dh * 512:(dh + 1) * 512],
                                                 start=(gp == 0 and par == 0),
                                                 stop=(gp == 7 and par == 1))
                    s_t.inc(inst)
                    st[f"kvp_{b}_{gp}"] = s_t.n

                for b in (ATT_BATCHES if ATT_BATCHES is not None else range(B)):
                    for gp in range(8):
                        i0, i1 = 2 * gp, 2 * gp + 1
                        t0, t1 = b * 16 + i0, b * 16 + i1
                        sl = gp % 2

                        @block.sync
                        def _(sync, gp=gp, t0=t0, t1=t1, b=b, sl=sl):
                            if b == 0 and gp == 0:
                                s_sst.wait(sync)
                            if b > 0 and gp == 0:
                                s_t.wait_at(sync, st[f"kvp_{b-1}_7"])
                                s_sst.wait_at(sync, st[f"axp_{b-1}_7"])
                                s_tr.wait_at(sync, st[f"a2trp_{b-1}_7"])
                            if gp >= 2:
                                s_t.wait_at(sync, st[f"kvp_{b}_{gp-2}"])
                            s_ld.inc(sync.dma_start(out=xnb[:, sl, 0, :], in_=xn2_til[:, t0, :]), 16)
                            s_ld.inc(sync.dma_start(out=xnb[:, sl, 1, :], in_=xn2_til[:, t1, :]), 16)
                            s_ld.wait(sync)
                            s_tr.inc(sync.dma_start(out=xnf[:, sl, 0, :, :], in_=xnb[:, sl, 0, :], transpose=True), 16)
                            s_tr.inc(sync.dma_start(out=xnf[:, sl, 1, :, :], in_=xnb[:, sl, 1, :], transpose=True), 16)
                            st[f"atr_{b}_{gp}"] = s_tr.n

                        @block.tensor
                        def _(t_, gp=gp, b=b, sl=sl):
                            s_tr.wait_at(t_, st[f"atr_{b}_{gp}"])
                            if gp == 0:
                                s_wc.wait(t_)
                            prev = st.get(f"expkp_{b}_{gp-2}", 0)
                            if prev:
                                s_a.wait_at(t_, prev)
                            if b > 0 and gp == 1:
                                s_v.wait_at(t_, st[f"aresp_{b-1}_7"])
                            qk = qkb[sl]
                            inst = None
                            for par in range(2):
                                for dc in range(8):
                                    inst = t_.matmul(qk[par][:, :], xnf[:, sl, par, dc, :], qkp_sb[:, dc, :],
                                                     start=(dc == 0), stop=(dc == 7))
                            s_t.inc(inst)
                            st[f"qkp_{b}_{gp}"] = s_t.n

                        if gp >= 1:
                            @block.tensor
                            def _(t_, gp=gp - 1, b=b):
                                emit_kv(t_, gp, b)

                        @block.scalar
                        def _(sc, gp=gp, i0=i0, i1=i1, b=b, sl=sl):
                            s_t.wait_at(sc, st[f"qkp_{b}_{gp}"])
                            prevk = st.get(f"kvp_{b}_{gp-1}", 0)
                            if prevk:
                                s_t.wait_at(sc, prevk)
                            prev = st.get(f"qtr_{b}_{gp-2}", st.get(f"qtr_{b-1}_{gp+6}", 0))
                            if prev:
                                s_tr.wait_at(sc, prev)
                            qk = qkb[sl]
                            sc.activation(out=qe_all[:, i0 % 4, :], in_=qk[0][:, 0:PD], func=AF.Exp,
                                          accum_out=zsum[:, b, i0:i0 + 1])
                            sc.activation(out=ke[:, 0, :], in_=qk[0][:, PD:2 * PD], func=AF.Exp)
                            sc.activation(out=qe_all[:, i1 % 4, :], in_=qk[1][:, 0:PD], func=AF.Exp,
                                          accum_out=zsum[:, b, i1:i1 + 1])
                            s_a.inc(sc.activation(out=ke[:, 1, :], in_=qk[1][:, PD:2 * PD], func=AF.Exp))
                            st[f"expkp_{b}_{gp}"] = s_a.n

                        @block.sync
                        def _(sync, gp=gp, i0=i0, i1=i1, b=b):
                            s_a.wait(sync)
                            if b > 0 and gp == 0:
                                s_v.wait_at(sync, st[f"aresp_{b-1}_7"])
                            s_tr.inc(sync.dma_start(out=qT[:, i0, :, :], in_=qe_all[:, i0 % 4, :], transpose=True), 16)
                            s_tr.inc(sync.dma_start(out=qT[:, i1, :, :], in_=qe_all[:, i1 % 4, :], transpose=True), 16)
                            st[f"qtr_{b}_{gp}"] = s_tr.n

                    @block.tensor
                    def _(t_, b=b):
                        emit_kv(t_, 7, b)

                    @block.vector
                    def _(v, b=b):
                        s_t.wait(v)
                        v.drain()
                        for eh in range(2):
                            for dh in range(2):
                                v.tensor_copy(kv_sb[:, eh, dh * 512:(dh + 1) * 512], ps_kv[eh][dh][:, :])
                        v.tensor_scalar(zinv[:, b, :], zsum[:, b, :], 1e-8, None, ALU.add)
                        v.drain()
                        s_v.inc(v.reciprocal(zinv[:, b, :], zinv[:, b, :]))
                        st[f"kvz_{b}"] = s_v.n

                    if DEBUG:
                        @block.scalar
                        def _(sc, b=b):
                            s_v.wait(sc)
                            s_sst.inc(sc.dma_start(out=dbg_z[b, :, :], in_=zinv[:, b, :]), 16)
                            s_sst.inc(sc.dma_start(out=dbg_kv[b, :, :], in_=kv_sb[:, :, :].rearrange("p a b -> p (a b)")), 16)
                            s_sst.inc(sc.dma_start(out=dbg_qe[b, :, :], in_=qe_all[:, :, :].rearrange("p a b -> p (a b)")), 16)

                    nmA = ps_nm
                    nmB = [ps_kv[0][0], ps_kv[0][1]]
                    for gp in range(8):
                        i0, i1 = 2 * gp, 2 * gp + 1
                        t0, t1 = b * 16 + i0, b * 16 + i1

                        if gp == 0:
                            @block.sync
                            def _(sync, b=b):
                                if b > 0:
                                    s_v.wait_at(sync, st[f"aresp_{b-1}_7"])
                                s_ld.inc(sync.dma_start(out=xt[:, 0, :], in_=x1_til[:, b * 16, :]), 16)
                                s_ld.inc(sync.dma_start(out=xt[:, 1, :], in_=x1_til[:, b * 16 + 1, :]), 16)

                        @block.tensor
                        def _(t_, gp=gp, i0=i0, i1=i1, b=b):
                            if gp == 0:
                                s_v.wait_at(t_, st[f"kvz_{b}"])
                                if b > 0:
                                    s_v.wait_at(t_, st[f"aresp_{b-1}_7"])
                            if gp >= 1:
                                s_v.wait_at(t_, st[f"aresp_{b}_{gp-1}"])
                            inst = None
                            for nmp, ii in ((nmA, i0), (nmB, i1)):
                                for dh in range(2):
                                    for eh in range(2):
                                        inst = t_.matmul(nmp[dh][:, :], qT[:, ii, eh, :],
                                                         kv_sb[:, eh, dh * 512:(dh + 1) * 512],
                                                         start=(eh == 0), stop=(eh == 1))
                            s_t.inc(inst)

                        @block.vector
                        def _(v, gp=gp, i0=i0, i1=i1, b=b):
                            s_t.wait(v)
                            s_ld.wait(v)
                            sl = gp % 2
                            prev = st.get(f"axp_{b}_{gp-2}", st.get(f"axp_{b-1}_{gp+6}", 0))
                            if prev:
                                s_sst.wait_at(v, prev)
                            for par, (nmp, ii) in enumerate(((nmA, i0), (nmB, i1))):
                                for dh in range(2):
                                    inst = v.scalar_tensor_tensor(aw[:, sl, par, dh * 512:(dh + 1) * 512],
                                                                  nmp[dh][:, :], zinv[:, b, ii:ii + 1],
                                                                  xt[:, par, dh * 512:(dh + 1) * 512],
                                                                  ALU.mult, ALU.add)
                            s_v.inc(inst)
                            st[f"aresp_{b}_{gp}"] = s_v.n
                            v.drain()
                            v.bn_stats(bns[:, 0, 0:6], aw[:, sl, 0, 0:512])
                            v.bn_stats(bns[:, 0, 6:12], aw[:, sl, 0, 512:1024])
                            v.bn_stats(bns[:, 1, 0:6], aw[:, sl, 1, 0:512])
                            v.bn_stats(bns[:, 1, 6:12], aw[:, sl, 1, 512:1024])
                            v.drain()
                            v.bn_aggr(mv[:, 0, :], bns[:, 0, :])
                            v.bn_aggr(mv[:, 1, :], bns[:, 1, :])
                            v.drain()
                            s_v.inc(v.tensor_scalar(rsd[:, :, :], mv[:, :, 1:2], EPS, None, ALU.add))

                        if gp < 7:
                            @block.sync
                            def _(sync, gp=gp, t0=t0, t1=t1, b=b):
                                s_v.wait_at(sync, st[f"aresp_{b}_{gp}"])
                                s_ld.inc(sync.dma_start(out=xt[:, 0, :], in_=x1_til[:, t0 + 2, :]), 16)
                                s_ld.inc(sync.dma_start(out=xt[:, 1, :], in_=x1_til[:, t1 + 2, :]), 16)

                        @block.scalar
                        def _(sc, gp=gp):
                            s_v.wait(sc)
                            sc.activation(out=rsd[:, :, :], in_=rsd[:, :, :], func=AF.Ln)
                            sc.drain()
                            s_a.inc(sc.activation(out=rsd[:, :, :], in_=rsd[:, :, :], func=AF.Exp, scale=-0.5))

                        @block.vector
                        def _(v, gp=gp, b=b):
                            s_a.wait(v)
                            sl = gp % 2
                            if gp >= 1:
                                s_sst.wait_at(v, st[f"axp_{b}_{gp-1}"])
                                s_tr.wait_at(v, st[f"a2trp_{b}_{gp-1}"])
                            v.tensor_scalar(xnb[:, 0, 0, :], aw[:, sl, 0, :], mv[:, 0, 0:1],
                                            rsd[:, 0, :], ALU.subtract, ALU.mult)
                            s_v.inc(v.tensor_scalar(xnb[:, 0, 1, :], aw[:, sl, 1, :], mv[:, 1, 0:1],
                                                    rsd[:, 1, :], ALU.subtract, ALU.mult))
                            st[f"aply_{b}_{gp}"] = s_v.n

                        @block.scalar
                        def _(sc, gp=gp, t0=t0, t1=t1, b=b):
                            s_v.wait(sc)
                            sl = gp % 2
                            s_sst.inc(sc.dma_start(out=x2_til[:, t0, :], in_=aw[:, sl, 0, :]), 16)
                            s_sst.inc(sc.dma_start(out=agxn_til[:, t0, :], in_=xnb[:, 0, 0, :]), 16)
                            s_sst.inc(sc.dma_start(out=x2_til[:, t1, :], in_=aw[:, sl, 1, :]), 16)
                            s_sst.inc(sc.dma_start(out=agxn_til[:, t1, :], in_=xnb[:, 0, 1, :]), 16)
                            st[f"axp_{b}_{gp}"] = s_sst.n

                        lgb = ps_kv[1][0]

                        @block.sync
                        def _(sync, gp=gp, b=b):
                            s_v.wait_at(sync, st[f"aply_{b}_{gp}"])
                            prev = st.get(f"a2mm_{b}_{gp-2}", st.get(f"a2mm_{b-1}_{gp+6}", 0))
                            s_t.wait_at(sync, prev)
                            s_tr.inc(sync.dma_start(out=xnf2[:, gp % 2, 0, :, :], in_=xnb[:, 0, 0, :], transpose=True), 16)
                            s_tr.inc(sync.dma_start(out=xnf2[:, gp % 2, 1, :, :], in_=xnb[:, 0, 1, :], transpose=True), 16)
                            st[f"a2trp_{b}_{gp}"] = s_tr.n

                        if gp >= 2:
                            emit_router_mm(b, gp - 2)

                    emit_router_mm(b, 6)
                    emit_router_mm(b, 7)
                    st["attn_prev_b"] = s_t.n

                for q in range(16):
                    t0, t1 = 2 * q, 2 * q + 1

                    @block.vector
                    def _(v, q=q, t0=t0, t1=t1):
                        if q == 0:
                            s_v.wait(v)
                        if q >= 1:
                            s_a.wait_at(v, st[f"p2exp_{q-1}"])
                            s_t.wait_at(v, st[f"c2mm_{q-1}"])
                        inst = None
                        for par, tt in ((0, t0), (1, t1)):
                            v.max(mx8[:, :], lga[:, q, par * 8:(par + 1) * 8])
                            v.drain()
                            v.max_index(ix8[:, :], mx8[:, :], lga[:, q, par * 8:(par + 1) * 8])
                            v.drain()
                            v.tensor_copy(idxf[:, tt:tt + 1], ix8[:, 0:1])
                            v.tensor_copy(idc[:, :], ix8[:, 0:1])
                            v.drain()
                            v.tensor_scalar(ohtm[:, par, :], iotaef[:, :], idc[:, :], None, ALU.is_equal)
                            inst = v.tensor_scalar(sh8[:, par, :], lga[:, q, par * 8:(par + 1) * 8],
                                                   mx8[:, 0:1], None, ALU.subtract)
                            v.drain()
                        s_v.inc(inst)
                        st[f"p2sh_{q}"] = s_v.n

                    @block.scalar
                    def _(sc, q=q, t0=t0, t1=t1):
                        s_v.wait_at(sc, st[f"p2sh_{q}"])
                        sc.activation(out=she[:, 0, :], in_=sh8[:, 0, :], func=AF.Exp,
                                      accum_out=esum[:, t0:t0 + 1])
                        s_a.inc(sc.activation(out=she[:, 1, :], in_=sh8[:, 1, :], func=AF.Exp,
                                              accum_out=esum[:, t1:t1 + 1]))
                        st[f"p2exp_{q}"] = s_a.n

                    @block.tensor
                    def _(t_, q=q):
                        s_v.wait_at(t_, st[f"p2sh_{q}"])
                        inst = None
                        for par in range(2):
                            inst = t_.matmul(ps_nm[0][0:E, 0:1], ohtm[:, par, :], ones128b[:, :],
                                             start=(q == 0 and par == 0),
                                             stop=(q == 15 and par == 1))
                        s_t.inc(inst)
                        st[f"c2mm_{q}"] = s_t.n

                @block.vector
                def _(v):
                    s_a.wait(v)
                    v.reciprocal(comb[:, :], esum[:, :])
                    v.drain()
                    s_v.inc(v.tensor_copy(idxw[:, :], idxf[:, :]))

                @block.vector
                def _(v):
                    s_t.wait(v)
                    s_v.inc(v.tensor_copy(cnt8[:, :], ps_nm[0][0:E, 0:1]))
                    st["cntev_1"] = s_v.n

                @block.gpsimd
                def _(g):
                    for k in range(8):
                        g.wait_ge(s_sst.h, st[f"axp_{k // 4}_{2 * (k % 4) + 1}"])
                        s_cc[5].inc(g.collective_compute(
                            "AllGather", ALU.bypass, replica_groups=[list(range(NCORE))],
                            ins=[ag_xn_in.ap()[512 * k:512 * (k + 1), :].opt()],
                            outs=[ag_xn_out.ap()[NCORE * 512 * k:NCORE * 512 * (k + 1), :].opt()]))
                    g.wait_ge(s_v.h, st["cntev_1"])
                    s_gd.inc(g.dma_start(out=cnt_dram.ap(), in_=cnt8[:, :]), 16)
                    s_gd.wait(g)
                    s_cc[4].inc(g.collective_compute(
                        "AllGather", ALU.bypass, replica_groups=[list(range(NCORE))],
                        ins=[cnt_dram.ap().opt()], outs=[cnt_ag.ap().opt()]))

        emit_routing(0)
        emit_expert(0)
        emit_return(0)
        emit_attention()

        @block.gpsimd
        def _(g):
            s_cc[2].wait_at(g, 1)
            s_gd.inc(g.dma_start(out=S_dram.ap().rearrange("(a b) one -> a (b one)", b=128),
                                 in_=S_init.ap().rearrange("(a b) one -> a (b one)", b=128)), 16)

        emit_routing(1)
        emit_expert(1)
        emit_return(1)

        @block.sync
        def _(sync):
            s_sst.wait(sync)
            s_gd.wait(sync)
            if DEBUG:
                s_ld.inc(sync.dma_start(out=dbg_x1[:, :], in_=x1_dram.ap()[:, :]), 16)
                s_ld.inc(sync.dma_start(out=dbg_x2[:, :], in_=x2_dram.ap()[:, :]), 16)
                s_ld.inc(sync.dma_start(out=dbg_xn2[:, :], in_=xn2_dram.ap()[:, :]), 16)
                s_ld.inc(sync.dma_start(out=dbg_xn3[:, :], in_=ag_xn_in.ap()[:, :]), 16)
                s_ld.inc(sync.dma_start(out=dbg_y[:, :], in_=y_dram.ap()[:, :]), 16)
                s_ld.inc(sync.dma_start(out=dbg_r[:, 0:NT], in_=rsel[:, :]), 16)
                s_ld.inc(sync.dma_start(out=dbg_r[:, NT:2 * NT], in_=wsc[:, :]), 16)
                s_ld.inc(sync.dma_start(out=dbg_r[:, 2 * NT:3 * NT], in_=gretf[:, :]), 16)
                s_ld.inc(sync.dma_start(out=dbg_r[:, 3 * NT:4 * NT], in_=comb[:, :]), 16)
                s_ld.inc(sync.dma_start(out=dbg_g[:, :], in_=g_u[:, :]), 16)
                s_ld.wait(sync)

    return nc


_CACHED = {}


def kernel(L0_attn_x, L0_attn_q_proj, L0_attn_k_proj, L0_ffn_wg, L0_ffn_W1,
           L0_ffn_b1, L0_ffn_W2, L0_ffn_b2):
    x = np.ascontiguousarray(np.asarray(L0_attn_x, dtype=np.float32))
    qk = np.ascontiguousarray(np.concatenate([np.asarray(L0_attn_q_proj, np.float32),
                                              np.asarray(L0_attn_k_proj, np.float32)], axis=1))
    wg = np.ascontiguousarray(np.asarray(L0_ffn_wg, np.float32))
    W1 = np.asarray(L0_ffn_W1, np.float32)
    b1 = np.asarray(L0_ffn_b1, np.float32)
    W2 = np.asarray(L0_ffn_W2, np.float32)
    b2 = np.asarray(L0_ffn_b2, np.float32)
    ident = np.eye(128, dtype=ml_dtypes.bfloat16)

    if "nc" not in _CACHED:
        _CACHED["nc"] = build()
    nc = _CACHED["nc"]

    in_maps = []
    for c in range(NCORE):
        pm = (np.arange(E) < c).astype(np.float32).reshape(E, 1)
        in_maps.append({
            "x": np.ascontiguousarray(x[2 * c:2 * c + 2].reshape(T4, D)),
            "qk": qk,
            "wg": wg,
            "w1": np.ascontiguousarray(W1[c]),
            "b1": np.ascontiguousarray(b1[c].reshape(32, 128).T),
            "w2": np.ascontiguousarray(W2[c]),
            "b2": np.ascontiguousarray(b2[c].reshape(1, D)),
            "premask": pm,
            "myc": np.full((128, 1), float(c), np.float32),
            "ident": ident,
        })
    res = run_bass_kernel_spmd(nc, in_maps, core_ids=list(range(NCORE)))
    out = np.concatenate([r["xo"].reshape(2, TSEQ, D) for r in res.results], axis=0)
    return out


# version 44
# speedup vs baseline: 1.0125x; 1.0125x over previous
import numpy as np
import ml_dtypes
import concourse.bass as bass
import concourse.mybir as mybir
from concourse.bass import IndirectOffsetOnAxis
from concourse.bass_utils import run_bass_kernel_spmd
from contextlib import ExitStack

F32 = mybir.dt.float32
BF16 = mybir.dt.bfloat16
U32 = mybir.dt.uint32
I32 = mybir.dt.int32
AF = mybir.ActivationFunctionType
ALU = mybir.AluOpType
AX = mybir.AxisListType

NCORE = 8
B = 2
TSEQ = 2048
T4 = B * TSEQ
NT = 32
D = 1024
E = 8
FF = 4096
PD = 256
CAP = 5121
NC41 = 41
RPAD = NC41 * 128
CHT = 384
NCH = 14
BIG = float(2 ** 30)
C23 = 12582912.0
EPS = 1e-5
DEBUG = False
ATT_BATCHES = None


class Sem:
    def __init__(self, ctx, nc, name):
        self.h = ctx.enter_context(nc.semaphore(name))
        self.n = 0

    def inc(self, inst, k=1):
        inst.then_inc(self.h, k)
        self.n += k

    def wait(self, eng):
        eng.wait_ge(self.h, self.n)

    def wait_at(self, eng, v):
        if v > 0:
            eng.wait_ge(self.h, v)


def build():
    nc = bass.Bass("TRN2", target_bir_lowering=False)
    x_in = nc.dram_tensor("x", [T4, D], F32, kind="ExternalInput").ap()
    qk_in = nc.dram_tensor("qk", [D, 2 * PD], F32, kind="ExternalInput").ap()
    wg_in = nc.dram_tensor("wg", [D, E], F32, kind="ExternalInput").ap()
    w1_in = nc.dram_tensor("w1", [D, FF], F32, kind="ExternalInput").ap()
    b1_in = nc.dram_tensor("b1", [128, 32], F32, kind="ExternalInput").ap()
    w2_in = nc.dram_tensor("w2", [FF, D], F32, kind="ExternalInput").ap()
    b2_in = nc.dram_tensor("b2", [1, D], F32, kind="ExternalInput").ap()
    premask_in = nc.dram_tensor("premask", [E, 1], F32, kind="ExternalInput").ap()
    myc_in = nc.dram_tensor("myc", [128, 1], F32, kind="ExternalInput").ap()
    ident_in = nc.dram_tensor("ident", [128, 128], BF16, kind="ExternalInput").ap()
    xo = nc.dram_tensor("xo", [T4, D], F32, kind="ExternalOutput").ap()
    if DEBUG:
        dbg_x1 = nc.dram_tensor("dbg_x1", [T4, D], F32, kind="ExternalOutput").ap()
        dbg_x2 = nc.dram_tensor("dbg_x2", [T4, D], F32, kind="ExternalOutput").ap()
        dbg_xn2 = nc.dram_tensor("dbg_xn2", [T4, D], BF16, kind="ExternalOutput").ap()
        dbg_xn3 = nc.dram_tensor("dbg_xn3", [T4, D], BF16, kind="ExternalOutput").ap()
        dbg_y = nc.dram_tensor("dbg_y", [RPAD, D], BF16, kind="ExternalOutput").ap()
        dbg_r = nc.dram_tensor("dbg_r", [128, 4 * NT], F32, kind="ExternalOutput").ap()
        dbg_g = nc.dram_tensor("dbg_g", [128, NC41], U32, kind="ExternalOutput").ap()
        dbg_z = nc.dram_tensor("dbg_z", [2, 128, 16], F32, kind="ExternalOutput").ap()
        dbg_kv = nc.dram_tensor("dbg_kv", [2, 128, 2048], BF16, kind="ExternalOutput").ap()
        dbg_qe = nc.dram_tensor("dbg_qe", [2, 128, 16 * PD], BF16, kind="ExternalOutput").ap()
        dbg_nm = nc.dram_tensor("dbg_nm", [2, 128, 1024], F32, kind="ExternalOutput").ap()
        dbg_t0 = nc.dram_tensor("dbg_t0", [3, 128, 1024], BF16, kind="ExternalOutput").ap()

    ag_xn_in = nc.dram_tensor("ag_xn_in", [T4, D], BF16)
    ag_xn_out = nc.dram_tensor("ag_xn_out", [NCORE * T4, D], BF16, addr_space="Shared")
    cnt_dram = nc.dram_tensor("cnt_dram", [1, E], F32)
    cnt_ag = nc.dram_tensor("cnt_ag", [NCORE, E], F32, addr_space="Shared")
    S_dram = nc.dram_tensor("S_dram", [E * RPAD, 1], I32)
    S_init = nc.dram_tensor("S_init", [E * RPAD, 1], I32)
    S_ag = nc.dram_tensor("S_ag", [NCORE * E * RPAD, 1], I32, addr_space="Shared")
    y_dA = nc.dram_tensor("y_dA", [3840, D], BF16)
    y_dB = nc.dram_tensor("y_dB", [1408, D], BF16)
    y_agA = nc.dram_tensor("y_agA", [NCORE * 3840, D], BF16, addr_space="Shared")
    y_agB = nc.dram_tensor("y_agB", [NCORE * 1408, D], BF16, addr_space="Shared")
    x1_dram = nc.dram_tensor("x1_dram", [T4, D], F32)
    x2_dram = nc.dram_tensor("x2_dram", [T4, D], F32)
    xn2_dram = nc.dram_tensor("xn2_dram", [T4, D], BF16)

    x_til = x_in.rearrange("(t p) d -> p t d", p=128)
    x1_til = x1_dram.ap().rearrange("(t p) d -> p t d", p=128)
    x2_til = x2_dram.ap().rearrange("(t p) d -> p t d", p=128)
    xo_til = xo.rearrange("(t p) d -> p t d", p=128)
    agxn_til = ag_xn_in.ap().rearrange("(t p) d -> p t d", p=128)
    xn2_til = xn2_dram.ap().rearrange("(t p) d -> p t d", p=128)
    y_tilA = y_dA.ap().rearrange("(c p) d -> p c d", p=128)
    y_tilB = y_dB.ap().rearrange("(c p) d -> p c d", p=128)
    S_rows = S_ag.ap().rearrange("(row c) one -> row (c one)", c=NC41)

    with ExitStack() as ctx:
        en = ctx.enter_context
        w1_sb = en(nc.sbuf_tensor("w1_sb", [128, 8, FF], BF16))
        w2_sb = en(nc.sbuf_tensor("w2_sb", [128, 32, D], BF16))
        wg_sb = en(nc.sbuf_tensor("wg_sb", [128, 8, E], BF16))
        b1_sb = en(nc.sbuf_tensor("b1_sb", [128, 32], F32))
        b2b = en(nc.sbuf_tensor("b2b", [128, D], F32))
        idn = en(nc.sbuf_tensor("idn", [128, 128], BF16))
        ones18 = en(nc.sbuf_tensor("ones18", [1, E], BF16))
        ones128b = en(nc.sbuf_tensor("ones128b", [128, 1], BF16))
        ohtm = en(nc.sbuf_tensor("ohtm", [128, 2, E], BF16))
        idc = en(nc.sbuf_tensor("idc", [128, 1], F32))
        ones8c = en(nc.sbuf_tensor("ones8c", [E, 1], F32))
        iota8c = en(nc.sbuf_tensor("iota8c", [E, 1], F32))
        iotaef = en(nc.sbuf_tensor("iotaef", [128, E], I32))
        locid = en(nc.sbuf_tensor("locid", [128, NT], I32))
        iota128 = en(nc.sbuf_tensor("iota128", [128, 1], I32))
        premask = en(nc.sbuf_tensor("premask_sb", [E, 1], F32))
        myc = en(nc.sbuf_tensor("myc_sb", [128, 1], F32))
        midxu = en(nc.sbuf_tensor("midxu", [128, E], U32))
        mbase = en(nc.sbuf_tensor("mbase", [128, 1], F32))
        xt = en(nc.sbuf_tensor("xt", [128, 2, D], F32))
        xnb = en(nc.sbuf_tensor("xnb", [128, 2, 2, D], BF16))
        xnf = en(nc.sbuf_tensor("xnf", [128, 2, 2, 8, 128], BF16, is_transpose_dest=True))
        bns = en(nc.sbuf_tensor("bns", [128, 2, 12], F32))
        mv = en(nc.sbuf_tensor("mv", [128, 2, 2], F32))
        rsd = en(nc.sbuf_tensor("rsd", [128, 2, 1], F32))
        nbi = en(nc.sbuf_tensor("nbi", [128, 2, 1], F32))
        g_u = en(nc.sbuf_tensor("g_u", [128, NC41], U32))
        gretu = en(nc.sbuf_tensor("gretu", [128, NT], U32))
        gretu2 = en(nc.sbuf_tensor("gretu2", [128, NT], U32))
        wsc = en(nc.sbuf_tensor("wsc", [128, NT], F32))
        cnt8 = en(nc.sbuf_tensor("cnt8", [E, 1], F32))
        cntall = en(nc.sbuf_tensor("cntall", [E, E], F32))
        base8 = en(nc.sbuf_tensor("base8", [E, 1], F32))
        mx8 = en(nc.sbuf_tensor("mx8", [128, 8], F32))
        ix8 = en(nc.sbuf_tensor("ix8", [128, 8], U32))
        idxf = en(nc.sbuf_tensor("idxf", [128, NT], BF16))
        idxw = en(nc.sbuf_tensor("idxw", [128, NT], F32))
        sh8 = en(nc.sbuf_tensor("sh8", [128, 2, E], F32))
        she = en(nc.sbuf_tensor("she", [128, 2, E], F32))
        esum = en(nc.sbuf_tensor("esum", [128, NT], F32))
        comb = en(nc.sbuf_tensor("comb", [128, NT], F32))
        rsel = en(nc.sbuf_tensor("rsel", [128, NT], F32))
        vld = en(nc.sbuf_tensor("vld", [128, NT], F32))
        scc = en(nc.sbuf_tensor("scc", [128, NT], F32))
        spp = en(nc.sbuf_tensor("spp", [128, NT], F32))
        offf = en(nc.sbuf_tensor("offf", [128, NT], F32))
        off2 = en(nc.sbuf_tensor("off2", [128, NT], F32))
        offu = en(nc.sbuf_tensor("offu", [128, NT], U32))
        gretf = en(nc.sbuf_tensor("gretf", [128, NT], F32))

        s_ld = Sem(ctx, nc, "s_ld")
        s_ld0 = Sem(ctx, nc, "s_ld0")
        s_wc = Sem(ctx, nc, "s_wc")
        s_tr = Sem(ctx, nc, "s_tr")
        s_str = Sem(ctx, nc, "s_str")
        s_sst = Sem(ctx, nc, "s_sst")
        s_gd = Sem(ctx, nc, "s_gd")
        s_gc = Sem(ctx, nc, "s_gc")
        s_v = Sem(ctx, nc, "s_v")
        s_a = Sem(ctx, nc, "s_a")
        s_t = Sem(ctx, nc, "s_t")
        s_fill = Sem(ctx, nc, "s_fill")
        s_m2 = Sem(ctx, nc, "s_m2")
        s_cc = [Sem(ctx, nc, f"s_cc{i}") for i in range(8)]
        s_ccy = [Sem(ctx, nc, f"s_ccy{i}") for i in range(6)]

        block = en(nc.Block())
        st = {"xnst": 0}

        c0 = ctx.enter_context(ExitStack())
        b2r = c0.enter_context(nc.sbuf_tensor("b2r", [1, D], F32))
        neg1 = c0.enter_context(nc.sbuf_tensor("neg1", [128, E * NC41], I32))
        ones1 = c0.enter_context(nc.sbuf_tensor("ones1", [1, 128], F32))

        @block.sync
        def _(sync):
            s_ld0.inc(sync.dma_start(out=b1_sb[:, :], in_=b1_in[:, :]), 16)
            s_ld0.inc(sync.dma_start(out=b2r[:, :], in_=b2_in[:, :]), 16)
            s_ld0.inc(sync.dma_start(out=idn[:, :], in_=ident_in[:, :]), 16)
            s_ld0.inc(sync.dma_start(out=premask[:, :], in_=premask_in[:, :]), 16)
            s_ld0.inc(sync.dma_start(out=myc[:, :], in_=myc_in[:, :]), 16)

        @block.gpsimd
        def _(g):
            g.iota(iotaef[:, :], pattern=[[1, E]], base=0, channel_multiplier=0)
            g.iota(iota8c[:, :], pattern=[[1, 1]], base=0, channel_multiplier=1,
                   allow_small_or_imprecise_dtypes=True)
            g.iota(locid[:, :], pattern=[[NCORE * 512, NT // 4], [128, 4]], base=0,
                   channel_multiplier=1)
            g.iota(iota128[:, :], pattern=[[1, 1]], base=0, channel_multiplier=1)
            g.memset(neg1[:, :], -1)
            g.memset(ones1[:, :], 1.0)
            g.memset(ones18[:, :], 1.0)
            g.memset(ones8c[:, :], 1.0)
            g.memset(ones128b[:, :], 1.0)
            st["rb_slot"] = g.to_reg(E * RPAD - 1)
            st["rb_srow"] = g.to_reg(NCORE * E * 128 - 1)
            st["rb_xn"] = g.to_reg(NCORE * T4 - 1)
            st["rb_yA"] = g.to_reg(NCORE * 3840 - 1)
            st["rb_yB"] = g.to_reg(NCORE * 1408 - 1)
            s_gc.inc(g.drain())
            s_wc.inc(g.dma_start(out=wg_sb[:, :, :], in_=wg_in.rearrange("(dc p) e -> p dc e", p=128)), 16)
            s_wc.inc(g.dma_start(out=w1_sb[:, :, :], in_=w1_in.rearrange("(dc p) f -> p dc f", p=128)), 16)
            s_wc.inc(g.dma_start(out=w2_sb[:, :, :], in_=w2_in.rearrange("(fc p) d -> p fc d", p=128)), 16)
            s_gd.inc(g.dma_start(out=S_dram.ap().rearrange("(e p c) one -> p e (c one)", e=E, p=128),
                                 in_=neg1[:, :].rearrange("p (e c) -> p e c", e=E)), 16)
            s_gd.inc(g.dma_start(out=S_init.ap().rearrange("(e p c) one -> p e (c one)", e=E, p=128),
                                 in_=neg1[:, :].rearrange("p (e c) -> p e c", e=E)), 16)

        @block.vector
        def _(v):
            s_ld0.wait(v)
            s_gc.wait_at(v, 1)
            v.scalar_tensor_tensor(mbase[:, :], myc[:, :], 128.0, iota128[:, :], ALU.mult, ALU.add)
            v.drain()
            for r in range(NCORE):
                v.tensor_scalar(offf[:, r:r + 1], mbase[:, :], float(r * 1024), None, ALU.add)
            v.drain()
            s_v.inc(v.tensor_copy(midxu[:, :], offf[:, 0:E]))

        ps_tmp = c0.enter_context(nc.psum_tensor("ps_b2", [128, 512], F32))
        for dh in range(2):
            @block.tensor
            def _(t_, dh=dh):
                if dh == 0:
                    s_ld0.wait(t_)
                    s_v.wait(t_)
                else:
                    s_v.wait(t_)
                s_t.inc(t_.matmul(ps_tmp[:, :], ones1[:, :], b2r[:, dh * 512:(dh + 1) * 512],
                                  start=True, stop=True))

            @block.vector
            def _(v, dh=dh):
                s_t.wait(v)
                s_v.inc(v.tensor_copy(b2b[:, dh * 512:(dh + 1) * 512], ps_tmp[:, :]))
        c0.close()

        def emit_routing(call):
            with ExitStack() as rc:
                er = rc.enter_context
                sfx = f"_r{call}"
                idxrow = er(nc.sbuf_tensor("idxrow" + sfx, [1, T4], BF16))
                ohT = er(nc.sbuf_tensor("ohT" + sfx, [E, T4], BF16))
                rT = er(nc.sbuf_tensor("rT" + sfx, [E, T4], F32))
                prodc = er(nc.sbuf_tensor("prodc" + sfx, [E, 2, 512], F32))
                Ssb = er(nc.sbuf_tensor("Ssb" + sfx, [128, E, NC41], I32))
                mrg_m = er(nc.sbuf_tensor("mrg_m" + sfx, [128, NC41], F32))
                mrg_t = er(nc.sbuf_tensor("mrg_t" + sfx, [128, NC41], F32))
                gacc = er(nc.sbuf_tensor("gacc" + sfx, [128, NC41], F32))
                ps_lg = er(nc.psum_tensor("ps_lg" + sfx, [128, 2, E], F32))
                ps_row = er(nc.psum_tensor("ps_row" + sfx, [1, 2, 512], BF16))
                ps_bc = er(nc.psum_tensor("ps_bc" + sfx, [E, 2, 512], F32))
                ps_b8 = er(nc.psum_tensor("ps_b8" + sfx, [E, 1], F32))
                ps_cnt = er(nc.psum_tensor("ps_cnt" + sfx, [E, 1], F32))
                ps_rs = er(nc.psum_tensor("ps_rs" + sfx, [128, NT], F32))

                xnst0 = s_sst.n
                cc_cnt = s_cc[call * 4 + 0]
                cc_xn = s_cc[call * 4 + 1]
                if call == 0:
                    @block.gpsimd
                    def _(g):
                        for k in range(8):
                            g.wait_ge(s_sst.h, xnst0 + 32 * (2 * k + 2))
                            cc_xn.inc(g.collective_compute(
                                "AllGather", ALU.bypass, replica_groups=[list(range(NCORE))],
                                ins=[ag_xn_in.ap()[512 * k:512 * (k + 1), :].opt()],
                                outs=[ag_xn_out.ap()[NCORE * 512 * k:NCORE * 512 * (k + 1), :].opt()]))

                    for gp in range(NT // 2):
                        t0, t1 = 2 * gp, 2 * gp + 1

                        @block.sync
                        def _(sync, gp=gp, t0=t0, t1=t1, call=call):
                            if gp == 0:
                                if call == 1:
                                    s_t.wait(sync)
                                    s_v.wait(sync)
                                    s_a.wait(sync)
                                    s_sst.wait(sync)
                            if call == 0:
                                if gp >= 1:
                                    s_v.wait_at(sync, st[f"lnap_{call}_{gp-1}"])
                                s_ld.inc(sync.dma_start(out=xt[:, 0, :], in_=x_til[:, t0, :]), 16)
                                s_ld.inc(sync.dma_start(out=xt[:, 1, :], in_=x_til[:, t1, :]), 16)
                            else:
                                if gp >= 1:
                                    s_tr.wait_at(sync, st[f"trpp_{call}_{gp-1}"])
                                s_ld.inc(sync.dma_start(out=xnb[:, 0, 0, :], in_=agxn_til[:, t0, :]), 16)
                                s_ld.inc(sync.dma_start(out=xnb[:, 0, 1, :], in_=agxn_til[:, t1, :]), 16)

                        if call == 0:
                            @block.vector
                            def _(v, gp=gp, t0=t0, t1=t1):
                                s_ld.wait(v)
                                if gp >= 1:
                                    s_tr.wait_at(v, st[f"trpp_0_{gp-1}"])
                                    s_sst.wait_at(v, xnst0 + 32 * gp)
                                v.bn_stats(bns[:, 0, 0:6], xt[:, 0, 0:512])
                                v.bn_stats(bns[:, 0, 6:12], xt[:, 0, 512:1024])
                                v.bn_stats(bns[:, 1, 0:6], xt[:, 1, 0:512])
                                v.bn_stats(bns[:, 1, 6:12], xt[:, 1, 512:1024])
                                v.drain()
                                v.bn_aggr(mv[:, 0, :], bns[:, 0, :])
                                v.bn_aggr(mv[:, 1, :], bns[:, 1, :])
                                v.drain()
                                s_v.inc(v.tensor_scalar(rsd[:, :, :], mv[:, :, 1:2], EPS, None, ALU.add))

                            @block.scalar
                            def _(sc, gp=gp):
                                s_v.wait(sc)
                                sc.activation(out=rsd[:, :, :], in_=rsd[:, :, :], func=AF.Ln)
                                sc.drain()
                                s_a.inc(sc.activation(out=rsd[:, :, :], in_=rsd[:, :, :], func=AF.Exp, scale=-0.5))

                            @block.vector
                            def _(v, gp=gp):
                                s_a.wait(v)
                                v.tensor_scalar(xnb[:, 0, 0, :], xt[:, 0, :], mv[:, 0, 0:1],
                                                rsd[:, 0, :], ALU.subtract, ALU.mult)
                                s_v.inc(v.tensor_scalar(xnb[:, 0, 1, :], xt[:, 1, :], mv[:, 1, 0:1],
                                                        rsd[:, 1, :], ALU.subtract, ALU.mult))
                                st[f"lnap_0_{gp}"] = s_v.n

                            @block.scalar
                            def _(sc, gp=gp, t0=t0, t1=t1):
                                s_v.wait(sc)
                                s_sst.inc(sc.dma_start(out=agxn_til[:, t0, :], in_=xnb[:, 0, 0, :]), 16)
                                s_sst.inc(sc.dma_start(out=agxn_til[:, t1, :], in_=xnb[:, 0, 1, :]), 16)

                        @block.sync
                        def _(sync, gp=gp, call=call):
                            if call == 0:
                                s_v.wait(sync)
                            else:
                                s_ld.wait(sync)
                            if gp >= 1:
                                s_t.wait_at(sync, st[f"rtp_{call}_{gp-1}"])
                            s_tr.inc(sync.dma_start(out=xnf[:, 0, 0, :, :], in_=xnb[:, 0, 0, :], transpose=True), 16)
                            s_tr.inc(sync.dma_start(out=xnf[:, 0, 1, :, :], in_=xnb[:, 0, 1, :], transpose=True), 16)
                            st[f"trpp_{call}_{gp}"] = s_tr.n

                        @block.tensor
                        def _(t_, gp=gp, call=call):
                            s_tr.wait(t_)
                            if gp == 0:
                                s_wc.wait_at(t_, 16)
                            if gp >= 1:
                                s_v.wait_at(t_, st[f"shp_{call}_{gp-1}"])
                            inst = None
                            for par in range(2):
                                for dc in range(8):
                                    inst = t_.matmul(ps_lg[:, par, :], xnf[:, 0, par, dc, :], wg_sb[:, dc, :],
                                                     start=(dc == 0), stop=(dc == 7))
                            s_t.inc(inst)
                            st[f"rtp_{call}_{gp}"] = s_t.n

                        @block.vector
                        def _(v, gp=gp, t0=t0, t1=t1, call=call):
                            s_t.wait(v)
                            if gp >= 1:
                                s_a.wait_at(v, st[f"expp_{call}_{gp-1}"])
                                s_t.wait_at(v, st[f"cntp_{call}_{gp-1}"])
                            for par, tt in ((0, t0), (1, t1)):
                                v.max(mx8[:, :], ps_lg[:, par, :])
                                v.drain()
                                v.max_index(ix8[:, :], mx8[:, :], ps_lg[:, par, :])
                                v.drain()
                                v.tensor_copy(idxf[:, tt:tt + 1], ix8[:, 0:1])
                                v.tensor_copy(idc[:, :], ix8[:, 0:1])
                                v.drain()
                                v.tensor_scalar(ohtm[:, par, :], iotaef[:, :], idc[:, :], None, ALU.is_equal)
                                inst = v.tensor_scalar(sh8[:, par, :], ps_lg[:, par, :], mx8[:, 0:1], None, ALU.subtract)
                                v.drain()
                            s_v.inc(inst)
                            st[f"shp_{call}_{gp}"] = s_v.n

                        @block.scalar
                        def _(sc, gp=gp, t0=t0, t1=t1, call=call):
                            s_v.wait(sc)
                            sc.activation(out=she[:, 0, :], in_=sh8[:, 0, :], func=AF.Exp,
                                          accum_out=esum[:, t0:t0 + 1])
                            s_a.inc(sc.activation(out=she[:, 1, :], in_=sh8[:, 1, :], func=AF.Exp,
                                                  accum_out=esum[:, t1:t1 + 1]))
                            st[f"expp_{call}_{gp}"] = s_a.n

                        @block.tensor
                        def _(t_, gp=gp, call=call):
                            s_v.wait(t_)
                            inst = None
                            for par in range(2):
                                inst = t_.matmul(ps_cnt[:, :], ohtm[:, par, :], ones128b[:, :],
                                                 start=(gp == 0 and par == 0),
                                                 stop=(gp == NT // 2 - 1 and par == 1))
                            s_t.inc(inst)
                            st[f"cntp_{call}_{gp}"] = s_t.n
                            if gp == NT // 2 - 1:
                                st[f"cntmm_{call}"] = s_t.n

                    @block.vector
                    def _(v, call=call):
                        s_t.wait(v)
                        s_v.inc(v.tensor_copy(cnt8[:, :], ps_cnt[:, :]))
                        st[f"cntev_{call}"] = s_v.n
                        s_a.wait(v)
                        v.reciprocal(comb[:, :], esum[:, :])
                        v.drain()
                        s_v.inc(v.tensor_copy(idxw[:, :], idxf[:, :]))

                    @block.gpsimd
                    def _(g, call=call):
                        g.wait_ge(s_v.h, st[f"cntev_{call}"])
                        s_gd.inc(g.dma_start(out=cnt_dram.ap(), in_=cnt8[:, :]), 16)
                        s_gd.wait(g)
                        cc_cnt.inc(g.collective_compute(
                            "AllGather", ALU.bypass, replica_groups=[list(range(NCORE))],
                            ins=[cnt_dram.ap().opt()], outs=[cnt_ag.ap().opt()]))

                    for q in range(8):
                        @block.tensor
                        def _(t_, q=q):
                            s_v.wait(t_)
                            inst = None
                            for j in range(4):
                                tl = q * 4 + j
                                inst = t_.transpose(ps_row[:, q % 2, j * 128:(j + 1) * 128],
                                                    idxf[:, tl:tl + 1], idn[:, :])
                            s_t.inc(inst)

                        @block.vector
                        def _(v, q=q):
                            s_t.wait(v)
                            if q == 0:
                                s_gd.wait(v)
                            s_v.inc(v.tensor_copy(idxrow[:, q * 512:(q + 1) * 512], ps_row[:, q % 2, :]))

                    for q in range(8):
                        @block.tensor
                        def _(t_, q=q):
                            s_v.wait(t_)
                            s_t.inc(t_.matmul(ps_bc[:, q % 2, :], ones18[:, :], idxrow[:, q * 512:(q + 1) * 512],
                                              start=True, stop=True))

                        @block.vector
                        def _(v, q=q):
                            s_t.wait(v)
                            s_v.inc(v.tensor_scalar(ohT[:, q * 512:(q + 1) * 512], ps_bc[:, q % 2, :],
                                                    iota8c[:, :], None, ALU.is_equal))


                @block.sync
                def _(sync):
                    cc_cnt.wait_at(sync, 1)
                    s_ld.inc(sync.dma_start(out=cntall[:, :], in_=cnt_ag.ap()[:, :]), 16)

                @block.tensor
                def _(t_):
                    s_ld.wait(t_)
                    s_t.inc(t_.matmul(ps_b8[:, :], cntall[:, :], premask[:, :], start=True, stop=True))

                @block.vector
                def _(v):
                    s_t.wait(v)
                    v.tensor_copy(base8[:, :], ps_b8[:, :])
                    v.drain()
                    s_v.inc(v.tensor_tensor_scan(rT[:, :], ohT[:, :], ohT[:, :], base8[:, :],
                                                 ALU.add, ALU.max))

                for q in range(8):
                    @block.vector
                    def _(v, q=q):
                        if q >= 2:
                            s_t.wait_at(v, st[f"rsmm_{call}_{q-2}"])
                        v.drain()
                        s_v.inc(v.tensor_tensor(prodc[:, q % 2, :], ohT[:, q * 512:(q + 1) * 512],
                                                rT[:, q * 512:(q + 1) * 512], ALU.mult))

                    @block.tensor
                    def _(t_, q=q, call=call):
                        s_v.wait(t_)
                        inst = None
                        for j in range(4):
                            tl = q * 4 + j
                            inst = t_.matmul(ps_rs[:, tl:tl + 1], prodc[:, q % 2, j * 128:(j + 1) * 128],
                                             ones8c[:, :], start=True, stop=True)
                        s_t.inc(inst)
                        st[f"rsmm_{call}_{q}"] = s_t.n

                @block.vector
                def _(v):
                    s_t.wait(v)
                    v.tensor_copy(rsel[:, :], ps_rs[:, :])
                    v.drain()
                    v.tensor_scalar(vld[:, :], rsel[:, :], float(CAP) + 0.5, None, ALU.is_le)
                    v.drain()
                    v.tensor_tensor(wsc[:, :], comb[:, :], vld[:, :], ALU.mult)
                    v.drain()
                    v.tensor_scalar(wsc[:, :], wsc[:, :], 0.5, None, ALU.mult)
                    v.tensor_scalar(off2[:, :], rsel[:, :], -1.0, 5120.0, ALU.add, ALU.min)
                    v.drain()
                    v.tensor_scalar(offf[:, :], off2[:, :], 3839.5, None, ALU.is_ge)
                    v.tensor_scalar(scc[:, :], off2[:, :], 4607.5, None, ALU.is_ge)
                    v.scalar_tensor_tensor(gretf[:, :], idxw[:, :], 3840.0, off2[:, :], ALU.mult, ALU.add)
                    v.tensor_scalar(spp[:, :], idxw[:, :], -128.0, 5376.0, ALU.mult, ALU.add)
                    v.drain()
                    v.scalar_tensor_tensor(gretf[:, :], offf[:, :], BIG, gretf[:, :], ALU.mult, ALU.add)
                    v.scalar_tensor_tensor(off2[:, :], idxw[:, :], 768.0, off2[:, :], ALU.mult, ALU.add)
                    v.tensor_tensor(scc[:, :], scc[:, :], spp[:, :], ALU.mult)
                    v.drain()
                    s_v.inc(v.tensor_copy(gretu[:, :], gretf[:, :]))
                    v.tensor_scalar(off2[:, :], off2[:, :], -3840.0, None, ALU.add)
                    v.tensor_scalar(offf[:, :], offf[:, :], -1.0, 1.0, ALU.mult, ALU.add)
                    v.drain()
                    v.tensor_tensor(off2[:, :], off2[:, :], scc[:, :], ALU.add)
                    v.drain()
                    v.scalar_tensor_tensor(off2[:, :], offf[:, :], BIG, off2[:, :], ALU.mult, ALU.add)
                    v.drain()
                    s_v.inc(v.tensor_copy(gretu2[:, :], off2[:, :]))
                    v.tensor_scalar(spp[:, :], rsel[:, :], -1.0, None, ALU.add)
                    v.drain()
                    v.tensor_scalar(scc[:, :], spp[:, :], 1.0 / 128.0, C23, ALU.mult, ALU.add)
                    v.drain()
                    v.tensor_scalar(scc[:, :], scc[:, :], -C23, None, ALU.add)
                    v.drain()
                    v.scalar_tensor_tensor(spp[:, :], scc[:, :], -128.0, spp[:, :], ALU.mult, ALU.add)
                    v.drain()
                    v.tensor_scalar(offf[:, :], spp[:, :], 0.0, None, ALU.is_lt)
                    v.drain()
                    v.tensor_tensor(scc[:, :], scc[:, :], offf[:, :], ALU.subtract)
                    v.scalar_tensor_tensor(spp[:, :], offf[:, :], 128.0, spp[:, :], ALU.mult, ALU.add)
                    v.drain()
                    v.scalar_tensor_tensor(off2[:, :], spp[:, :], 41.0, scc[:, :], ALU.mult, ALU.add)
                    v.drain()
                    v.scalar_tensor_tensor(offf[:, :], idxw[:, :], float(RPAD), off2[:, :], ALU.mult, ALU.add)
                    v.drain()
                    v.tensor_scalar(off2[:, :], vld[:, :], -BIG, BIG, ALU.mult, ALU.add)
                    v.drain()
                    v.tensor_tensor(offf[:, :], offf[:, :], off2[:, :], ALU.add)
                    v.drain()
                    s_v.inc(v.tensor_copy(offu[:, :], offf[:, :]))

                cc_S = s_cc[call * 4 + 2]

                @block.gpsimd
                def _(g, call=call):
                    s_v.wait(g)
                    s_gd.wait(g)
                    for t in range(NT):
                        s_gd.inc(g.indirect_dma_start(
                            out=S_dram.ap(),
                            out_offset=IndirectOffsetOnAxis(offu[:, t:t + 1], 0),
                            in_=locid[:, t:t + 1], in_offset=None,
                            bounds_check=st["rb_slot"], oob_is_err=False,
                        ), 16)
                    g.wait_ge(s_gd.h, s_gd.n)
                    cc_S.inc(g.collective_compute(
                        "AllGather", ALU.bypass, replica_groups=[list(range(NCORE))],
                        ins=[S_dram.ap().opt()], outs=[S_ag.ap().opt()]))
                    cc_S.wait_at(g, 1)
                    for r in range(NCORE):
                        s_gd.inc(g.indirect_dma_start(
                            out=Ssb[:, r, :], out_offset=None,
                            in_=S_rows, in_offset=IndirectOffsetOnAxis(midxu[:, r:r + 1], 0),
                            bounds_check=st["rb_srow"], oob_is_err=False,
                        ), 16)

                @block.vector
                def _(v):
                    s_gd.wait(v)
                    for r in range(NCORE):
                        v.tensor_scalar(mrg_m[:, :], Ssb[:, r, :], -0.5, None, ALU.is_ge)
                        v.tensor_scalar(mrg_t[:, :], Ssb[:, r, :], float(r * 512), None, ALU.add)
                        v.drain()
                        if r == 0:
                            v.tensor_tensor(gacc[:, :], mrg_t[:, :], mrg_m[:, :], ALU.mult)
                        else:
                            v.tensor_tensor(mrg_t[:, :], mrg_t[:, :], mrg_m[:, :], ALU.mult)
                            v.drain()
                            v.tensor_tensor(gacc[:, :], gacc[:, :], mrg_t[:, :], ALU.add)
                        v.drain()
                    s_v.inc(v.tensor_copy(g_u[:, :], gacc[:, :]))
                    st[f"routing_done_{call}"] = s_v.n

        def emit_expert(call):
            with ExitStack() as fc:
                ef = fc.enter_context
                sfx = f"_f{call}"
                xg = ef(nc.sbuf_tensor("xg" + sfx, [128, 2, 3, D], BF16))
                xgf = ef(nc.sbuf_tensor("xgf" + sfx, [128, 2, 3, 8, 128], BF16, is_transpose_dest=True))
                h_sb = ef(nc.sbuf_tensor("h_sb" + sfx, [128, 16, CHT], BF16))
                y_sb = ef(nc.sbuf_tensor("y_sb" + sfx, [128, 3, D], BF16))
                ps_ha = [ef(nc.psum_tensor(f"ps_h{i}" + sfx, [128, CHT], F32)) for i in range(2)]
                ps_y = [[ef(nc.psum_tensor(f"ps_y{stt}{dh}" + sfx, [128, 512], F32)) for dh in range(2)]
                        for stt in range(3)]

                fill0 = s_fill.n
                tr0 = s_tr.n
                m20 = s_m2.n
                ev0 = s_v.n
                st0 = s_sst.n
                st[f"yst_{call}_base"] = st0
                routing_done = st[f"routing_done_{call}"]
                cc_xn = s_cc[call * 4 + 1]
                cc_y = s_cc[call * 4 + 3]

                ntile = NC41
                chunks = [(ch, min(3, ntile - ch * 3)) for ch in range(NCH)]

                for ch, nk in chunks:
                    par = ch % 2

                    @block.gpsimd
                    def _(g, ch=ch, nk=nk, par=par):
                        if ch == 0:
                            s_v.wait_at(g, routing_done)
                            cc_xn.wait_at(g, 8)
                        if ch >= 2:
                            s_tr.wait_at(g, st[f"xtr_{call}_{ch-2}"])
                        for k in range(nk):
                            s_gd.inc(g.indirect_dma_start(
                                out=xg[:, par, k, :], out_offset=None,
                                in_=ag_xn_out.ap(), in_offset=IndirectOffsetOnAxis(
                                    g_u[:, ch * 3 + k:ch * 3 + k + 1], 0),
                                bounds_check=st["rb_xn"], oob_is_err=False,
                            ), 16)

                    @block.sync
                    def _(sync, ch=ch, nk=nk, par=par):
                        s_gd.wait(sync)
                        if ch >= 2:
                            s_fill.wait_at(sync, fill0 + 32 * (ch - 1))
                        inst = None
                        for k in range(nk):
                            s_tr.inc(sync.dma_start(out=xgf[:, par, k, :, :], in_=xg[:, par, k, :],
                                                    transpose=True), 16)
                        st[f"xtr_{call}_{ch}"] = s_tr.n

                    if ch == 11:
                        @block.gpsimd
                        def _(g, call=call):
                            g.wait_ge(s_sst.h, st[f"yst_{call}_base"] + 10 * 16)
                            s_ccy[call * 3].inc(g.collective_compute(
                                "AllGather", ALU.bypass, replica_groups=[list(range(NCORE))],
                                ins=[y_dA.ap().opt()], outs=[y_agA.ap().opt()]))
                    if ch == 13:
                        @block.gpsimd
                        def _(g, call=call):
                            g.wait_ge(s_sst.h, st[f"yst_{call}_base"] + 12 * 16)
                            s_ccy[call * 3 + 1].inc(g.collective_compute(
                                "AllGather", ALU.bypass, replica_groups=[list(range(NCORE))],
                                ins=[y_dB.ap()[0:768, :].opt()],
                                outs=[y_agB.ap()[0:NCORE * 768, :].opt()]))

                    for fh in range(2):
                        fillv = []
                        abase = s_a.n

                        for ft in range(16):
                            fg = fh * 16 + ft

                            @block.tensor
                            def _(t_, ch=ch, nk=nk, par=par, fh=fh, ft=ft, fg=fg, abase=abase):
                                if fh == 0 and ft == 0:
                                    s_tr.wait(t_)
                                    if ch == 0:
                                        s_wc.wait(t_)
                                if ft >= 2:
                                    s_a.wait_at(t_, abase + ft - 1)
                                elif s_a.n > 0:
                                    s_a.wait_at(t_, s_a.n)
                                inst = None
                                for dc in range(8):
                                    inst = t_.matmul(ps_ha[ft % 2][:, 0:nk * 128],
                                                     w1_sb[:, dc, fg * 128:(fg + 1) * 128],
                                                     xgf[:, par, 0:nk, dc, :],
                                                     start=(dc == 0), stop=(dc == 7))
                                s_fill.inc(inst)
                                fillv.append(s_fill.n)

                        @block.scalar
                        def _(sc, ch=ch, nk=nk, fh=fh, fillv=fillv):
                            for ft in range(16):
                                fg = fh * 16 + ft
                                s_fill.wait_at(sc, fillv[ft])
                                s_a.inc(sc.activation(out=h_sb[:, ft, 0:nk * 128], in_=ps_ha[ft % 2][:, 0:nk * 128],
                                                      func=AF.Gelu, bias=b1_sb[:, fg:fg + 1], scale=1.0))

                        @block.tensor
                        def _(t_, ch=ch, nk=nk, fh=fh):
                            s_a.wait(t_)
                            if fh == 0:
                                s_v.wait_at(t_, ev0 + ch * 6)
                            for stt in range(3 if nk == 3 else nk):
                                for dh in range(2):
                                    inst = None
                                    for fc in range(16):
                                        inst = t_.matmul(ps_y[stt][dh][:, :],
                                                         h_sb[:, fc, stt * 128:(stt + 1) * 128],
                                                         w2_sb[:, fh * 16 + fc, dh * 512:(dh + 1) * 512],
                                                         start=(fh == 0 and fc == 0),
                                                         stop=(fh == 1 and fc == 15))
                                    if fh == 1:
                                        s_m2.inc(inst)

                    @block.vector
                    def _(v, ch=ch, nk=nk, par=par):
                        s_m2.wait(v)
                        s_sst.wait_at(v, st0 + ch * 16)
                        inst = None
                        for stt in range(nk):
                            for dh in range(2):
                                inst = v.tensor_tensor(y_sb[:, stt, dh * 512:(dh + 1) * 512],
                                                       ps_y[stt][dh][:, :],
                                                       b2b[:, dh * 512:(dh + 1) * 512], ALU.add)
                        s_v.inc(inst, 6)

                    @block.scalar
                    def _(sc, ch=ch, nk=nk):
                        s_v.wait(sc)
                        if ch < 10:
                            s_sst.inc(sc.dma_start(out=y_tilA[:, ch * 3:ch * 3 + nk, :],
                                                   in_=y_sb[:, 0:nk, :]), 16)
                        else:
                            s_sst.inc(sc.dma_start(out=y_tilB[:, (ch - 10) * 3:(ch - 10) * 3 + nk, :],
                                                   in_=y_sb[:, 0:nk, :]), 16)



                @block.gpsimd
                def _(g):
                    s_sst.wait(g)
                    s_ccy[call * 3 + 1].inc(g.collective_compute(
                        "AllGather", ALU.bypass, replica_groups=[list(range(NCORE))],
                        ins=[y_dB.ap()[768:1408, :].opt()],
                        outs=[y_agB.ap()[NCORE * 768:NCORE * 1408, :].opt()]))

        def emit_return(call):
            rc = ctx.enter_context(ExitStack())
            ygat = rc.enter_context(nc.sbuf_tensor(f"ygat{call}", [128, 2, 2, D], BF16))
            xtr = rc.enter_context(nc.sbuf_tensor(f"xtr{call}", [128, 2, 2, D], F32))
            xnwr = rc.enter_context(nc.sbuf_tensor(f"xnwr{call}", [128, 2, 2, D], F32))
            if call == 0:
                xnbr = rc.enter_context(nc.sbuf_tensor(f"xnbr{call}", [128, 2, 2, D], BF16))
            for gp in range(NT // 2):
                t0, t1 = 2 * gp, 2 * gp + 1

                @block.gpsimd
                def _(g, gp=gp, t0=t0, t1=t1, call=call):
                    if gp == 0:
                        s_ccy[call * 3].wait_at(g, 1)
                        s_ccy[call * 3 + 1].wait_at(g, 2)
                    if gp >= 2:
                        s_v.wait_at(g, st[f"resp_{call}_{gp-2}"])
                    sl = gp % 2
                    for par, tt in ((0, t0), (1, t1)):
                        s_gd.inc(g.indirect_dma_start(
                            out=ygat[:, sl, par, :], out_offset=None,
                            in_=y_agA.ap(), in_offset=IndirectOffsetOnAxis(gretu[:, tt:tt + 1], 0),
                            bounds_check=st["rb_yA"], oob_is_err=False,
                        ), 16)
                        s_gd.inc(g.indirect_dma_start(
                            out=ygat[:, sl, par, :], out_offset=None,
                            in_=y_agB.ap(), in_offset=IndirectOffsetOnAxis(gretu2[:, tt:tt + 1], 0),
                            bounds_check=st["rb_yB"], oob_is_err=False,
                        ), 16)

                @block.sync
                def _(sync, gp=gp, t0=t0, t1=t1, call=call):
                    if gp < 2:
                        s_ccy[call * 3 + 1].wait_at(sync, 2)
                    else:
                        s_v.wait_at(sync, st[f"resp_{call}_{gp-2}"])
                    sl = gp % 2
                    src = x_til if call == 0 else x2_til
                    s_ld.inc(sync.dma_start(out=xtr[:, sl, 0, :], in_=src[:, t0, :]), 16)
                    s_ld.inc(sync.dma_start(out=xtr[:, sl, 1, :], in_=src[:, t1, :]), 16)

                @block.vector
                def _(v, gp=gp, t0=t0, t1=t1, call=call):
                    s_gd.wait(v)
                    s_ld.wait(v)
                    if gp >= 2:
                        s_sst.wait_at(v, st[f"xstp_{call}_{gp-2}"])
                    sl = gp % 2
                    v.scalar_tensor_tensor(xnwr[:, sl, 0, :], ygat[:, sl, 0, :], wsc[:, t0:t0 + 1],
                                           xtr[:, sl, 0, :], ALU.mult, ALU.add)
                    inst = v.scalar_tensor_tensor(xnwr[:, sl, 1, :], ygat[:, sl, 1, :], wsc[:, t1:t1 + 1],
                                                  xtr[:, sl, 1, :], ALU.mult, ALU.add)
                    s_v.inc(inst)
                    st[f"resp_{call}_{gp}"] = s_v.n
                    if call == 0:
                        v.drain()
                        v.bn_stats(bns[:, 0, 0:6], xnwr[:, sl, 0, 0:512])
                        v.bn_stats(bns[:, 0, 6:12], xnwr[:, sl, 0, 512:1024])
                        v.bn_stats(bns[:, 1, 0:6], xnwr[:, sl, 1, 0:512])
                        v.bn_stats(bns[:, 1, 6:12], xnwr[:, sl, 1, 512:1024])
                        v.drain()
                        v.bn_aggr(mv[:, 0, :], bns[:, 0, :])
                        v.bn_aggr(mv[:, 1, :], bns[:, 1, :])
                        v.drain()
                        s_v.inc(v.tensor_scalar(rsd[:, :, :], mv[:, :, 1:2], EPS, None, ALU.add))

                if call == 0:
                    @block.scalar
                    def _(sc, gp=gp):
                        s_v.wait(sc)
                        sc.activation(out=rsd[:, :, :], in_=rsd[:, :, :], func=AF.Ln)
                        sc.drain()
                        s_a.inc(sc.activation(out=rsd[:, :, :], in_=rsd[:, :, :], func=AF.Exp, scale=-0.5))

                    @block.vector
                    def _(v, gp=gp):
                        s_a.wait(v)
                        if gp >= 2:
                            s_sst.wait_at(v, st[f"xn2p_{gp-2}"])
                        sl = gp % 2
                        v.tensor_scalar(xnbr[:, sl, 0, :], xnwr[:, sl, 0, :], mv[:, 0, 0:1],
                                        rsd[:, 0, :], ALU.subtract, ALU.mult)
                        s_v.inc(v.tensor_scalar(xnbr[:, sl, 1, :], xnwr[:, sl, 1, :], mv[:, 1, 0:1],
                                                rsd[:, 1, :], ALU.subtract, ALU.mult))

                    @block.scalar
                    def _(sc, gp=gp, t0=t0, t1=t1):
                        s_v.wait(sc)
                        sl = gp % 2
                        s_sst.inc(sc.dma_start(out=x1_til[:, t0, :], in_=xnwr[:, sl, 0, :]), 16)
                        s_sst.inc(sc.dma_start(out=x1_til[:, t1, :], in_=xnwr[:, sl, 1, :]), 16)
                        st[f"xstp_0_{gp}"] = s_sst.n
                        s_sst.inc(sc.dma_start(out=xn2_til[:, t0, :], in_=xnbr[:, sl, 0, :]), 16)
                        s_sst.inc(sc.dma_start(out=xn2_til[:, t1, :], in_=xnbr[:, sl, 1, :]), 16)
                        st[f"xn2p_{gp}"] = s_sst.n
                else:
                    @block.scalar
                    def _(sc, gp=gp, t0=t0, t1=t1):
                        s_v.wait(sc)
                        sl = gp % 2
                        s_sst.inc(sc.dma_start(out=xo_til[:, t0, :], in_=xnwr[:, sl, 0, :]), 16)
                        s_sst.inc(sc.dma_start(out=xo_til[:, t1, :], in_=xnwr[:, sl, 1, :]), 16)
                        st[f"xstp_1_{gp}"] = s_sst.n
            rc.close()

        def emit_attention():
            with ExitStack() as ac:
                ea = ac.enter_context
                qkp_sb = ea(nc.sbuf_tensor("qkp_sb", [128, 8, 2 * PD], BF16))
                xnf2 = ea(nc.sbuf_tensor("xnf2", [128, 2, 2, 8, 128], BF16, is_transpose_dest=True))
                lga = ea(nc.sbuf_tensor("lga", [128, 16, 16], F32))
                aw = ea(nc.sbuf_tensor("aw", [128, 2, 2, D], F32))

                def emit_router_mm(b, g):
                    lgb = ps_kv[1][0]

                    @block.tensor
                    def _(t_, g=g, b=b):
                        s_tr.wait_at(t_, st[f"a2trp_{b}_{g}"])
                        if g == 0:
                            s_v.wait_at(t_, st[f"kvz_{b}"])
                        prev = st.get(f"a2sh_{b}_{g-1}", st.get(f"a2sh_{b-1}_7", 0))
                        s_v.wait_at(t_, prev)
                        inst = None
                        for par in range(2):
                            for dc in range(8):
                                inst = t_.matmul(lgb[:, par * 8:(par + 1) * 8],
                                                 xnf2[:, g % 2, par, dc, :], wg_sb[:, dc, :],
                                                 start=(dc == 0), stop=(dc == 7))
                        s_t.inc(inst)
                        st[f"a2mm_{b}_{g}"] = s_t.n

                    @block.vector
                    def _(v, g=g, b=b):
                        s_t.wait_at(v, st[f"a2mm_{b}_{g}"])
                        s_v.inc(v.tensor_copy(lga[:, b * 8 + g, :], lgb[:, 0:16]))
                        st[f"a2sh_{b}_{g}"] = s_v.n
                qe_all = ea(nc.sbuf_tensor("qe_all", [128, 4, PD], BF16))
                qT = ea(nc.sbuf_tensor("qT", [128, 16, 2, 128], BF16, is_transpose_dest=True))
                ke = ea(nc.sbuf_tensor("ke", [128, 2, PD], BF16))
                kv_sb = ea(nc.sbuf_tensor("kv_sb", [128, 2, D], BF16))
                zsum = ea(nc.sbuf_tensor("zsum", [128, 2, 16], F32))
                zinv = ea(nc.sbuf_tensor("zinv", [128, 2, 16], F32))
                ps_qk = [ea(nc.psum_tensor(f"ps_qk{i}", [128, 512], F32)) for i in range(2)]
                ps_kv = [[ea(nc.psum_tensor(f"ps_kv{eh}{dh}", [128, 512], F32)) for dh in range(2)]
                         for eh in range(2)]
                ps_nm = [ea(nc.psum_tensor(f"ps_nm{i}", [128, 512], F32)) for i in range(2)]

                @block.gpsimd
                def _(g):
                    s_sst.wait_at(g, st["xn2p_15"])
                    s_wc.inc(g.dma_start(out=qkp_sb[:, :, :],
                                         in_=qk_in.rearrange("(dc p) e -> p dc e", p=128)), 16)

                qkb = [ps_qk, ps_nm]

                def emit_kv(t_, gp, b):
                    s_a.wait_at(t_, st[f"expkp_{b}_{gp}"])
                    if b > 0 and gp == 0:
                        s_v.wait_at(t_, st[f"aresp_{b-1}_7"])
                        s_v.wait_at(t_, st[f"a2sh_{b-1}_7"])
                    sl = gp % 2
                    inst = None
                    for par in range(2):
                        for eh in range(2):
                            for dh in range(2):
                                inst = t_.matmul(ps_kv[eh][dh][:, :],
                                                 ke[:, par, eh * 128:(eh + 1) * 128],
                                                 xnb[:, sl, par, dh * 512:(dh + 1) * 512],
                                                 start=(gp == 0 and par == 0),
                                                 stop=(gp == 7 and par == 1))
                    s_t.inc(inst)
                    st[f"kvp_{b}_{gp}"] = s_t.n

                for b in (ATT_BATCHES if ATT_BATCHES is not None else range(B)):
                    for gp in range(8):
                        i0, i1 = 2 * gp, 2 * gp + 1
                        t0, t1 = b * 16 + i0, b * 16 + i1
                        sl = gp % 2

                        @block.sync
                        def _(sync, gp=gp, t0=t0, t1=t1, b=b, sl=sl):
                            if b == 0 and gp == 0:
                                s_sst.wait(sync)
                            if b > 0 and gp == 0:
                                s_t.wait_at(sync, st[f"kvp_{b-1}_7"])
                                s_sst.wait_at(sync, st[f"axp_{b-1}_7"])
                                s_tr.wait_at(sync, st[f"a2trp_{b-1}_7"])
                            if gp >= 2:
                                s_t.wait_at(sync, st[f"kvp_{b}_{gp-2}"])
                            s_ld.inc(sync.dma_start(out=xnb[:, sl, 0, :], in_=xn2_til[:, t0, :]), 16)
                            s_ld.inc(sync.dma_start(out=xnb[:, sl, 1, :], in_=xn2_til[:, t1, :]), 16)
                            s_ld.wait(sync)
                            s_tr.inc(sync.dma_start(out=xnf[:, sl, 0, :, :], in_=xnb[:, sl, 0, :], transpose=True), 16)
                            s_tr.inc(sync.dma_start(out=xnf[:, sl, 1, :, :], in_=xnb[:, sl, 1, :], transpose=True), 16)
                            st[f"atr_{b}_{gp}"] = s_tr.n
                            if gp >= 2:
                                qgp = gp - 2
                                j0, j1 = 2 * qgp, 2 * qgp + 1
                                s_a.wait_at(sync, st[f"expkp_{b}_{qgp}"])
                                if b > 0 and qgp <= 1:
                                    s_v.wait_at(sync, st[f"aresp_{b-1}_7"])
                                s_tr.inc(sync.dma_start(out=qT[:, j0, :, :], in_=qe_all[:, j0 % 4, :], transpose=True), 16)
                                s_tr.inc(sync.dma_start(out=qT[:, j1, :, :], in_=qe_all[:, j1 % 4, :], transpose=True), 16)
                                st[f"qtr_{b}_{qgp}"] = s_tr.n

                        @block.tensor
                        def _(t_, gp=gp, b=b, sl=sl):
                            s_tr.wait_at(t_, st[f"atr_{b}_{gp}"])
                            if gp == 0:
                                s_wc.wait(t_)
                            prev = st.get(f"expkp_{b}_{gp-2}", 0)
                            if prev:
                                s_a.wait_at(t_, prev)
                            if b > 0 and gp == 1:
                                s_v.wait_at(t_, st[f"aresp_{b-1}_7"])
                            qk = qkb[sl]
                            inst = None
                            for par in range(2):
                                for dc in range(8):
                                    inst = t_.matmul(qk[par][:, :], xnf[:, sl, par, dc, :], qkp_sb[:, dc, :],
                                                     start=(dc == 0), stop=(dc == 7))
                            s_t.inc(inst)
                            st[f"qkp_{b}_{gp}"] = s_t.n

                        if gp >= 1:
                            @block.tensor
                            def _(t_, gp=gp - 1, b=b):
                                emit_kv(t_, gp, b)

                        @block.scalar
                        def _(sc, gp=gp, i0=i0, i1=i1, b=b, sl=sl):
                            s_t.wait_at(sc, st[f"qkp_{b}_{gp}"])
                            prevk = st.get(f"kvp_{b}_{gp-1}", 0)
                            if prevk:
                                s_t.wait_at(sc, prevk)
                            prev = st.get(f"qtr_{b}_{gp-2}", st.get(f"qtr_{b-1}_{gp+6}", 0))
                            if prev:
                                s_tr.wait_at(sc, prev)
                            qk = qkb[sl]
                            sc.activation(out=qe_all[:, i0 % 4, :], in_=qk[0][:, 0:PD], func=AF.Exp,
                                          accum_out=zsum[:, b, i0:i0 + 1])
                            sc.activation(out=ke[:, 0, :], in_=qk[0][:, PD:2 * PD], func=AF.Exp)
                            sc.activation(out=qe_all[:, i1 % 4, :], in_=qk[1][:, 0:PD], func=AF.Exp,
                                          accum_out=zsum[:, b, i1:i1 + 1])
                            s_a.inc(sc.activation(out=ke[:, 1, :], in_=qk[1][:, PD:2 * PD], func=AF.Exp))
                            st[f"expkp_{b}_{gp}"] = s_a.n

                    @block.sync
                    def _(sync, b=b):
                        for qgp in (6, 7):
                            j0, j1 = 2 * qgp, 2 * qgp + 1
                            s_a.wait_at(sync, st[f"expkp_{b}_{qgp}"])
                            s_tr.inc(sync.dma_start(out=qT[:, j0, :, :], in_=qe_all[:, j0 % 4, :], transpose=True), 16)
                            s_tr.inc(sync.dma_start(out=qT[:, j1, :, :], in_=qe_all[:, j1 % 4, :], transpose=True), 16)
                            st[f"qtr_{b}_{qgp}"] = s_tr.n

                    @block.tensor
                    def _(t_, b=b):
                        emit_kv(t_, 7, b)

                    @block.vector
                    def _(v, b=b):
                        s_t.wait(v)
                        v.drain()
                        for eh in range(2):
                            for dh in range(2):
                                v.tensor_copy(kv_sb[:, eh, dh * 512:(dh + 1) * 512], ps_kv[eh][dh][:, :])
                        v.tensor_scalar(zinv[:, b, :], zsum[:, b, :], 1e-8, None, ALU.add)
                        v.drain()
                        s_v.inc(v.reciprocal(zinv[:, b, :], zinv[:, b, :]))
                        st[f"kvz_{b}"] = s_v.n

                    if DEBUG:
                        @block.scalar
                        def _(sc, b=b):
                            s_v.wait(sc)
                            s_sst.inc(sc.dma_start(out=dbg_z[b, :, :], in_=zinv[:, b, :]), 16)
                            s_sst.inc(sc.dma_start(out=dbg_kv[b, :, :], in_=kv_sb[:, :, :].rearrange("p a b -> p (a b)")), 16)
                            s_sst.inc(sc.dma_start(out=dbg_qe[b, :, :], in_=qe_all[:, :, :].rearrange("p a b -> p (a b)")), 16)

                    nmA = ps_nm
                    nmB = [ps_kv[0][0], ps_kv[0][1]]
                    for gp in range(8):
                        i0, i1 = 2 * gp, 2 * gp + 1
                        t0, t1 = b * 16 + i0, b * 16 + i1

                        if gp == 0:
                            @block.sync
                            def _(sync, b=b):
                                if b > 0:
                                    s_v.wait_at(sync, st[f"aresp_{b-1}_7"])
                                s_ld.inc(sync.dma_start(out=xt[:, 0, :], in_=x1_til[:, b * 16, :]), 16)
                                s_ld.inc(sync.dma_start(out=xt[:, 1, :], in_=x1_til[:, b * 16 + 1, :]), 16)

                        @block.tensor
                        def _(t_, gp=gp, i0=i0, i1=i1, b=b):
                            if gp == 0:
                                s_v.wait_at(t_, st[f"kvz_{b}"])
                                if b > 0:
                                    s_v.wait_at(t_, st[f"aresp_{b-1}_7"])
                            if gp >= 1:
                                s_v.wait_at(t_, st[f"aresp_{b}_{gp-1}"])
                            inst = None
                            for nmp, ii in ((nmA, i0), (nmB, i1)):
                                for dh in range(2):
                                    for eh in range(2):
                                        inst = t_.matmul(nmp[dh][:, :], qT[:, ii, eh, :],
                                                         kv_sb[:, eh, dh * 512:(dh + 1) * 512],
                                                         start=(eh == 0), stop=(eh == 1))
                            s_t.inc(inst)

                        @block.vector
                        def _(v, gp=gp, i0=i0, i1=i1, b=b):
                            s_t.wait(v)
                            s_ld.wait(v)
                            sl = gp % 2
                            prev = st.get(f"axp_{b}_{gp-2}", st.get(f"axp_{b-1}_{gp+6}", 0))
                            if prev:
                                s_sst.wait_at(v, prev)
                            for par, (nmp, ii) in enumerate(((nmA, i0), (nmB, i1))):
                                for dh in range(2):
                                    inst = v.scalar_tensor_tensor(aw[:, sl, par, dh * 512:(dh + 1) * 512],
                                                                  nmp[dh][:, :], zinv[:, b, ii:ii + 1],
                                                                  xt[:, par, dh * 512:(dh + 1) * 512],
                                                                  ALU.mult, ALU.add)
                            s_v.inc(inst)
                            st[f"aresp_{b}_{gp}"] = s_v.n
                            v.drain()
                            v.bn_stats(bns[:, 0, 0:6], aw[:, sl, 0, 0:512])
                            v.bn_stats(bns[:, 0, 6:12], aw[:, sl, 0, 512:1024])
                            v.bn_stats(bns[:, 1, 0:6], aw[:, sl, 1, 0:512])
                            v.bn_stats(bns[:, 1, 6:12], aw[:, sl, 1, 512:1024])
                            v.drain()
                            v.bn_aggr(mv[:, 0, :], bns[:, 0, :])
                            v.bn_aggr(mv[:, 1, :], bns[:, 1, :])
                            v.drain()
                            s_v.inc(v.tensor_scalar(rsd[:, :, :], mv[:, :, 1:2], EPS, None, ALU.add))

                        if gp < 7:
                            @block.sync
                            def _(sync, gp=gp, t0=t0, t1=t1, b=b):
                                s_v.wait_at(sync, st[f"aresp_{b}_{gp}"])
                                s_ld.inc(sync.dma_start(out=xt[:, 0, :], in_=x1_til[:, t0 + 2, :]), 16)
                                s_ld.inc(sync.dma_start(out=xt[:, 1, :], in_=x1_til[:, t1 + 2, :]), 16)

                        @block.scalar
                        def _(sc, gp=gp):
                            s_v.wait(sc)
                            sc.activation(out=rsd[:, :, :], in_=rsd[:, :, :], func=AF.Ln)
                            sc.drain()
                            s_a.inc(sc.activation(out=rsd[:, :, :], in_=rsd[:, :, :], func=AF.Exp, scale=-0.5))

                        @block.vector
                        def _(v, gp=gp, b=b):
                            s_a.wait(v)
                            sl = gp % 2
                            if gp >= 1:
                                s_sst.wait_at(v, st[f"axp_{b}_{gp-1}"])
                                s_tr.wait_at(v, st[f"a2trp_{b}_{gp-1}"])
                            v.tensor_scalar(xnb[:, 0, 0, :], aw[:, sl, 0, :], mv[:, 0, 0:1],
                                            rsd[:, 0, :], ALU.subtract, ALU.mult)
                            s_v.inc(v.tensor_scalar(xnb[:, 0, 1, :], aw[:, sl, 1, :], mv[:, 1, 0:1],
                                                    rsd[:, 1, :], ALU.subtract, ALU.mult))
                            st[f"aply_{b}_{gp}"] = s_v.n

                        @block.scalar
                        def _(sc, gp=gp, t0=t0, t1=t1, b=b):
                            s_v.wait(sc)
                            sl = gp % 2
                            s_sst.inc(sc.dma_start(out=x2_til[:, t0, :], in_=aw[:, sl, 0, :]), 16)
                            s_sst.inc(sc.dma_start(out=agxn_til[:, t0, :], in_=xnb[:, 0, 0, :]), 16)
                            s_sst.inc(sc.dma_start(out=x2_til[:, t1, :], in_=aw[:, sl, 1, :]), 16)
                            s_sst.inc(sc.dma_start(out=agxn_til[:, t1, :], in_=xnb[:, 0, 1, :]), 16)
                            st[f"axp_{b}_{gp}"] = s_sst.n

                        lgb = ps_kv[1][0]

                        @block.sync
                        def _(sync, gp=gp, b=b):
                            s_v.wait_at(sync, st[f"aply_{b}_{gp}"])
                            prev = st.get(f"a2mm_{b}_{gp-2}", st.get(f"a2mm_{b-1}_{gp+6}", 0))
                            s_t.wait_at(sync, prev)
                            s_tr.inc(sync.dma_start(out=xnf2[:, gp % 2, 0, :, :], in_=xnb[:, 0, 0, :], transpose=True), 16)
                            s_tr.inc(sync.dma_start(out=xnf2[:, gp % 2, 1, :, :], in_=xnb[:, 0, 1, :], transpose=True), 16)
                            st[f"a2trp_{b}_{gp}"] = s_tr.n

                        if gp >= 2:
                            emit_router_mm(b, gp - 2)

                    emit_router_mm(b, 6)
                    emit_router_mm(b, 7)
                    st["attn_prev_b"] = s_t.n

                for q in range(16):
                    t0, t1 = 2 * q, 2 * q + 1

                    @block.vector
                    def _(v, q=q, t0=t0, t1=t1):
                        if q == 0:
                            s_v.wait(v)
                        if q >= 1:
                            s_a.wait_at(v, st[f"p2exp_{q-1}"])
                            s_t.wait_at(v, st[f"c2mm_{q-1}"])
                        inst = None
                        for par, tt in ((0, t0), (1, t1)):
                            v.max(mx8[:, :], lga[:, q, par * 8:(par + 1) * 8])
                            v.drain()
                            v.max_index(ix8[:, :], mx8[:, :], lga[:, q, par * 8:(par + 1) * 8])
                            v.drain()
                            v.tensor_copy(idxf[:, tt:tt + 1], ix8[:, 0:1])
                            v.tensor_copy(idc[:, :], ix8[:, 0:1])
                            v.drain()
                            v.tensor_scalar(ohtm[:, par, :], iotaef[:, :], idc[:, :], None, ALU.is_equal)
                            inst = v.tensor_scalar(sh8[:, par, :], lga[:, q, par * 8:(par + 1) * 8],
                                                   mx8[:, 0:1], None, ALU.subtract)
                            v.drain()
                        s_v.inc(inst)
                        st[f"p2sh_{q}"] = s_v.n

                    @block.scalar
                    def _(sc, q=q, t0=t0, t1=t1):
                        s_v.wait_at(sc, st[f"p2sh_{q}"])
                        sc.activation(out=she[:, 0, :], in_=sh8[:, 0, :], func=AF.Exp,
                                      accum_out=esum[:, t0:t0 + 1])
                        s_a.inc(sc.activation(out=she[:, 1, :], in_=sh8[:, 1, :], func=AF.Exp,
                                              accum_out=esum[:, t1:t1 + 1]))
                        st[f"p2exp_{q}"] = s_a.n

                    @block.tensor
                    def _(t_, q=q):
                        s_v.wait_at(t_, st[f"p2sh_{q}"])
                        inst = None
                        for par in range(2):
                            inst = t_.matmul(ps_nm[0][0:E, 0:1], ohtm[:, par, :], ones128b[:, :],
                                             start=(q == 0 and par == 0),
                                             stop=(q == 15 and par == 1))
                        s_t.inc(inst)
                        st[f"c2mm_{q}"] = s_t.n

                @block.vector
                def _(v):
                    s_a.wait(v)
                    v.reciprocal(comb[:, :], esum[:, :])
                    v.drain()
                    s_v.inc(v.tensor_copy(idxw[:, :], idxf[:, :]))

                @block.vector
                def _(v):
                    s_t.wait(v)
                    s_v.inc(v.tensor_copy(cnt8[:, :], ps_nm[0][0:E, 0:1]))
                    st["cntev_1"] = s_v.n

                @block.gpsimd
                def _(g):
                    for k in range(8):
                        g.wait_ge(s_sst.h, st[f"axp_{k // 4}_{2 * (k % 4) + 1}"])
                        s_cc[5].inc(g.collective_compute(
                            "AllGather", ALU.bypass, replica_groups=[list(range(NCORE))],
                            ins=[ag_xn_in.ap()[512 * k:512 * (k + 1), :].opt()],
                            outs=[ag_xn_out.ap()[NCORE * 512 * k:NCORE * 512 * (k + 1), :].opt()]))
                    g.wait_ge(s_v.h, st["cntev_1"])
                    s_gd.inc(g.dma_start(out=cnt_dram.ap(), in_=cnt8[:, :]), 16)
                    s_gd.wait(g)
                    s_cc[4].inc(g.collective_compute(
                        "AllGather", ALU.bypass, replica_groups=[list(range(NCORE))],
                        ins=[cnt_dram.ap().opt()], outs=[cnt_ag.ap().opt()]))

        emit_routing(0)
        emit_expert(0)
        emit_return(0)
        emit_attention()

        @block.gpsimd
        def _(g):
            s_cc[2].wait_at(g, 1)
            s_gd.inc(g.dma_start(out=S_dram.ap().rearrange("(a b) one -> a (b one)", b=128),
                                 in_=S_init.ap().rearrange("(a b) one -> a (b one)", b=128)), 16)

        emit_routing(1)
        emit_expert(1)
        emit_return(1)

        @block.sync
        def _(sync):
            s_sst.wait(sync)
            s_gd.wait(sync)
            if DEBUG:
                s_ld.inc(sync.dma_start(out=dbg_x1[:, :], in_=x1_dram.ap()[:, :]), 16)
                s_ld.inc(sync.dma_start(out=dbg_x2[:, :], in_=x2_dram.ap()[:, :]), 16)
                s_ld.inc(sync.dma_start(out=dbg_xn2[:, :], in_=xn2_dram.ap()[:, :]), 16)
                s_ld.inc(sync.dma_start(out=dbg_xn3[:, :], in_=ag_xn_in.ap()[:, :]), 16)
                s_ld.inc(sync.dma_start(out=dbg_y[:, :], in_=y_dram.ap()[:, :]), 16)
                s_ld.inc(sync.dma_start(out=dbg_r[:, 0:NT], in_=rsel[:, :]), 16)
                s_ld.inc(sync.dma_start(out=dbg_r[:, NT:2 * NT], in_=wsc[:, :]), 16)
                s_ld.inc(sync.dma_start(out=dbg_r[:, 2 * NT:3 * NT], in_=gretf[:, :]), 16)
                s_ld.inc(sync.dma_start(out=dbg_r[:, 3 * NT:4 * NT], in_=comb[:, :]), 16)
                s_ld.inc(sync.dma_start(out=dbg_g[:, :], in_=g_u[:, :]), 16)
                s_ld.wait(sync)

    return nc


_CACHED = {}


def kernel(L0_attn_x, L0_attn_q_proj, L0_attn_k_proj, L0_ffn_wg, L0_ffn_W1,
           L0_ffn_b1, L0_ffn_W2, L0_ffn_b2):
    x = np.ascontiguousarray(np.asarray(L0_attn_x, dtype=np.float32))
    qk = np.ascontiguousarray(np.concatenate([np.asarray(L0_attn_q_proj, np.float32),
                                              np.asarray(L0_attn_k_proj, np.float32)], axis=1))
    wg = np.ascontiguousarray(np.asarray(L0_ffn_wg, np.float32))
    W1 = np.asarray(L0_ffn_W1, np.float32)
    b1 = np.asarray(L0_ffn_b1, np.float32)
    W2 = np.asarray(L0_ffn_W2, np.float32)
    b2 = np.asarray(L0_ffn_b2, np.float32)
    ident = np.eye(128, dtype=ml_dtypes.bfloat16)

    if "nc" not in _CACHED:
        _CACHED["nc"] = build()
    nc = _CACHED["nc"]

    in_maps = []
    for c in range(NCORE):
        pm = (np.arange(E) < c).astype(np.float32).reshape(E, 1)
        in_maps.append({
            "x": np.ascontiguousarray(x[2 * c:2 * c + 2].reshape(T4, D)),
            "qk": qk,
            "wg": wg,
            "w1": np.ascontiguousarray(W1[c]),
            "b1": np.ascontiguousarray(b1[c].reshape(32, 128).T),
            "w2": np.ascontiguousarray(W2[c]),
            "b2": np.ascontiguousarray(b2[c].reshape(1, D)),
            "premask": pm,
            "myc": np.full((128, 1), float(c), np.float32),
            "ident": ident,
        })
    res = run_bass_kernel_spmd(nc, in_maps, core_ids=list(range(NCORE)))
    out = np.concatenate([r["xo"].reshape(2, TSEQ, D) for r in res.results], axis=0)
    return out


# version 46
# speedup vs baseline: 1.0457x; 1.0328x over previous
import numpy as np
import ml_dtypes
import concourse.bass as bass
import concourse.mybir as mybir
from concourse.bass import IndirectOffsetOnAxis
from concourse.bass_utils import run_bass_kernel_spmd
from contextlib import ExitStack

F32 = mybir.dt.float32
BF16 = mybir.dt.bfloat16
U32 = mybir.dt.uint32
I32 = mybir.dt.int32
AF = mybir.ActivationFunctionType
ALU = mybir.AluOpType
AX = mybir.AxisListType

NCORE = 8
B = 2
TSEQ = 2048
T4 = B * TSEQ
NT = 32
D = 1024
E = 8
FF = 4096
PD = 256
CAP = 5121
NC41 = 41
RPAD = NC41 * 128
CHT = 384
NCH = 14
BIG = float(2 ** 30)
C23 = 12582912.0
EPS = 1e-5
DEBUG = False
ATT_BATCHES = None


class Sem:
    def __init__(self, ctx, nc, name):
        self.h = ctx.enter_context(nc.semaphore(name))
        self.n = 0

    def inc(self, inst, k=1):
        inst.then_inc(self.h, k)
        self.n += k

    def wait(self, eng):
        eng.wait_ge(self.h, self.n)

    def wait_at(self, eng, v):
        if v > 0:
            eng.wait_ge(self.h, v)


def build():
    nc = bass.Bass("TRN2", target_bir_lowering=False)
    x_in = nc.dram_tensor("x", [T4, D], F32, kind="ExternalInput").ap()
    qk_in = nc.dram_tensor("qk", [D, 2 * PD], F32, kind="ExternalInput").ap()
    wg_in = nc.dram_tensor("wg", [D, E], F32, kind="ExternalInput").ap()
    w1_in = nc.dram_tensor("w1", [D, FF], F32, kind="ExternalInput").ap()
    b1_in = nc.dram_tensor("b1", [128, 32], F32, kind="ExternalInput").ap()
    w2_in = nc.dram_tensor("w2", [FF, D], F32, kind="ExternalInput").ap()
    b2_in = nc.dram_tensor("b2", [1, D], F32, kind="ExternalInput").ap()
    premask_in = nc.dram_tensor("premask", [E, 1], F32, kind="ExternalInput").ap()
    myc_in = nc.dram_tensor("myc", [128, 1], F32, kind="ExternalInput").ap()
    ident_in = nc.dram_tensor("ident", [128, 128], BF16, kind="ExternalInput").ap()
    xo = nc.dram_tensor("xo", [T4, D], F32, kind="ExternalOutput").ap()
    if DEBUG:
        dbg_x1 = nc.dram_tensor("dbg_x1", [T4, D], F32, kind="ExternalOutput").ap()
        dbg_x2 = nc.dram_tensor("dbg_x2", [T4, D], F32, kind="ExternalOutput").ap()
        dbg_xn2 = nc.dram_tensor("dbg_xn2", [T4, D], BF16, kind="ExternalOutput").ap()
        dbg_xn3 = nc.dram_tensor("dbg_xn3", [T4, D], BF16, kind="ExternalOutput").ap()
        dbg_y = nc.dram_tensor("dbg_y", [RPAD, D], BF16, kind="ExternalOutput").ap()
        dbg_r = nc.dram_tensor("dbg_r", [128, 4 * NT], F32, kind="ExternalOutput").ap()
        dbg_g = nc.dram_tensor("dbg_g", [128, NC41], U32, kind="ExternalOutput").ap()
        dbg_z = nc.dram_tensor("dbg_z", [2, 128, 16], F32, kind="ExternalOutput").ap()
        dbg_kv = nc.dram_tensor("dbg_kv", [2, 128, 2048], BF16, kind="ExternalOutput").ap()
        dbg_qe = nc.dram_tensor("dbg_qe", [2, 128, 16 * PD], BF16, kind="ExternalOutput").ap()
        dbg_nm = nc.dram_tensor("dbg_nm", [2, 128, 1024], F32, kind="ExternalOutput").ap()
        dbg_t0 = nc.dram_tensor("dbg_t0", [3, 128, 1024], BF16, kind="ExternalOutput").ap()

    ag_xn_in = nc.dram_tensor("ag_xn_in", [T4, D], BF16)
    ag_xn_out = nc.dram_tensor("ag_xn_out", [NCORE * T4, D], BF16, addr_space="Shared")
    cnt_dram = nc.dram_tensor("cnt_dram", [1, E], F32)
    cnt_ag = nc.dram_tensor("cnt_ag", [NCORE, E], F32, addr_space="Shared")
    S_dram = nc.dram_tensor("S_dram", [E * RPAD, 1], I32)
    S_init = nc.dram_tensor("S_init", [E * RPAD, 1], I32)
    S_ag = nc.dram_tensor("S_ag", [NCORE * E * RPAD, 1], I32, addr_space="Shared")
    y_dA = nc.dram_tensor("y_dA", [3840, D], BF16)
    y_dB = nc.dram_tensor("y_dB", [1408, D], BF16)
    y_agA = nc.dram_tensor("y_agA", [NCORE * 3840, D], BF16, addr_space="Shared")
    y_agB = nc.dram_tensor("y_agB", [NCORE * 1408, D], BF16, addr_space="Shared")
    x1_dram = nc.dram_tensor("x1_dram", [T4, D], F32)
    x2_dram = nc.dram_tensor("x2_dram", [T4, D], F32)
    xn2_dram = nc.dram_tensor("xn2_dram", [T4, D], BF16)

    x_til = x_in.rearrange("(t p) d -> p t d", p=128)
    x1_til = x1_dram.ap().rearrange("(t p) d -> p t d", p=128)
    x2_til = x2_dram.ap().rearrange("(t p) d -> p t d", p=128)
    xo_til = xo.rearrange("(t p) d -> p t d", p=128)
    agxn_til = ag_xn_in.ap().rearrange("(t p) d -> p t d", p=128)
    xn2_til = xn2_dram.ap().rearrange("(t p) d -> p t d", p=128)
    y_tilA = y_dA.ap().rearrange("(c p) d -> p c d", p=128)
    y_tilB = y_dB.ap().rearrange("(c p) d -> p c d", p=128)
    S_rows = S_ag.ap().rearrange("(row c) one -> row (c one)", c=NC41)

    with ExitStack() as ctx:
        en = ctx.enter_context
        w1_sb = en(nc.sbuf_tensor("w1_sb", [128, 8, FF], BF16))
        w2_sb = en(nc.sbuf_tensor("w2_sb", [128, 32, D], BF16))
        wg_sb = en(nc.sbuf_tensor("wg_sb", [128, 8, E], BF16))
        b1_sb = en(nc.sbuf_tensor("b1_sb", [128, 32], F32))
        b2b = en(nc.sbuf_tensor("b2b", [128, D], F32))
        idn = en(nc.sbuf_tensor("idn", [128, 128], BF16))
        ones18 = en(nc.sbuf_tensor("ones18", [1, E], BF16))
        ones128b = en(nc.sbuf_tensor("ones128b", [128, 1], BF16))
        ohtm = en(nc.sbuf_tensor("ohtm", [128, 2, E], BF16))
        idc = en(nc.sbuf_tensor("idc", [128, 1], F32))
        ones8c = en(nc.sbuf_tensor("ones8c", [E, 1], F32))
        iota8c = en(nc.sbuf_tensor("iota8c", [E, 1], F32))
        iotaef = en(nc.sbuf_tensor("iotaef", [128, E], I32))
        locid = en(nc.sbuf_tensor("locid", [128, NT], I32))
        iota128 = en(nc.sbuf_tensor("iota128", [128, 1], I32))
        premask = en(nc.sbuf_tensor("premask_sb", [E, 1], F32))
        myc = en(nc.sbuf_tensor("myc_sb", [128, 1], F32))
        midxu = en(nc.sbuf_tensor("midxu", [128, E], U32))
        mbase = en(nc.sbuf_tensor("mbase", [128, 1], F32))
        xt = en(nc.sbuf_tensor("xt", [128, 2, D], F32))
        xnb = en(nc.sbuf_tensor("xnb", [128, 2, 2, D], BF16))
        xnf = en(nc.sbuf_tensor("xnf", [128, 2, 2, 8, 128], BF16, is_transpose_dest=True))
        bns = en(nc.sbuf_tensor("bns", [128, 2, 12], F32))
        mv = en(nc.sbuf_tensor("mv", [128, 2, 2], F32))
        rsd = en(nc.sbuf_tensor("rsd", [128, 2, 1], F32))
        nbi = en(nc.sbuf_tensor("nbi", [128, 2, 1], F32))
        g_u = en(nc.sbuf_tensor("g_u", [128, NC41], U32))
        gretu = en(nc.sbuf_tensor("gretu", [128, NT], U32))
        gretu2 = en(nc.sbuf_tensor("gretu2", [128, NT], U32))
        wsc = en(nc.sbuf_tensor("wsc", [128, NT], F32))
        cnt8 = en(nc.sbuf_tensor("cnt8", [E, 1], F32))
        cntall = en(nc.sbuf_tensor("cntall", [E, E], F32))
        base8 = en(nc.sbuf_tensor("base8", [E, 1], F32))
        mx8 = en(nc.sbuf_tensor("mx8", [128, 8], F32))
        ix8 = en(nc.sbuf_tensor("ix8", [128, 8], U32))
        idxf = en(nc.sbuf_tensor("idxf", [128, NT], BF16))
        idxw = en(nc.sbuf_tensor("idxw", [128, NT], F32))
        sh8 = en(nc.sbuf_tensor("sh8", [128, 2, E], F32))
        she = en(nc.sbuf_tensor("she", [128, 2, E], F32))
        esum = en(nc.sbuf_tensor("esum", [128, NT], F32))
        comb = en(nc.sbuf_tensor("comb", [128, NT], F32))
        rsel = en(nc.sbuf_tensor("rsel", [128, NT], F32))
        vld = en(nc.sbuf_tensor("vld", [128, NT], F32))
        scc = en(nc.sbuf_tensor("scc", [128, NT], F32))
        spp = en(nc.sbuf_tensor("spp", [128, NT], F32))
        offf = en(nc.sbuf_tensor("offf", [128, NT], F32))
        off2 = en(nc.sbuf_tensor("off2", [128, NT], F32))
        offu = en(nc.sbuf_tensor("offu", [128, NT], U32))
        gretf = en(nc.sbuf_tensor("gretf", [128, NT], F32))

        s_ld = Sem(ctx, nc, "s_ld")
        s_ld0 = Sem(ctx, nc, "s_ld0")
        s_wc = Sem(ctx, nc, "s_wc")
        s_tr = Sem(ctx, nc, "s_tr")
        s_str = Sem(ctx, nc, "s_str")
        s_sst = Sem(ctx, nc, "s_sst")
        s_gd = Sem(ctx, nc, "s_gd")
        s_gc = Sem(ctx, nc, "s_gc")
        s_v = Sem(ctx, nc, "s_v")
        s_a = Sem(ctx, nc, "s_a")
        s_t = Sem(ctx, nc, "s_t")
        s_fill = Sem(ctx, nc, "s_fill")
        s_m2 = Sem(ctx, nc, "s_m2")
        s_cc = [Sem(ctx, nc, f"s_cc{i}") for i in range(8)]
        s_ccy = [Sem(ctx, nc, f"s_ccy{i}") for i in range(6)]

        block = en(nc.Block())
        st = {"xnst": 0}

        c0 = ctx.enter_context(ExitStack())
        b2r = c0.enter_context(nc.sbuf_tensor("b2r", [1, D], F32))
        neg1 = c0.enter_context(nc.sbuf_tensor("neg1", [128, E * NC41], I32))
        ones1 = c0.enter_context(nc.sbuf_tensor("ones1", [1, 128], F32))

        @block.sync
        def _(sync):
            s_ld0.inc(sync.dma_start(out=b1_sb[:, :], in_=b1_in[:, :]), 16)
            s_ld0.inc(sync.dma_start(out=b2r[:, :], in_=b2_in[:, :]), 16)
            s_ld0.inc(sync.dma_start(out=idn[:, :], in_=ident_in[:, :]), 16)
            s_ld0.inc(sync.dma_start(out=premask[:, :], in_=premask_in[:, :]), 16)
            s_ld0.inc(sync.dma_start(out=myc[:, :], in_=myc_in[:, :]), 16)

        @block.gpsimd
        def _(g):
            g.iota(iotaef[:, :], pattern=[[1, E]], base=0, channel_multiplier=0)
            g.iota(iota8c[:, :], pattern=[[1, 1]], base=0, channel_multiplier=1,
                   allow_small_or_imprecise_dtypes=True)
            g.iota(locid[:, :], pattern=[[NCORE * 512, NT // 4], [128, 4]], base=0,
                   channel_multiplier=1)
            g.iota(iota128[:, :], pattern=[[1, 1]], base=0, channel_multiplier=1)
            g.memset(neg1[:, :], -1)
            g.memset(ones1[:, :], 1.0)
            g.memset(ones18[:, :], 1.0)
            g.memset(ones8c[:, :], 1.0)
            g.memset(ones128b[:, :], 1.0)
            st["rb_slot"] = g.to_reg(E * RPAD - 1)
            st["rb_srow"] = g.to_reg(NCORE * E * 128 - 1)
            st["rb_xn"] = g.to_reg(NCORE * T4 - 1)
            st["rb_yA"] = g.to_reg(NCORE * 3840 - 1)
            st["rb_yB"] = g.to_reg(NCORE * 1408 - 1)
            s_gc.inc(g.drain())
            s_wc.inc(g.dma_start(out=wg_sb[:, :, :], in_=wg_in.rearrange("(dc p) e -> p dc e", p=128)), 16)
            s_wc.inc(g.dma_start(out=w1_sb[:, :, :], in_=w1_in.rearrange("(dc p) f -> p dc f", p=128)), 16)
            s_wc.inc(g.dma_start(out=w2_sb[:, :, :], in_=w2_in.rearrange("(fc p) d -> p fc d", p=128)), 16)
            s_gd.inc(g.dma_start(out=S_dram.ap().rearrange("(e p c) one -> p e (c one)", e=E, p=128),
                                 in_=neg1[:, :].rearrange("p (e c) -> p e c", e=E)), 16)
            s_gd.inc(g.dma_start(out=S_init.ap().rearrange("(e p c) one -> p e (c one)", e=E, p=128),
                                 in_=neg1[:, :].rearrange("p (e c) -> p e c", e=E)), 16)

        @block.vector
        def _(v):
            s_ld0.wait(v)
            s_gc.wait_at(v, 1)
            v.scalar_tensor_tensor(mbase[:, :], myc[:, :], 128.0, iota128[:, :], ALU.mult, ALU.add)
            v.drain()
            for r in range(NCORE):
                v.tensor_scalar(offf[:, r:r + 1], mbase[:, :], float(r * 1024), None, ALU.add)
            v.drain()
            s_v.inc(v.tensor_copy(midxu[:, :], offf[:, 0:E]))

        ps_tmp = c0.enter_context(nc.psum_tensor("ps_b2", [128, 512], F32))
        for dh in range(2):
            @block.tensor
            def _(t_, dh=dh):
                if dh == 0:
                    s_ld0.wait(t_)
                    s_v.wait(t_)
                else:
                    s_v.wait(t_)
                s_t.inc(t_.matmul(ps_tmp[:, :], ones1[:, :], b2r[:, dh * 512:(dh + 1) * 512],
                                  start=True, stop=True))

            @block.vector
            def _(v, dh=dh):
                s_t.wait(v)
                s_v.inc(v.tensor_copy(b2b[:, dh * 512:(dh + 1) * 512], ps_tmp[:, :]))
        c0.close()

        def emit_routing(call):
            with ExitStack() as rc:
                er = rc.enter_context
                sfx = f"_r{call}"
                idxrow = er(nc.sbuf_tensor("idxrow" + sfx, [1, T4], BF16))
                ohT = er(nc.sbuf_tensor("ohT" + sfx, [E, T4], BF16))
                rT = er(nc.sbuf_tensor("rT" + sfx, [E, T4], F32))
                prodc = er(nc.sbuf_tensor("prodc" + sfx, [E, 2, 512], F32))
                Ssb = er(nc.sbuf_tensor("Ssb" + sfx, [128, E, NC41], I32))
                mrg_m = er(nc.sbuf_tensor("mrg_m" + sfx, [128, NC41], F32))
                mrg_t = er(nc.sbuf_tensor("mrg_t" + sfx, [128, NC41], F32))
                gacc = er(nc.sbuf_tensor("gacc" + sfx, [128, NC41], F32))
                ps_lg = er(nc.psum_tensor("ps_lg" + sfx, [128, 2, E], F32))
                ps_xnt = er(nc.psum_tensor("ps_xnt" + sfx, [128, 2, 8, 128], BF16))
                ps_bc = er(nc.psum_tensor("ps_bc" + sfx, [E, 2, 512], F32))
                ps_b8 = er(nc.psum_tensor("ps_b8" + sfx, [E, 1], F32))
                ps_cnt = er(nc.psum_tensor("ps_cnt" + sfx, [E, 1], F32))
                ps_rs = er(nc.psum_tensor("ps_rs" + sfx, [128, NT], F32))

                xnst0 = s_sst.n
                cc_cnt = s_cc[call * 4 + 0]
                cc_xn = s_cc[call * 4 + 1]
                if call == 0:
                    @block.gpsimd
                    def _(g):
                        for k in range(8):
                            g.wait_ge(s_sst.h, xnst0 + 32 * (2 * k + 2))
                            cc_xn.inc(g.collective_compute(
                                "AllGather", ALU.bypass, replica_groups=[list(range(NCORE))],
                                ins=[ag_xn_in.ap()[512 * k:512 * (k + 1), :].opt()],
                                outs=[ag_xn_out.ap()[NCORE * 512 * k:NCORE * 512 * (k + 1), :].opt()]))

                    for gp in range(NT // 2):
                        t0, t1 = 2 * gp, 2 * gp + 1

                        @block.sync
                        def _(sync, gp=gp, t0=t0, t1=t1, call=call):
                            if gp == 0:
                                if call == 1:
                                    s_t.wait(sync)
                                    s_v.wait(sync)
                                    s_a.wait(sync)
                                    s_sst.wait(sync)
                            if call == 0:
                                if gp >= 1:
                                    s_v.wait_at(sync, st[f"lnap_{call}_{gp-1}"])
                                s_ld.inc(sync.dma_start(out=xt[:, 0, :], in_=x_til[:, t0, :]), 16)
                                s_ld.inc(sync.dma_start(out=xt[:, 1, :], in_=x_til[:, t1, :]), 16)
                            else:
                                if gp >= 1:
                                    s_tr.wait_at(sync, st[f"trpp_{call}_{gp-1}"])
                                s_ld.inc(sync.dma_start(out=xnb[:, 0, 0, :], in_=agxn_til[:, t0, :]), 16)
                                s_ld.inc(sync.dma_start(out=xnb[:, 0, 1, :], in_=agxn_til[:, t1, :]), 16)

                        if call == 0:
                            @block.vector
                            def _(v, gp=gp, t0=t0, t1=t1):
                                s_ld.wait(v)
                                if gp >= 1:
                                    s_t.wait_at(v, st[f"xtm_0_{gp-1}"])
                                    s_sst.wait_at(v, xnst0 + 32 * gp)
                                v.bn_stats(bns[:, 0, 0:6], xt[:, 0, 0:512])
                                v.bn_stats(bns[:, 0, 6:12], xt[:, 0, 512:1024])
                                v.bn_stats(bns[:, 1, 0:6], xt[:, 1, 0:512])
                                v.bn_stats(bns[:, 1, 6:12], xt[:, 1, 512:1024])
                                v.drain()
                                v.bn_aggr(mv[:, 0, :], bns[:, 0, :])
                                v.bn_aggr(mv[:, 1, :], bns[:, 1, :])
                                v.drain()
                                s_v.inc(v.tensor_scalar(rsd[:, :, :], mv[:, :, 1:2], EPS, None, ALU.add))

                            @block.scalar
                            def _(sc, gp=gp):
                                s_v.wait(sc)
                                sc.activation(out=rsd[:, :, :], in_=rsd[:, :, :], func=AF.Ln)
                                sc.drain()
                                s_a.inc(sc.activation(out=rsd[:, :, :], in_=rsd[:, :, :], func=AF.Exp, scale=-0.5))
                                st[f"rsq_0_{gp}"] = s_a.n

                            @block.vector
                            def _(v, gp=gp):
                                s_a.wait_at(v, st[f"rsq_0_{gp}"])
                                v.tensor_scalar(xnb[:, 0, 0, :], xt[:, 0, :], mv[:, 0, 0:1],
                                                rsd[:, 0, :], ALU.subtract, ALU.mult)
                                s_v.inc(v.tensor_scalar(xnb[:, 0, 1, :], xt[:, 1, :], mv[:, 1, 0:1],
                                                        rsd[:, 1, :], ALU.subtract, ALU.mult))
                                st[f"lnap_0_{gp}"] = s_v.n

                            @block.scalar
                            def _(sc, gp=gp, t0=t0, t1=t1):
                                s_v.wait(sc)
                                s_sst.inc(sc.dma_start(out=agxn_til[:, t0, :], in_=xnb[:, 0, 0, :]), 16)
                                s_sst.inc(sc.dma_start(out=agxn_til[:, t1, :], in_=xnb[:, 0, 1, :]), 16)

                        @block.tensor
                        def _(t_, gp=gp, call=call):
                            s_v.wait(t_)
                            if gp >= 1:
                                s_a.wait_at(t_, st[f"xev_{call}_{gp-1}"])
                            inst = None
                            for par in range(2):
                                for dc in range(8):
                                    inst = t_.transpose(ps_xnt[:, par, dc, :],
                                                        xnb[:, 0, par, dc * 128:(dc + 1) * 128],
                                                        idn[:, :])
                            s_t.inc(inst)
                            st[f"xtm_{call}_{gp}"] = s_t.n

                        @block.scalar
                        def _(sc, gp=gp, call=call):
                            s_t.wait_at(sc, st[f"xtm_{call}_{gp}"])
                            if gp >= 1:
                                s_t.wait_at(sc, st[f"rtp_{call}_{gp-1}"])
                            sc.activation(out=xnf[:, 0, 0, :, :], in_=ps_xnt[:, 0, :, :], func=AF.Copy)
                            s_a.inc(sc.activation(out=xnf[:, 0, 1, :, :], in_=ps_xnt[:, 1, :, :], func=AF.Copy))
                            st[f"xev_{call}_{gp}"] = s_a.n

                        @block.tensor
                        def _(t_, gp=gp, call=call):
                            s_a.wait_at(t_, st[f"xev_{call}_{gp}"])
                            if gp == 0:
                                s_wc.wait_at(t_, 16)
                            if gp >= 1:
                                s_v.wait_at(t_, st[f"shp_{call}_{gp-1}"])
                            inst = None
                            for par in range(2):
                                for dc in range(8):
                                    inst = t_.matmul(ps_lg[:, par, :], xnf[:, 0, par, dc, :], wg_sb[:, dc, :],
                                                     start=(dc == 0), stop=(dc == 7))
                            s_t.inc(inst)
                            st[f"rtp_{call}_{gp}"] = s_t.n

                        @block.vector
                        def _(v, gp=gp, t0=t0, t1=t1, call=call):
                            s_t.wait(v)
                            if gp >= 1:
                                s_a.wait_at(v, st[f"expp_{call}_{gp-1}"])
                                s_t.wait_at(v, st[f"cntp_{call}_{gp-1}"])
                            for par, tt in ((0, t0), (1, t1)):
                                v.max(mx8[:, :], ps_lg[:, par, :])
                                v.drain()
                                v.max_index(ix8[:, :], mx8[:, :], ps_lg[:, par, :])
                                v.drain()
                                v.tensor_copy(idxf[:, tt:tt + 1], ix8[:, 0:1])
                                v.tensor_copy(idc[:, :], ix8[:, 0:1])
                                v.drain()
                                v.tensor_scalar(ohtm[:, par, :], iotaef[:, :], idc[:, :], None, ALU.is_equal)
                                inst = v.tensor_scalar(sh8[:, par, :], ps_lg[:, par, :], mx8[:, 0:1], None, ALU.subtract)
                                v.drain()
                            s_v.inc(inst)
                            st[f"shp_{call}_{gp}"] = s_v.n

                        @block.scalar
                        def _(sc, gp=gp, t0=t0, t1=t1, call=call):
                            s_v.wait(sc)
                            sc.activation(out=she[:, 0, :], in_=sh8[:, 0, :], func=AF.Exp,
                                          accum_out=esum[:, t0:t0 + 1])
                            s_a.inc(sc.activation(out=she[:, 1, :], in_=sh8[:, 1, :], func=AF.Exp,
                                                  accum_out=esum[:, t1:t1 + 1]))
                            st[f"expp_{call}_{gp}"] = s_a.n

                        @block.tensor
                        def _(t_, gp=gp, call=call):
                            s_v.wait(t_)
                            inst = None
                            for par in range(2):
                                inst = t_.matmul(ps_cnt[:, :], ohtm[:, par, :], ones128b[:, :],
                                                 start=(gp == 0 and par == 0),
                                                 stop=(gp == NT // 2 - 1 and par == 1))
                            s_t.inc(inst)
                            st[f"cntp_{call}_{gp}"] = s_t.n
                            if gp == NT // 2 - 1:
                                st[f"cntmm_{call}"] = s_t.n

                    @block.vector
                    def _(v, call=call):
                        s_t.wait(v)
                        s_v.inc(v.tensor_copy(cnt8[:, :], ps_cnt[:, :]))
                        st[f"cntev_{call}"] = s_v.n
                        s_a.wait(v)
                        v.reciprocal(comb[:, :], esum[:, :])
                        v.drain()
                        s_v.inc(v.tensor_copy(idxw[:, :], idxf[:, :]))

                    @block.gpsimd
                    def _(g, call=call):
                        g.wait_ge(s_v.h, st[f"cntev_{call}"])
                        s_gd.inc(g.dma_start(out=cnt_dram.ap(), in_=cnt8[:, :]), 16)
                        s_gd.wait(g)
                        cc_cnt.inc(g.collective_compute(
                            "AllGather", ALU.bypass, replica_groups=[list(range(NCORE))],
                            ins=[cnt_dram.ap().opt()], outs=[cnt_ag.ap().opt()]))

                    for q in range(8):
                        @block.tensor
                        def _(t_, q=q):
                            s_v.wait(t_)
                            inst = None
                            for j in range(4):
                                tl = q * 4 + j
                                inst = t_.transpose(ps_xnt[0:1, q % 2, j, :],
                                                    idxf[:, tl:tl + 1], idn[:, :])
                            s_t.inc(inst)

                        @block.vector
                        def _(v, q=q):
                            s_t.wait(v)
                            if q == 0:
                                s_gd.wait(v)
                            s_v.inc(v.tensor_copy(idxrow[:, q * 512:(q + 1) * 512],
                                                  ps_xnt[0:1, q % 2, 0:4, :].rearrange("p a b -> p (a b)")))

                    for q in range(8):
                        @block.tensor
                        def _(t_, q=q):
                            s_v.wait(t_)
                            s_t.inc(t_.matmul(ps_bc[:, q % 2, :], ones18[:, :], idxrow[:, q * 512:(q + 1) * 512],
                                              start=True, stop=True))

                        @block.vector
                        def _(v, q=q):
                            s_t.wait(v)
                            s_v.inc(v.tensor_scalar(ohT[:, q * 512:(q + 1) * 512], ps_bc[:, q % 2, :],
                                                    iota8c[:, :], None, ALU.is_equal))


                @block.sync
                def _(sync):
                    cc_cnt.wait_at(sync, 1)
                    s_ld.inc(sync.dma_start(out=cntall[:, :], in_=cnt_ag.ap()[:, :]), 16)

                @block.tensor
                def _(t_):
                    s_ld.wait(t_)
                    s_t.inc(t_.matmul(ps_b8[:, :], cntall[:, :], premask[:, :], start=True, stop=True))

                @block.vector
                def _(v):
                    s_t.wait(v)
                    v.tensor_copy(base8[:, :], ps_b8[:, :])
                    v.drain()
                    s_v.inc(v.tensor_tensor_scan(rT[:, :], ohT[:, :], ohT[:, :], base8[:, :],
                                                 ALU.add, ALU.max))

                for q in range(8):
                    @block.vector
                    def _(v, q=q):
                        if q >= 2:
                            s_t.wait_at(v, st[f"rsmm_{call}_{q-2}"])
                        v.drain()
                        s_v.inc(v.tensor_tensor(prodc[:, q % 2, :], ohT[:, q * 512:(q + 1) * 512],
                                                rT[:, q * 512:(q + 1) * 512], ALU.mult))

                    @block.tensor
                    def _(t_, q=q, call=call):
                        s_v.wait(t_)
                        inst = None
                        for j in range(4):
                            tl = q * 4 + j
                            inst = t_.matmul(ps_rs[:, tl:tl + 1], prodc[:, q % 2, j * 128:(j + 1) * 128],
                                             ones8c[:, :], start=True, stop=True)
                        s_t.inc(inst)
                        st[f"rsmm_{call}_{q}"] = s_t.n

                @block.vector
                def _(v):
                    s_t.wait(v)
                    v.tensor_copy(rsel[:, :], ps_rs[:, :])
                    v.drain()
                    v.tensor_scalar(vld[:, :], rsel[:, :], float(CAP) + 0.5, None, ALU.is_le)
                    v.drain()
                    v.tensor_tensor(wsc[:, :], comb[:, :], vld[:, :], ALU.mult)
                    v.drain()
                    v.tensor_scalar(wsc[:, :], wsc[:, :], 0.5, None, ALU.mult)
                    v.tensor_scalar(off2[:, :], rsel[:, :], -1.0, 5120.0, ALU.add, ALU.min)
                    v.drain()
                    v.tensor_scalar(offf[:, :], off2[:, :], 3839.5, None, ALU.is_ge)
                    v.tensor_scalar(scc[:, :], off2[:, :], 4607.5, None, ALU.is_ge)
                    v.scalar_tensor_tensor(gretf[:, :], idxw[:, :], 3840.0, off2[:, :], ALU.mult, ALU.add)
                    v.tensor_scalar(spp[:, :], idxw[:, :], -128.0, 5376.0, ALU.mult, ALU.add)
                    v.drain()
                    v.scalar_tensor_tensor(gretf[:, :], offf[:, :], BIG, gretf[:, :], ALU.mult, ALU.add)
                    v.scalar_tensor_tensor(off2[:, :], idxw[:, :], 768.0, off2[:, :], ALU.mult, ALU.add)
                    v.tensor_tensor(scc[:, :], scc[:, :], spp[:, :], ALU.mult)
                    v.drain()
                    s_v.inc(v.tensor_copy(gretu[:, :], gretf[:, :]))
                    v.tensor_scalar(off2[:, :], off2[:, :], -3840.0, None, ALU.add)
                    v.tensor_scalar(offf[:, :], offf[:, :], -1.0, 1.0, ALU.mult, ALU.add)
                    v.drain()
                    v.tensor_tensor(off2[:, :], off2[:, :], scc[:, :], ALU.add)
                    v.drain()
                    v.scalar_tensor_tensor(off2[:, :], offf[:, :], BIG, off2[:, :], ALU.mult, ALU.add)
                    v.drain()
                    s_v.inc(v.tensor_copy(gretu2[:, :], off2[:, :]))
                    v.tensor_scalar(spp[:, :], rsel[:, :], -1.0, None, ALU.add)
                    v.drain()
                    v.tensor_scalar(scc[:, :], spp[:, :], 1.0 / 128.0, C23, ALU.mult, ALU.add)
                    v.drain()
                    v.tensor_scalar(scc[:, :], scc[:, :], -C23, None, ALU.add)
                    v.drain()
                    v.scalar_tensor_tensor(spp[:, :], scc[:, :], -128.0, spp[:, :], ALU.mult, ALU.add)
                    v.drain()
                    v.tensor_scalar(offf[:, :], spp[:, :], 0.0, None, ALU.is_lt)
                    v.drain()
                    v.tensor_tensor(scc[:, :], scc[:, :], offf[:, :], ALU.subtract)
                    v.scalar_tensor_tensor(spp[:, :], offf[:, :], 128.0, spp[:, :], ALU.mult, ALU.add)
                    v.drain()
                    v.scalar_tensor_tensor(off2[:, :], spp[:, :], 41.0, scc[:, :], ALU.mult, ALU.add)
                    v.drain()
                    v.scalar_tensor_tensor(offf[:, :], idxw[:, :], float(RPAD), off2[:, :], ALU.mult, ALU.add)
                    v.drain()
                    v.tensor_scalar(off2[:, :], vld[:, :], -BIG, BIG, ALU.mult, ALU.add)
                    v.drain()
                    v.tensor_tensor(offf[:, :], offf[:, :], off2[:, :], ALU.add)
                    v.drain()
                    s_v.inc(v.tensor_copy(offu[:, :], offf[:, :]))

                cc_S = s_cc[call * 4 + 2]

                @block.gpsimd
                def _(g, call=call):
                    s_v.wait(g)
                    s_gd.wait(g)
                    for t in range(NT):
                        s_gd.inc(g.indirect_dma_start(
                            out=S_dram.ap(),
                            out_offset=IndirectOffsetOnAxis(offu[:, t:t + 1], 0),
                            in_=locid[:, t:t + 1], in_offset=None,
                            bounds_check=st["rb_slot"], oob_is_err=False,
                        ), 16)
                    g.wait_ge(s_gd.h, s_gd.n)
                    cc_S.inc(g.collective_compute(
                        "AllGather", ALU.bypass, replica_groups=[list(range(NCORE))],
                        ins=[S_dram.ap().opt()], outs=[S_ag.ap().opt()]))
                    cc_S.wait_at(g, 1)
                    for r in range(NCORE):
                        s_gd.inc(g.indirect_dma_start(
                            out=Ssb[:, r, :], out_offset=None,
                            in_=S_rows, in_offset=IndirectOffsetOnAxis(midxu[:, r:r + 1], 0),
                            bounds_check=st["rb_srow"], oob_is_err=False,
                        ), 16)

                @block.vector
                def _(v):
                    s_gd.wait(v)
                    for r in range(NCORE):
                        v.tensor_scalar(mrg_m[:, :], Ssb[:, r, :], -0.5, None, ALU.is_ge)
                        v.tensor_scalar(mrg_t[:, :], Ssb[:, r, :], float(r * 512), None, ALU.add)
                        v.drain()
                        if r == 0:
                            v.tensor_tensor(gacc[:, :], mrg_t[:, :], mrg_m[:, :], ALU.mult)
                        else:
                            v.tensor_tensor(mrg_t[:, :], mrg_t[:, :], mrg_m[:, :], ALU.mult)
                            v.drain()
                            v.tensor_tensor(gacc[:, :], gacc[:, :], mrg_t[:, :], ALU.add)
                        v.drain()
                    s_v.inc(v.tensor_copy(g_u[:, :], gacc[:, :]))
                    st[f"routing_done_{call}"] = s_v.n

        def emit_expert(call):
            with ExitStack() as fc:
                ef = fc.enter_context
                sfx = f"_f{call}"
                xg = ef(nc.sbuf_tensor("xg" + sfx, [128, 2, 3, D], BF16))
                xgf = ef(nc.sbuf_tensor("xgf" + sfx, [128, 2, 3, 8, 128], BF16, is_transpose_dest=True))
                h_sb = ef(nc.sbuf_tensor("h_sb" + sfx, [128, 16, CHT], BF16))
                y_sb = ef(nc.sbuf_tensor("y_sb" + sfx, [128, 3, D], BF16))
                ps_ha = [ef(nc.psum_tensor(f"ps_h{i}" + sfx, [128, CHT], F32)) for i in range(2)]
                ps_y = [[ef(nc.psum_tensor(f"ps_y{stt}{dh}" + sfx, [128, 512], F32)) for dh in range(2)]
                        for stt in range(3)]

                fill0 = s_fill.n
                tr0 = s_tr.n
                m20 = s_m2.n
                ev0 = s_v.n
                st0 = s_sst.n
                st[f"yst_{call}_base"] = st0
                routing_done = st[f"routing_done_{call}"]
                cc_xn = s_cc[call * 4 + 1]
                cc_y = s_cc[call * 4 + 3]

                ntile = NC41
                chunks = [(ch, min(3, ntile - ch * 3)) for ch in range(NCH)]

                for ch, nk in chunks:
                    par = ch % 2

                    @block.gpsimd
                    def _(g, ch=ch, nk=nk, par=par):
                        if ch == 0:
                            s_v.wait_at(g, routing_done)
                            cc_xn.wait_at(g, 8)
                        if ch >= 2:
                            s_tr.wait_at(g, st[f"xtr_{call}_{ch-2}"])
                        for k in range(nk):
                            s_gd.inc(g.indirect_dma_start(
                                out=xg[:, par, k, :], out_offset=None,
                                in_=ag_xn_out.ap(), in_offset=IndirectOffsetOnAxis(
                                    g_u[:, ch * 3 + k:ch * 3 + k + 1], 0),
                                bounds_check=st["rb_xn"], oob_is_err=False,
                            ), 16)

                    @block.sync
                    def _(sync, ch=ch, nk=nk, par=par):
                        s_gd.wait(sync)
                        if ch >= 2:
                            s_fill.wait_at(sync, fill0 + 32 * (ch - 1))
                        inst = None
                        for k in range(nk):
                            s_tr.inc(sync.dma_start(out=xgf[:, par, k, :, :], in_=xg[:, par, k, :],
                                                    transpose=True), 16)
                        st[f"xtr_{call}_{ch}"] = s_tr.n

                    if ch == 11:
                        @block.gpsimd
                        def _(g, call=call):
                            g.wait_ge(s_sst.h, st[f"yst_{call}_base"] + 10 * 16)
                            s_ccy[call * 3].inc(g.collective_compute(
                                "AllGather", ALU.bypass, replica_groups=[list(range(NCORE))],
                                ins=[y_dA.ap().opt()], outs=[y_agA.ap().opt()]))
                    if ch == 13:
                        @block.gpsimd
                        def _(g, call=call):
                            g.wait_ge(s_sst.h, st[f"yst_{call}_base"] + 12 * 16)
                            s_ccy[call * 3 + 1].inc(g.collective_compute(
                                "AllGather", ALU.bypass, replica_groups=[list(range(NCORE))],
                                ins=[y_dB.ap()[0:768, :].opt()],
                                outs=[y_agB.ap()[0:NCORE * 768, :].opt()]))

                    for fh in range(2):
                        fillv = []
                        abase = s_a.n

                        for ft in range(16):
                            fg = fh * 16 + ft

                            @block.tensor
                            def _(t_, ch=ch, nk=nk, par=par, fh=fh, ft=ft, fg=fg, abase=abase):
                                if fh == 0 and ft == 0:
                                    s_tr.wait(t_)
                                    if ch == 0:
                                        s_wc.wait(t_)
                                if ft >= 2:
                                    s_a.wait_at(t_, abase + ft - 1)
                                elif s_a.n > 0:
                                    s_a.wait_at(t_, s_a.n)
                                inst = None
                                for dc in range(8):
                                    inst = t_.matmul(ps_ha[ft % 2][:, 0:nk * 128],
                                                     w1_sb[:, dc, fg * 128:(fg + 1) * 128],
                                                     xgf[:, par, 0:nk, dc, :],
                                                     start=(dc == 0), stop=(dc == 7))
                                s_fill.inc(inst)
                                fillv.append(s_fill.n)

                        @block.scalar
                        def _(sc, ch=ch, nk=nk, fh=fh, fillv=fillv):
                            for ft in range(16):
                                fg = fh * 16 + ft
                                s_fill.wait_at(sc, fillv[ft])
                                s_a.inc(sc.activation(out=h_sb[:, ft, 0:nk * 128], in_=ps_ha[ft % 2][:, 0:nk * 128],
                                                      func=AF.Gelu, bias=b1_sb[:, fg:fg + 1], scale=1.0))

                        @block.tensor
                        def _(t_, ch=ch, nk=nk, fh=fh):
                            s_a.wait(t_)
                            if fh == 0:
                                s_v.wait_at(t_, ev0 + ch * 6)
                            for stt in range(3 if nk == 3 else nk):
                                for dh in range(2):
                                    inst = None
                                    for fc in range(16):
                                        inst = t_.matmul(ps_y[stt][dh][:, :],
                                                         h_sb[:, fc, stt * 128:(stt + 1) * 128],
                                                         w2_sb[:, fh * 16 + fc, dh * 512:(dh + 1) * 512],
                                                         start=(fh == 0 and fc == 0),
                                                         stop=(fh == 1 and fc == 15))
                                    if fh == 1:
                                        s_m2.inc(inst)

                    @block.vector
                    def _(v, ch=ch, nk=nk, par=par):
                        s_m2.wait(v)
                        s_sst.wait_at(v, st0 + ch * 16)
                        inst = None
                        for stt in range(nk):
                            for dh in range(2):
                                inst = v.tensor_tensor(y_sb[:, stt, dh * 512:(dh + 1) * 512],
                                                       ps_y[stt][dh][:, :],
                                                       b2b[:, dh * 512:(dh + 1) * 512], ALU.add)
                        s_v.inc(inst, 6)

                    @block.scalar
                    def _(sc, ch=ch, nk=nk):
                        s_v.wait(sc)
                        if ch < 10:
                            s_sst.inc(sc.dma_start(out=y_tilA[:, ch * 3:ch * 3 + nk, :],
                                                   in_=y_sb[:, 0:nk, :]), 16)
                        else:
                            s_sst.inc(sc.dma_start(out=y_tilB[:, (ch - 10) * 3:(ch - 10) * 3 + nk, :],
                                                   in_=y_sb[:, 0:nk, :]), 16)



                @block.gpsimd
                def _(g):
                    s_sst.wait(g)
                    s_ccy[call * 3 + 1].inc(g.collective_compute(
                        "AllGather", ALU.bypass, replica_groups=[list(range(NCORE))],
                        ins=[y_dB.ap()[768:1408, :].opt()],
                        outs=[y_agB.ap()[NCORE * 768:NCORE * 1408, :].opt()]))

        def emit_return(call):
            rc = ctx.enter_context(ExitStack())
            ygat = rc.enter_context(nc.sbuf_tensor(f"ygat{call}", [128, 2, 2, D], BF16))
            xtr = rc.enter_context(nc.sbuf_tensor(f"xtr{call}", [128, 2, 2, D], F32))
            xnwr = rc.enter_context(nc.sbuf_tensor(f"xnwr{call}", [128, 2, 2, D], F32))
            if call == 0:
                xnbr = rc.enter_context(nc.sbuf_tensor(f"xnbr{call}", [128, 2, 2, D], BF16))
            for gp in range(NT // 2):
                t0, t1 = 2 * gp, 2 * gp + 1

                @block.gpsimd
                def _(g, gp=gp, t0=t0, t1=t1, call=call):
                    if gp == 0:
                        s_ccy[call * 3].wait_at(g, 1)
                        s_ccy[call * 3 + 1].wait_at(g, 2)
                    if gp >= 2:
                        s_v.wait_at(g, st[f"resp_{call}_{gp-2}"])
                    sl = gp % 2
                    for par, tt in ((0, t0), (1, t1)):
                        s_gd.inc(g.indirect_dma_start(
                            out=ygat[:, sl, par, :], out_offset=None,
                            in_=y_agA.ap(), in_offset=IndirectOffsetOnAxis(gretu[:, tt:tt + 1], 0),
                            bounds_check=st["rb_yA"], oob_is_err=False,
                        ), 16)
                        s_gd.inc(g.indirect_dma_start(
                            out=ygat[:, sl, par, :], out_offset=None,
                            in_=y_agB.ap(), in_offset=IndirectOffsetOnAxis(gretu2[:, tt:tt + 1], 0),
                            bounds_check=st["rb_yB"], oob_is_err=False,
                        ), 16)

                @block.sync
                def _(sync, gp=gp, t0=t0, t1=t1, call=call):
                    if gp < 2:
                        s_ccy[call * 3 + 1].wait_at(sync, 2)
                    else:
                        s_v.wait_at(sync, st[f"resp_{call}_{gp-2}"])
                    sl = gp % 2
                    src = x_til if call == 0 else x2_til
                    s_ld.inc(sync.dma_start(out=xtr[:, sl, 0, :], in_=src[:, t0, :]), 16)
                    s_ld.inc(sync.dma_start(out=xtr[:, sl, 1, :], in_=src[:, t1, :]), 16)

                @block.vector
                def _(v, gp=gp, t0=t0, t1=t1, call=call):
                    s_gd.wait(v)
                    s_ld.wait(v)
                    if gp >= 2:
                        s_sst.wait_at(v, st[f"xstp_{call}_{gp-2}"])
                    sl = gp % 2
                    v.scalar_tensor_tensor(xnwr[:, sl, 0, :], ygat[:, sl, 0, :], wsc[:, t0:t0 + 1],
                                           xtr[:, sl, 0, :], ALU.mult, ALU.add)
                    inst = v.scalar_tensor_tensor(xnwr[:, sl, 1, :], ygat[:, sl, 1, :], wsc[:, t1:t1 + 1],
                                                  xtr[:, sl, 1, :], ALU.mult, ALU.add)
                    s_v.inc(inst)
                    st[f"resp_{call}_{gp}"] = s_v.n
                    if call == 0:
                        v.drain()
                        v.bn_stats(bns[:, 0, 0:6], xnwr[:, sl, 0, 0:512])
                        v.bn_stats(bns[:, 0, 6:12], xnwr[:, sl, 0, 512:1024])
                        v.bn_stats(bns[:, 1, 0:6], xnwr[:, sl, 1, 0:512])
                        v.bn_stats(bns[:, 1, 6:12], xnwr[:, sl, 1, 512:1024])
                        v.drain()
                        v.bn_aggr(mv[:, 0, :], bns[:, 0, :])
                        v.bn_aggr(mv[:, 1, :], bns[:, 1, :])
                        v.drain()
                        s_v.inc(v.tensor_scalar(rsd[:, :, :], mv[:, :, 1:2], EPS, None, ALU.add))

                if call == 0:
                    @block.scalar
                    def _(sc, gp=gp):
                        s_v.wait(sc)
                        sc.activation(out=rsd[:, :, :], in_=rsd[:, :, :], func=AF.Ln)
                        sc.drain()
                        s_a.inc(sc.activation(out=rsd[:, :, :], in_=rsd[:, :, :], func=AF.Exp, scale=-0.5))

                    @block.vector
                    def _(v, gp=gp):
                        s_a.wait(v)
                        if gp >= 2:
                            s_sst.wait_at(v, st[f"xn2p_{gp-2}"])
                        sl = gp % 2
                        v.tensor_scalar(xnbr[:, sl, 0, :], xnwr[:, sl, 0, :], mv[:, 0, 0:1],
                                        rsd[:, 0, :], ALU.subtract, ALU.mult)
                        s_v.inc(v.tensor_scalar(xnbr[:, sl, 1, :], xnwr[:, sl, 1, :], mv[:, 1, 0:1],
                                                rsd[:, 1, :], ALU.subtract, ALU.mult))

                    @block.scalar
                    def _(sc, gp=gp, t0=t0, t1=t1):
                        s_v.wait(sc)
                        sl = gp % 2
                        s_sst.inc(sc.dma_start(out=x1_til[:, t0, :], in_=xnwr[:, sl, 0, :]), 16)
                        s_sst.inc(sc.dma_start(out=x1_til[:, t1, :], in_=xnwr[:, sl, 1, :]), 16)
                        st[f"xstp_0_{gp}"] = s_sst.n
                        s_sst.inc(sc.dma_start(out=xn2_til[:, t0, :], in_=xnbr[:, sl, 0, :]), 16)
                        s_sst.inc(sc.dma_start(out=xn2_til[:, t1, :], in_=xnbr[:, sl, 1, :]), 16)
                        st[f"xn2p_{gp}"] = s_sst.n
                else:
                    @block.scalar
                    def _(sc, gp=gp, t0=t0, t1=t1):
                        s_v.wait(sc)
                        sl = gp % 2
                        s_sst.inc(sc.dma_start(out=xo_til[:, t0, :], in_=xnwr[:, sl, 0, :]), 16)
                        s_sst.inc(sc.dma_start(out=xo_til[:, t1, :], in_=xnwr[:, sl, 1, :]), 16)
                        st[f"xstp_1_{gp}"] = s_sst.n
            rc.close()

        def emit_attention():
            with ExitStack() as ac:
                ea = ac.enter_context
                qkp_sb = ea(nc.sbuf_tensor("qkp_sb", [128, 8, 2 * PD], BF16))
                xnf2 = ea(nc.sbuf_tensor("xnf2", [128, 2, 2, 8, 128], BF16, is_transpose_dest=True))
                lga = ea(nc.sbuf_tensor("lga", [128, 16, 16], F32))
                aw = ea(nc.sbuf_tensor("aw", [128, 2, 2, D], F32))

                def emit_router_mm(b, g):
                    lgb = ps_kv[1][0]

                    @block.tensor
                    def _(t_, g=g, b=b):
                        s_tr.wait_at(t_, st[f"a2trp_{b}_{g}"])
                        if g == 0:
                            s_v.wait_at(t_, st[f"kvz_{b}"])
                        prev = st.get(f"a2sh_{b}_{g-1}", st.get(f"a2sh_{b-1}_7", 0))
                        s_v.wait_at(t_, prev)
                        inst = None
                        for par in range(2):
                            for dc in range(8):
                                inst = t_.matmul(lgb[:, par * 8:(par + 1) * 8],
                                                 xnf2[:, g % 2, par, dc, :], wg_sb[:, dc, :],
                                                 start=(dc == 0), stop=(dc == 7))
                        s_t.inc(inst)
                        st[f"a2mm_{b}_{g}"] = s_t.n

                    @block.vector
                    def _(v, g=g, b=b):
                        s_t.wait_at(v, st[f"a2mm_{b}_{g}"])
                        s_v.inc(v.tensor_copy(lga[:, b * 8 + g, :], lgb[:, 0:16]))
                        st[f"a2sh_{b}_{g}"] = s_v.n
                qe_all = ea(nc.sbuf_tensor("qe_all", [128, 4, PD], BF16))
                qT = ea(nc.sbuf_tensor("qT", [128, 16, 2, 128], BF16, is_transpose_dest=True))
                ke = ea(nc.sbuf_tensor("ke", [128, 2, PD], BF16))
                kv_sb = ea(nc.sbuf_tensor("kv_sb", [128, 2, D], BF16))
                zsum = ea(nc.sbuf_tensor("zsum", [128, 2, 16], F32))
                zinv = ea(nc.sbuf_tensor("zinv", [128, 2, 16], F32))
                ps_qk = [ea(nc.psum_tensor(f"ps_qk{i}", [128, 512], F32)) for i in range(2)]
                ps_kv = [[ea(nc.psum_tensor(f"ps_kv{eh}{dh}", [128, 512], F32)) for dh in range(2)]
                         for eh in range(2)]
                ps_nm = [ea(nc.psum_tensor(f"ps_nm{i}", [128, 512], F32)) for i in range(2)]

                @block.gpsimd
                def _(g):
                    s_sst.wait_at(g, st["xn2p_15"])
                    s_wc.inc(g.dma_start(out=qkp_sb[:, :, :],
                                         in_=qk_in.rearrange("(dc p) e -> p dc e", p=128)), 16)

                qkb = [ps_qk, ps_nm]

                def emit_kv(t_, gp, b):
                    s_a.wait_at(t_, st[f"expkp_{b}_{gp}"])
                    if b > 0 and gp == 0:
                        s_v.wait_at(t_, st[f"aresp_{b-1}_7"])
                        s_v.wait_at(t_, st[f"a2sh_{b-1}_7"])
                    sl = gp % 2
                    inst = None
                    for par in range(2):
                        for eh in range(2):
                            for dh in range(2):
                                inst = t_.matmul(ps_kv[eh][dh][:, :],
                                                 ke[:, par, eh * 128:(eh + 1) * 128],
                                                 xnb[:, sl, par, dh * 512:(dh + 1) * 512],
                                                 start=(gp == 0 and par == 0),
                                                 stop=(gp == 7 and par == 1))
                    s_t.inc(inst)
                    st[f"kvp_{b}_{gp}"] = s_t.n

                for b in (ATT_BATCHES if ATT_BATCHES is not None else range(B)):
                    for gp in range(8):
                        i0, i1 = 2 * gp, 2 * gp + 1
                        t0, t1 = b * 16 + i0, b * 16 + i1
                        sl = gp % 2

                        @block.sync
                        def _(sync, gp=gp, t0=t0, t1=t1, b=b, sl=sl):
                            if b == 0 and gp == 0:
                                s_sst.wait(sync)
                            if b > 0 and gp == 0:
                                s_t.wait_at(sync, st[f"kvp_{b-1}_7"])
                                s_sst.wait_at(sync, st[f"axp_{b-1}_7"])
                                s_tr.wait_at(sync, st[f"a2trp_{b-1}_7"])
                            if gp >= 2:
                                s_t.wait_at(sync, st[f"kvp_{b}_{gp-2}"])
                            s_ld.inc(sync.dma_start(out=xnb[:, sl, 0, :], in_=xn2_til[:, t0, :]), 16)
                            s_ld.inc(sync.dma_start(out=xnb[:, sl, 1, :], in_=xn2_til[:, t1, :]), 16)
                            s_ld.wait(sync)
                            s_tr.inc(sync.dma_start(out=xnf[:, sl, 0, :, :], in_=xnb[:, sl, 0, :], transpose=True), 16)
                            s_tr.inc(sync.dma_start(out=xnf[:, sl, 1, :, :], in_=xnb[:, sl, 1, :], transpose=True), 16)
                            st[f"atr_{b}_{gp}"] = s_tr.n
                            if gp >= 2:
                                qgp = gp - 2
                                j0, j1 = 2 * qgp, 2 * qgp + 1
                                s_a.wait_at(sync, st[f"expkp_{b}_{qgp}"])
                                if b > 0 and qgp <= 1:
                                    s_v.wait_at(sync, st[f"aresp_{b-1}_7"])
                                s_tr.inc(sync.dma_start(out=qT[:, j0, :, :], in_=qe_all[:, j0 % 4, :], transpose=True), 16)
                                s_tr.inc(sync.dma_start(out=qT[:, j1, :, :], in_=qe_all[:, j1 % 4, :], transpose=True), 16)
                                st[f"qtr_{b}_{qgp}"] = s_tr.n

                        @block.tensor
                        def _(t_, gp=gp, b=b, sl=sl):
                            s_tr.wait_at(t_, st[f"atr_{b}_{gp}"])
                            if gp == 0:
                                s_wc.wait(t_)
                            prev = st.get(f"expkp_{b}_{gp-2}", 0)
                            if prev:
                                s_a.wait_at(t_, prev)
                            if b > 0 and gp == 1:
                                s_v.wait_at(t_, st[f"aresp_{b-1}_7"])
                            qk = qkb[sl]
                            inst = None
                            for par in range(2):
                                for dc in range(8):
                                    inst = t_.matmul(qk[par][:, :], xnf[:, sl, par, dc, :], qkp_sb[:, dc, :],
                                                     start=(dc == 0), stop=(dc == 7))
                            s_t.inc(inst)
                            st[f"qkp_{b}_{gp}"] = s_t.n

                        if gp >= 1:
                            @block.tensor
                            def _(t_, gp=gp - 1, b=b):
                                emit_kv(t_, gp, b)

                        @block.scalar
                        def _(sc, gp=gp, i0=i0, i1=i1, b=b, sl=sl):
                            s_t.wait_at(sc, st[f"qkp_{b}_{gp}"])
                            prevk = st.get(f"kvp_{b}_{gp-1}", 0)
                            if prevk:
                                s_t.wait_at(sc, prevk)
                            prev = st.get(f"qtr_{b}_{gp-2}", st.get(f"qtr_{b-1}_{gp+6}", 0))
                            if prev:
                                s_tr.wait_at(sc, prev)
                            qk = qkb[sl]
                            sc.activation(out=qe_all[:, i0 % 4, :], in_=qk[0][:, 0:PD], func=AF.Exp,
                                          accum_out=zsum[:, b, i0:i0 + 1])
                            sc.activation(out=ke[:, 0, :], in_=qk[0][:, PD:2 * PD], func=AF.Exp)
                            sc.activation(out=qe_all[:, i1 % 4, :], in_=qk[1][:, 0:PD], func=AF.Exp,
                                          accum_out=zsum[:, b, i1:i1 + 1])
                            s_a.inc(sc.activation(out=ke[:, 1, :], in_=qk[1][:, PD:2 * PD], func=AF.Exp))
                            st[f"expkp_{b}_{gp}"] = s_a.n

                    @block.sync
                    def _(sync, b=b):
                        for qgp in (6, 7):
                            j0, j1 = 2 * qgp, 2 * qgp + 1
                            s_a.wait_at(sync, st[f"expkp_{b}_{qgp}"])
                            s_tr.inc(sync.dma_start(out=qT[:, j0, :, :], in_=qe_all[:, j0 % 4, :], transpose=True), 16)
                            s_tr.inc(sync.dma_start(out=qT[:, j1, :, :], in_=qe_all[:, j1 % 4, :], transpose=True), 16)
                            st[f"qtr_{b}_{qgp}"] = s_tr.n

                    @block.tensor
                    def _(t_, b=b):
                        emit_kv(t_, 7, b)

                    @block.vector
                    def _(v, b=b):
                        s_t.wait(v)
                        v.drain()
                        for eh in range(2):
                            for dh in range(2):
                                v.tensor_copy(kv_sb[:, eh, dh * 512:(dh + 1) * 512], ps_kv[eh][dh][:, :])
                        v.tensor_scalar(zinv[:, b, :], zsum[:, b, :], 1e-8, None, ALU.add)
                        v.drain()
                        s_v.inc(v.reciprocal(zinv[:, b, :], zinv[:, b, :]))
                        st[f"kvz_{b}"] = s_v.n

                    if DEBUG:
                        @block.scalar
                        def _(sc, b=b):
                            s_v.wait(sc)
                            s_sst.inc(sc.dma_start(out=dbg_z[b, :, :], in_=zinv[:, b, :]), 16)
                            s_sst.inc(sc.dma_start(out=dbg_kv[b, :, :], in_=kv_sb[:, :, :].rearrange("p a b -> p (a b)")), 16)
                            s_sst.inc(sc.dma_start(out=dbg_qe[b, :, :], in_=qe_all[:, :, :].rearrange("p a b -> p (a b)")), 16)

                    nmA = ps_nm
                    nmB = [ps_kv[0][0], ps_kv[0][1]]
                    for gp in range(8):
                        i0, i1 = 2 * gp, 2 * gp + 1
                        t0, t1 = b * 16 + i0, b * 16 + i1

                        if gp == 0:
                            @block.sync
                            def _(sync, b=b):
                                if b > 0:
                                    s_v.wait_at(sync, st[f"aresp_{b-1}_7"])
                                s_ld.inc(sync.dma_start(out=xt[:, 0, :], in_=x1_til[:, b * 16, :]), 16)
                                s_ld.inc(sync.dma_start(out=xt[:, 1, :], in_=x1_til[:, b * 16 + 1, :]), 16)

                        @block.tensor
                        def _(t_, gp=gp, i0=i0, i1=i1, b=b):
                            if gp == 0:
                                s_v.wait_at(t_, st[f"kvz_{b}"])
                                if b > 0:
                                    s_v.wait_at(t_, st[f"aresp_{b-1}_7"])
                            if gp >= 1:
                                s_v.wait_at(t_, st[f"aresp_{b}_{gp-1}"])
                            inst = None
                            for nmp, ii in ((nmA, i0), (nmB, i1)):
                                for dh in range(2):
                                    for eh in range(2):
                                        inst = t_.matmul(nmp[dh][:, :], qT[:, ii, eh, :],
                                                         kv_sb[:, eh, dh * 512:(dh + 1) * 512],
                                                         start=(eh == 0), stop=(eh == 1))
                            s_t.inc(inst)

                        @block.vector
                        def _(v, gp=gp, i0=i0, i1=i1, b=b):
                            s_t.wait(v)
                            s_ld.wait(v)
                            sl = gp % 2
                            prev = st.get(f"axp_{b}_{gp-2}", st.get(f"axp_{b-1}_{gp+6}", 0))
                            if prev:
                                s_sst.wait_at(v, prev)
                            for par, (nmp, ii) in enumerate(((nmA, i0), (nmB, i1))):
                                for dh in range(2):
                                    inst = v.scalar_tensor_tensor(aw[:, sl, par, dh * 512:(dh + 1) * 512],
                                                                  nmp[dh][:, :], zinv[:, b, ii:ii + 1],
                                                                  xt[:, par, dh * 512:(dh + 1) * 512],
                                                                  ALU.mult, ALU.add)
                            s_v.inc(inst)
                            st[f"aresp_{b}_{gp}"] = s_v.n
                            v.drain()
                            v.bn_stats(bns[:, 0, 0:6], aw[:, sl, 0, 0:512])
                            v.bn_stats(bns[:, 0, 6:12], aw[:, sl, 0, 512:1024])
                            v.bn_stats(bns[:, 1, 0:6], aw[:, sl, 1, 0:512])
                            v.bn_stats(bns[:, 1, 6:12], aw[:, sl, 1, 512:1024])
                            v.drain()
                            v.bn_aggr(mv[:, 0, :], bns[:, 0, :])
                            v.bn_aggr(mv[:, 1, :], bns[:, 1, :])
                            v.drain()
                            s_v.inc(v.tensor_scalar(rsd[:, :, :], mv[:, :, 1:2], EPS, None, ALU.add))

                        if gp < 7:
                            @block.sync
                            def _(sync, gp=gp, t0=t0, t1=t1, b=b):
                                s_v.wait_at(sync, st[f"aresp_{b}_{gp}"])
                                s_ld.inc(sync.dma_start(out=xt[:, 0, :], in_=x1_til[:, t0 + 2, :]), 16)
                                s_ld.inc(sync.dma_start(out=xt[:, 1, :], in_=x1_til[:, t1 + 2, :]), 16)

                        @block.scalar
                        def _(sc, gp=gp):
                            s_v.wait(sc)
                            sc.activation(out=rsd[:, :, :], in_=rsd[:, :, :], func=AF.Ln)
                            sc.drain()
                            s_a.inc(sc.activation(out=rsd[:, :, :], in_=rsd[:, :, :], func=AF.Exp, scale=-0.5))

                        @block.vector
                        def _(v, gp=gp, b=b):
                            s_a.wait(v)
                            sl = gp % 2
                            if gp >= 1:
                                s_sst.wait_at(v, st[f"axp_{b}_{gp-1}"])
                                s_tr.wait_at(v, st[f"a2trp_{b}_{gp-1}"])
                            v.tensor_scalar(xnb[:, 0, 0, :], aw[:, sl, 0, :], mv[:, 0, 0:1],
                                            rsd[:, 0, :], ALU.subtract, ALU.mult)
                            s_v.inc(v.tensor_scalar(xnb[:, 0, 1, :], aw[:, sl, 1, :], mv[:, 1, 0:1],
                                                    rsd[:, 1, :], ALU.subtract, ALU.mult))
                            st[f"aply_{b}_{gp}"] = s_v.n

                        @block.scalar
                        def _(sc, gp=gp, t0=t0, t1=t1, b=b):
                            s_v.wait(sc)
                            sl = gp % 2
                            s_sst.inc(sc.dma_start(out=x2_til[:, t0, :], in_=aw[:, sl, 0, :]), 16)
                            s_sst.inc(sc.dma_start(out=agxn_til[:, t0, :], in_=xnb[:, 0, 0, :]), 16)
                            s_sst.inc(sc.dma_start(out=x2_til[:, t1, :], in_=aw[:, sl, 1, :]), 16)
                            s_sst.inc(sc.dma_start(out=agxn_til[:, t1, :], in_=xnb[:, 0, 1, :]), 16)
                            st[f"axp_{b}_{gp}"] = s_sst.n

                        lgb = ps_kv[1][0]

                        @block.sync
                        def _(sync, gp=gp, b=b):
                            s_v.wait_at(sync, st[f"aply_{b}_{gp}"])
                            prev = st.get(f"a2mm_{b}_{gp-2}", st.get(f"a2mm_{b-1}_{gp+6}", 0))
                            s_t.wait_at(sync, prev)
                            s_tr.inc(sync.dma_start(out=xnf2[:, gp % 2, 0, :, :], in_=xnb[:, 0, 0, :], transpose=True), 16)
                            s_tr.inc(sync.dma_start(out=xnf2[:, gp % 2, 1, :, :], in_=xnb[:, 0, 1, :], transpose=True), 16)
                            st[f"a2trp_{b}_{gp}"] = s_tr.n

                        if gp >= 2:
                            emit_router_mm(b, gp - 2)

                    emit_router_mm(b, 6)
                    emit_router_mm(b, 7)
                    st["attn_prev_b"] = s_t.n

                for q in range(16):
                    t0, t1 = 2 * q, 2 * q + 1

                    @block.vector
                    def _(v, q=q, t0=t0, t1=t1):
                        if q == 0:
                            s_v.wait(v)
                        if q >= 1:
                            s_a.wait_at(v, st[f"p2exp_{q-1}"])
                            s_t.wait_at(v, st[f"c2mm_{q-1}"])
                        inst = None
                        for par, tt in ((0, t0), (1, t1)):
                            v.max(mx8[:, :], lga[:, q, par * 8:(par + 1) * 8])
                            v.drain()
                            v.max_index(ix8[:, :], mx8[:, :], lga[:, q, par * 8:(par + 1) * 8])
                            v.drain()
                            v.tensor_copy(idxf[:, tt:tt + 1], ix8[:, 0:1])
                            v.tensor_copy(idc[:, :], ix8[:, 0:1])
                            v.drain()
                            v.tensor_scalar(ohtm[:, par, :], iotaef[:, :], idc[:, :], None, ALU.is_equal)
                            inst = v.tensor_scalar(sh8[:, par, :], lga[:, q, par * 8:(par + 1) * 8],
                                                   mx8[:, 0:1], None, ALU.subtract)
                            v.drain()
                        s_v.inc(inst)
                        st[f"p2sh_{q}"] = s_v.n

                    @block.scalar
                    def _(sc, q=q, t0=t0, t1=t1):
                        s_v.wait_at(sc, st[f"p2sh_{q}"])
                        sc.activation(out=she[:, 0, :], in_=sh8[:, 0, :], func=AF.Exp,
                                      accum_out=esum[:, t0:t0 + 1])
                        s_a.inc(sc.activation(out=she[:, 1, :], in_=sh8[:, 1, :], func=AF.Exp,
                                              accum_out=esum[:, t1:t1 + 1]))
                        st[f"p2exp_{q}"] = s_a.n

                    @block.tensor
                    def _(t_, q=q):
                        s_v.wait_at(t_, st[f"p2sh_{q}"])
                        inst = None
                        for par in range(2):
                            inst = t_.matmul(ps_nm[0][0:E, 0:1], ohtm[:, par, :], ones128b[:, :],
                                             start=(q == 0 and par == 0),
                                             stop=(q == 15 and par == 1))
                        s_t.inc(inst)
                        st[f"c2mm_{q}"] = s_t.n

                @block.vector
                def _(v):
                    s_a.wait(v)
                    v.reciprocal(comb[:, :], esum[:, :])
                    v.drain()
                    s_v.inc(v.tensor_copy(idxw[:, :], idxf[:, :]))

                @block.vector
                def _(v):
                    s_t.wait(v)
                    s_v.inc(v.tensor_copy(cnt8[:, :], ps_nm[0][0:E, 0:1]))
                    st["cntev_1"] = s_v.n

                @block.gpsimd
                def _(g):
                    for k in range(8):
                        g.wait_ge(s_sst.h, st[f"axp_{k // 4}_{2 * (k % 4) + 1}"])
                        s_cc[5].inc(g.collective_compute(
                            "AllGather", ALU.bypass, replica_groups=[list(range(NCORE))],
                            ins=[ag_xn_in.ap()[512 * k:512 * (k + 1), :].opt()],
                            outs=[ag_xn_out.ap()[NCORE * 512 * k:NCORE * 512 * (k + 1), :].opt()]))
                    g.wait_ge(s_v.h, st["cntev_1"])
                    s_gd.inc(g.dma_start(out=cnt_dram.ap(), in_=cnt8[:, :]), 16)
                    s_gd.wait(g)
                    s_cc[4].inc(g.collective_compute(
                        "AllGather", ALU.bypass, replica_groups=[list(range(NCORE))],
                        ins=[cnt_dram.ap().opt()], outs=[cnt_ag.ap().opt()]))

        emit_routing(0)
        emit_expert(0)
        emit_return(0)
        emit_attention()

        @block.gpsimd
        def _(g):
            s_cc[2].wait_at(g, 1)
            s_gd.inc(g.dma_start(out=S_dram.ap().rearrange("(a b) one -> a (b one)", b=128),
                                 in_=S_init.ap().rearrange("(a b) one -> a (b one)", b=128)), 16)

        emit_routing(1)
        emit_expert(1)
        emit_return(1)

        @block.sync
        def _(sync):
            s_sst.wait(sync)
            s_gd.wait(sync)
            if DEBUG:
                s_ld.inc(sync.dma_start(out=dbg_x1[:, :], in_=x1_dram.ap()[:, :]), 16)
                s_ld.inc(sync.dma_start(out=dbg_x2[:, :], in_=x2_dram.ap()[:, :]), 16)
                s_ld.inc(sync.dma_start(out=dbg_xn2[:, :], in_=xn2_dram.ap()[:, :]), 16)
                s_ld.inc(sync.dma_start(out=dbg_xn3[:, :], in_=ag_xn_in.ap()[:, :]), 16)
                s_ld.inc(sync.dma_start(out=dbg_y[:, :], in_=y_dram.ap()[:, :]), 16)
                s_ld.inc(sync.dma_start(out=dbg_r[:, 0:NT], in_=rsel[:, :]), 16)
                s_ld.inc(sync.dma_start(out=dbg_r[:, NT:2 * NT], in_=wsc[:, :]), 16)
                s_ld.inc(sync.dma_start(out=dbg_r[:, 2 * NT:3 * NT], in_=gretf[:, :]), 16)
                s_ld.inc(sync.dma_start(out=dbg_r[:, 3 * NT:4 * NT], in_=comb[:, :]), 16)
                s_ld.inc(sync.dma_start(out=dbg_g[:, :], in_=g_u[:, :]), 16)
                s_ld.wait(sync)

    return nc


_CACHED = {}


def kernel(L0_attn_x, L0_attn_q_proj, L0_attn_k_proj, L0_ffn_wg, L0_ffn_W1,
           L0_ffn_b1, L0_ffn_W2, L0_ffn_b2):
    x = np.ascontiguousarray(np.asarray(L0_attn_x, dtype=np.float32))
    qk = np.ascontiguousarray(np.concatenate([np.asarray(L0_attn_q_proj, np.float32),
                                              np.asarray(L0_attn_k_proj, np.float32)], axis=1))
    wg = np.ascontiguousarray(np.asarray(L0_ffn_wg, np.float32))
    W1 = np.asarray(L0_ffn_W1, np.float32)
    b1 = np.asarray(L0_ffn_b1, np.float32)
    W2 = np.asarray(L0_ffn_W2, np.float32)
    b2 = np.asarray(L0_ffn_b2, np.float32)
    ident = np.eye(128, dtype=ml_dtypes.bfloat16)

    if "nc" not in _CACHED:
        _CACHED["nc"] = build()
    nc = _CACHED["nc"]

    in_maps = []
    for c in range(NCORE):
        pm = (np.arange(E) < c).astype(np.float32).reshape(E, 1)
        in_maps.append({
            "x": np.ascontiguousarray(x[2 * c:2 * c + 2].reshape(T4, D)),
            "qk": qk,
            "wg": wg,
            "w1": np.ascontiguousarray(W1[c]),
            "b1": np.ascontiguousarray(b1[c].reshape(32, 128).T),
            "w2": np.ascontiguousarray(W2[c]),
            "b2": np.ascontiguousarray(b2[c].reshape(1, D)),
            "premask": pm,
            "myc": np.full((128, 1), float(c), np.float32),
            "ident": ident,
        })
    res = run_bass_kernel_spmd(nc, in_maps, core_ids=list(range(NCORE)))
    out = np.concatenate([r["xo"].reshape(2, TSEQ, D) for r in res.results], axis=0)
    return out


# version 47
# speedup vs baseline: 1.0505x; 1.0046x over previous
import numpy as np
import ml_dtypes
import concourse.bass as bass
import concourse.mybir as mybir
from concourse.bass import IndirectOffsetOnAxis
from concourse.bass_utils import run_bass_kernel_spmd
from contextlib import ExitStack

F32 = mybir.dt.float32
BF16 = mybir.dt.bfloat16
U32 = mybir.dt.uint32
I32 = mybir.dt.int32
AF = mybir.ActivationFunctionType
ALU = mybir.AluOpType
AX = mybir.AxisListType

NCORE = 8
B = 2
TSEQ = 2048
T4 = B * TSEQ
NT = 32
D = 1024
E = 8
FF = 4096
PD = 256
CAP = 5121
NC41 = 41
RPAD = NC41 * 128
CHT = 384
NCH = 14
BIG = float(2 ** 30)
C23 = 12582912.0
EPS = 1e-5
DEBUG = False
ATT_BATCHES = None


class Sem:
    def __init__(self, ctx, nc, name):
        self.h = ctx.enter_context(nc.semaphore(name))
        self.n = 0

    def inc(self, inst, k=1):
        inst.then_inc(self.h, k)
        self.n += k

    def wait(self, eng):
        eng.wait_ge(self.h, self.n)

    def wait_at(self, eng, v):
        if v > 0:
            eng.wait_ge(self.h, v)


def build():
    nc = bass.Bass("TRN2", target_bir_lowering=False)
    x_in = nc.dram_tensor("x", [T4, D], F32, kind="ExternalInput").ap()
    qk_in = nc.dram_tensor("qk", [D, 2 * PD], F32, kind="ExternalInput").ap()
    wg_in = nc.dram_tensor("wg", [D, E], F32, kind="ExternalInput").ap()
    w1_in = nc.dram_tensor("w1", [D, FF], F32, kind="ExternalInput").ap()
    b1_in = nc.dram_tensor("b1", [128, 32], F32, kind="ExternalInput").ap()
    w2_in = nc.dram_tensor("w2", [FF, D], F32, kind="ExternalInput").ap()
    b2_in = nc.dram_tensor("b2", [1, D], F32, kind="ExternalInput").ap()
    premask_in = nc.dram_tensor("premask", [E, 1], F32, kind="ExternalInput").ap()
    myc_in = nc.dram_tensor("myc", [128, 1], F32, kind="ExternalInput").ap()
    ident_in = nc.dram_tensor("ident", [128, 128], BF16, kind="ExternalInput").ap()
    xo = nc.dram_tensor("xo", [T4, D], F32, kind="ExternalOutput").ap()
    if DEBUG:
        dbg_x1 = nc.dram_tensor("dbg_x1", [T4, D], F32, kind="ExternalOutput").ap()
        dbg_x2 = nc.dram_tensor("dbg_x2", [T4, D], F32, kind="ExternalOutput").ap()
        dbg_xn2 = nc.dram_tensor("dbg_xn2", [T4, D], BF16, kind="ExternalOutput").ap()
        dbg_xn3 = nc.dram_tensor("dbg_xn3", [T4, D], BF16, kind="ExternalOutput").ap()
        dbg_y = nc.dram_tensor("dbg_y", [RPAD, D], BF16, kind="ExternalOutput").ap()
        dbg_r = nc.dram_tensor("dbg_r", [128, 4 * NT], F32, kind="ExternalOutput").ap()
        dbg_g = nc.dram_tensor("dbg_g", [128, NC41], U32, kind="ExternalOutput").ap()
        dbg_z = nc.dram_tensor("dbg_z", [2, 128, 16], F32, kind="ExternalOutput").ap()
        dbg_kv = nc.dram_tensor("dbg_kv", [2, 128, 2048], BF16, kind="ExternalOutput").ap()
        dbg_qe = nc.dram_tensor("dbg_qe", [2, 128, 16 * PD], BF16, kind="ExternalOutput").ap()
        dbg_nm = nc.dram_tensor("dbg_nm", [2, 128, 1024], F32, kind="ExternalOutput").ap()
        dbg_t0 = nc.dram_tensor("dbg_t0", [3, 128, 1024], BF16, kind="ExternalOutput").ap()

    ag_xn_in = nc.dram_tensor("ag_xn_in", [T4, D], BF16)
    ag_xn_out = nc.dram_tensor("ag_xn_out", [NCORE * T4, D], BF16, addr_space="Shared")
    cnt_dram = nc.dram_tensor("cnt_dram", [1, E], F32)
    cnt_ag = nc.dram_tensor("cnt_ag", [NCORE, E], F32, addr_space="Shared")
    S_dram = nc.dram_tensor("S_dram", [E * RPAD, 1], I32)
    S_init = nc.dram_tensor("S_init", [E * RPAD, 1], I32)
    S_ag = nc.dram_tensor("S_ag", [NCORE * E * RPAD, 1], I32, addr_space="Shared")
    y_dA = nc.dram_tensor("y_dA", [3840, D], BF16)
    y_dB = nc.dram_tensor("y_dB", [1408, D], BF16)
    y_agA = nc.dram_tensor("y_agA", [NCORE * 3840, D], BF16, addr_space="Shared")
    y_agB = nc.dram_tensor("y_agB", [NCORE * 1408, D], BF16, addr_space="Shared")
    x1_dram = nc.dram_tensor("x1_dram", [T4, D], F32)
    x2_dram = nc.dram_tensor("x2_dram", [T4, D], F32)
    xn2_dram = nc.dram_tensor("xn2_dram", [T4, D], BF16)

    x_til = x_in.rearrange("(t p) d -> p t d", p=128)
    x1_til = x1_dram.ap().rearrange("(t p) d -> p t d", p=128)
    x2_til = x2_dram.ap().rearrange("(t p) d -> p t d", p=128)
    xo_til = xo.rearrange("(t p) d -> p t d", p=128)
    agxn_til = ag_xn_in.ap().rearrange("(t p) d -> p t d", p=128)
    xn2_til = xn2_dram.ap().rearrange("(t p) d -> p t d", p=128)
    y_tilA = y_dA.ap().rearrange("(c p) d -> p c d", p=128)
    y_tilB = y_dB.ap().rearrange("(c p) d -> p c d", p=128)
    S_rows = S_ag.ap().rearrange("(row c) one -> row (c one)", c=NC41)

    with ExitStack() as ctx:
        en = ctx.enter_context
        w1_sb = en(nc.sbuf_tensor("w1_sb", [128, 8, FF], BF16))
        w2_sb = en(nc.sbuf_tensor("w2_sb", [128, 32, D], BF16))
        wg_sb = en(nc.sbuf_tensor("wg_sb", [128, 8, E], BF16))
        b1_sb = en(nc.sbuf_tensor("b1_sb", [128, 32], F32))
        b2b = en(nc.sbuf_tensor("b2b", [128, D], F32))
        idn = en(nc.sbuf_tensor("idn", [128, 128], BF16))
        ones18 = en(nc.sbuf_tensor("ones18", [1, E], BF16))
        ones128b = en(nc.sbuf_tensor("ones128b", [128, 1], BF16))
        ohtm = en(nc.sbuf_tensor("ohtm", [128, 2, E], BF16))
        idc = en(nc.sbuf_tensor("idc", [128, 1], F32))
        ones8c = en(nc.sbuf_tensor("ones8c", [E, 1], F32))
        iota8c = en(nc.sbuf_tensor("iota8c", [E, 1], F32))
        iotaef = en(nc.sbuf_tensor("iotaef", [128, E], I32))
        locid = en(nc.sbuf_tensor("locid", [128, NT], I32))
        iota128 = en(nc.sbuf_tensor("iota128", [128, 1], I32))
        premask = en(nc.sbuf_tensor("premask_sb", [E, 1], F32))
        myc = en(nc.sbuf_tensor("myc_sb", [128, 1], F32))
        midxu = en(nc.sbuf_tensor("midxu", [128, E], U32))
        mbase = en(nc.sbuf_tensor("mbase", [128, 1], F32))
        xt = en(nc.sbuf_tensor("xt", [128, 2, D], F32))
        xnb = en(nc.sbuf_tensor("xnb", [128, 2, 2, D], BF16))
        xnf = en(nc.sbuf_tensor("xnf", [128, 2, 2, 8, 128], BF16, is_transpose_dest=True))
        bns = en(nc.sbuf_tensor("bns", [128, 2, 12], F32))
        mv = en(nc.sbuf_tensor("mv", [128, 2, 2], F32))
        rsd = en(nc.sbuf_tensor("rsd", [128, 2, 1], F32))
        nbi = en(nc.sbuf_tensor("nbi", [128, 2, 1], F32))
        g_u = en(nc.sbuf_tensor("g_u", [128, NC41], U32))
        gretu = en(nc.sbuf_tensor("gretu", [128, NT], U32))
        gretu2 = en(nc.sbuf_tensor("gretu2", [128, NT], U32))
        wsc = en(nc.sbuf_tensor("wsc", [128, NT], F32))
        cnt8 = en(nc.sbuf_tensor("cnt8", [E, 1], F32))
        cntall = en(nc.sbuf_tensor("cntall", [E, E], F32))
        base8 = en(nc.sbuf_tensor("base8", [E, 1], F32))
        mx8 = en(nc.sbuf_tensor("mx8", [128, 8], F32))
        ix8 = en(nc.sbuf_tensor("ix8", [128, 8], U32))
        idxf = en(nc.sbuf_tensor("idxf", [128, NT], BF16))
        idxw = en(nc.sbuf_tensor("idxw", [128, NT], F32))
        sh8 = en(nc.sbuf_tensor("sh8", [128, 2, E], F32))
        she = en(nc.sbuf_tensor("she", [128, 2, E], F32))
        esum = en(nc.sbuf_tensor("esum", [128, NT], F32))
        comb = en(nc.sbuf_tensor("comb", [128, NT], F32))
        rsel = en(nc.sbuf_tensor("rsel", [128, NT], F32))
        vld = en(nc.sbuf_tensor("vld", [128, NT], F32))
        scc = en(nc.sbuf_tensor("scc", [128, NT], F32))
        spp = en(nc.sbuf_tensor("spp", [128, NT], F32))
        offf = en(nc.sbuf_tensor("offf", [128, NT], F32))
        off2 = en(nc.sbuf_tensor("off2", [128, NT], F32))
        offu = en(nc.sbuf_tensor("offu", [128, NT], U32))
        gretf = en(nc.sbuf_tensor("gretf", [128, NT], F32))

        s_ld = Sem(ctx, nc, "s_ld")
        s_ld0 = Sem(ctx, nc, "s_ld0")
        s_wc = Sem(ctx, nc, "s_wc")
        s_tr = Sem(ctx, nc, "s_tr")
        s_str = Sem(ctx, nc, "s_str")
        s_sst = Sem(ctx, nc, "s_sst")
        s_gd = Sem(ctx, nc, "s_gd")
        s_gc = Sem(ctx, nc, "s_gc")
        s_v = Sem(ctx, nc, "s_v")
        s_a = Sem(ctx, nc, "s_a")
        s_t = Sem(ctx, nc, "s_t")
        s_fill = Sem(ctx, nc, "s_fill")
        s_m2 = Sem(ctx, nc, "s_m2")
        s_cc = [Sem(ctx, nc, f"s_cc{i}") for i in range(8)]
        s_ccy = [Sem(ctx, nc, f"s_ccy{i}") for i in range(6)]

        block = en(nc.Block())
        st = {"xnst": 0}

        c0 = ctx.enter_context(ExitStack())
        b2r = c0.enter_context(nc.sbuf_tensor("b2r", [1, D], F32))
        neg1 = c0.enter_context(nc.sbuf_tensor("neg1", [128, E * NC41], I32))
        ones1 = c0.enter_context(nc.sbuf_tensor("ones1", [1, 128], F32))

        @block.sync
        def _(sync):
            s_ld0.inc(sync.dma_start(out=b1_sb[:, :], in_=b1_in[:, :]), 16)
            s_ld0.inc(sync.dma_start(out=b2r[:, :], in_=b2_in[:, :]), 16)
            s_ld0.inc(sync.dma_start(out=idn[:, :], in_=ident_in[:, :]), 16)
            s_ld0.inc(sync.dma_start(out=premask[:, :], in_=premask_in[:, :]), 16)
            s_ld0.inc(sync.dma_start(out=myc[:, :], in_=myc_in[:, :]), 16)

        @block.gpsimd
        def _(g):
            g.iota(iotaef[:, :], pattern=[[1, E]], base=0, channel_multiplier=0)
            g.iota(iota8c[:, :], pattern=[[1, 1]], base=0, channel_multiplier=1,
                   allow_small_or_imprecise_dtypes=True)
            g.iota(locid[:, :], pattern=[[NCORE * 512, NT // 4], [128, 4]], base=0,
                   channel_multiplier=1)
            g.iota(iota128[:, :], pattern=[[1, 1]], base=0, channel_multiplier=1)
            g.memset(neg1[:, :], -1)
            g.memset(ones1[:, :], 1.0)
            g.memset(ones18[:, :], 1.0)
            g.memset(ones8c[:, :], 1.0)
            g.memset(ones128b[:, :], 1.0)
            st["rb_slot"] = g.to_reg(E * RPAD - 1)
            st["rb_srow"] = g.to_reg(NCORE * E * 128 - 1)
            st["rb_xn"] = g.to_reg(NCORE * T4 - 1)
            st["rb_yA"] = g.to_reg(NCORE * 3840 - 1)
            st["rb_yB"] = g.to_reg(NCORE * 1408 - 1)
            s_gc.inc(g.drain())
            s_wc.inc(g.dma_start(out=wg_sb[:, :, :], in_=wg_in.rearrange("(dc p) e -> p dc e", p=128)), 16)
            s_wc.inc(g.dma_start(out=w1_sb[:, :, :], in_=w1_in.rearrange("(dc p) f -> p dc f", p=128)), 16)
            s_wc.inc(g.dma_start(out=w2_sb[:, :, :], in_=w2_in.rearrange("(fc p) d -> p fc d", p=128)), 16)
            s_gd.inc(g.dma_start(out=S_dram.ap().rearrange("(e p c) one -> p e (c one)", e=E, p=128),
                                 in_=neg1[:, :].rearrange("p (e c) -> p e c", e=E)), 16)
            s_gd.inc(g.dma_start(out=S_init.ap().rearrange("(e p c) one -> p e (c one)", e=E, p=128),
                                 in_=neg1[:, :].rearrange("p (e c) -> p e c", e=E)), 16)

        @block.vector
        def _(v):
            s_ld0.wait(v)
            s_gc.wait_at(v, 1)
            v.scalar_tensor_tensor(mbase[:, :], myc[:, :], 128.0, iota128[:, :], ALU.mult, ALU.add)
            v.drain()
            for r in range(NCORE):
                v.tensor_scalar(offf[:, r:r + 1], mbase[:, :], float(r * 1024), None, ALU.add)
            v.drain()
            s_v.inc(v.tensor_copy(midxu[:, :], offf[:, 0:E]))

        ps_tmp = c0.enter_context(nc.psum_tensor("ps_b2", [128, 512], F32))
        for dh in range(2):
            @block.tensor
            def _(t_, dh=dh):
                if dh == 0:
                    s_ld0.wait(t_)
                    s_v.wait(t_)
                else:
                    s_v.wait(t_)
                s_t.inc(t_.matmul(ps_tmp[:, :], ones1[:, :], b2r[:, dh * 512:(dh + 1) * 512],
                                  start=True, stop=True))

            @block.vector
            def _(v, dh=dh):
                s_t.wait(v)
                s_v.inc(v.tensor_copy(b2b[:, dh * 512:(dh + 1) * 512], ps_tmp[:, :]))
        c0.close()

        def emit_routing(call):
            with ExitStack() as rc:
                er = rc.enter_context
                sfx = f"_r{call}"
                idxrow = er(nc.sbuf_tensor("idxrow" + sfx, [1, T4], BF16))
                ohT = er(nc.sbuf_tensor("ohT" + sfx, [E, T4], BF16))
                rT = er(nc.sbuf_tensor("rT" + sfx, [E, T4], F32))
                prodc = er(nc.sbuf_tensor("prodc" + sfx, [E, 2, 512], F32))
                Ssb = er(nc.sbuf_tensor("Ssb" + sfx, [128, E, NC41], I32))
                mrg_m = er(nc.sbuf_tensor("mrg_m" + sfx, [128, NC41], F32))
                mrg_t = er(nc.sbuf_tensor("mrg_t" + sfx, [128, NC41], F32))
                gacc = er(nc.sbuf_tensor("gacc" + sfx, [128, NC41], F32))
                ps_lg = er(nc.psum_tensor("ps_lg" + sfx, [128, 2, E], F32))
                ps_xnt = er(nc.psum_tensor("ps_xnt" + sfx, [128, 2, 8, 128], BF16))
                ps_bc = er(nc.psum_tensor("ps_bc" + sfx, [E, 2, 512], F32))
                ps_b8 = er(nc.psum_tensor("ps_b8" + sfx, [E, 1], F32))
                ps_cnt = er(nc.psum_tensor("ps_cnt" + sfx, [E, 1], F32))
                ps_rs = er(nc.psum_tensor("ps_rs" + sfx, [128, NT], F32))

                xnst0 = s_sst.n
                cc_cnt = s_cc[call * 4 + 0]
                cc_xn = s_cc[call * 4 + 1]
                if call == 0:
                    @block.gpsimd
                    def _(g):
                        for k in range(8):
                            g.wait_ge(s_sst.h, xnst0 + 32 * (2 * k + 2))
                            cc_xn.inc(g.collective_compute(
                                "AllGather", ALU.bypass, replica_groups=[list(range(NCORE))],
                                ins=[ag_xn_in.ap()[512 * k:512 * (k + 1), :].opt()],
                                outs=[ag_xn_out.ap()[NCORE * 512 * k:NCORE * 512 * (k + 1), :].opt()]))

                    for gp in range(NT // 2):
                        t0, t1 = 2 * gp, 2 * gp + 1

                        @block.sync
                        def _(sync, gp=gp, t0=t0, t1=t1, call=call):
                            if gp == 0:
                                if call == 1:
                                    s_t.wait(sync)
                                    s_v.wait(sync)
                                    s_a.wait(sync)
                                    s_sst.wait(sync)
                            if call == 0:
                                if gp >= 1:
                                    s_v.wait_at(sync, st[f"lnap_{call}_{gp-1}"])
                                s_ld.inc(sync.dma_start(out=xt[:, 0, :], in_=x_til[:, t0, :]), 16)
                                s_ld.inc(sync.dma_start(out=xt[:, 1, :], in_=x_til[:, t1, :]), 16)
                            else:
                                if gp >= 1:
                                    s_tr.wait_at(sync, st[f"trpp_{call}_{gp-1}"])
                                s_ld.inc(sync.dma_start(out=xnb[:, 0, 0, :], in_=agxn_til[:, t0, :]), 16)
                                s_ld.inc(sync.dma_start(out=xnb[:, 0, 1, :], in_=agxn_til[:, t1, :]), 16)

                        if call == 0:
                            @block.vector
                            def _(v, gp=gp, t0=t0, t1=t1):
                                s_ld.wait(v)
                                if gp >= 1:
                                    s_t.wait_at(v, st[f"xtm_0_{gp-1}"])
                                    s_sst.wait_at(v, xnst0 + 32 * gp)
                                v.bn_stats(bns[:, 0, 0:6], xt[:, 0, 0:512])
                                v.bn_stats(bns[:, 0, 6:12], xt[:, 0, 512:1024])
                                v.bn_stats(bns[:, 1, 0:6], xt[:, 1, 0:512])
                                v.bn_stats(bns[:, 1, 6:12], xt[:, 1, 512:1024])
                                v.drain()
                                v.bn_aggr(mv[:, 0, :], bns[:, 0, :])
                                v.bn_aggr(mv[:, 1, :], bns[:, 1, :])
                                v.drain()
                                s_v.inc(v.tensor_scalar(rsd[:, :, :], mv[:, :, 1:2], EPS, None, ALU.add))

                            @block.scalar
                            def _(sc, gp=gp):
                                s_v.wait(sc)
                                sc.activation(out=rsd[:, :, :], in_=rsd[:, :, :], func=AF.Ln)
                                sc.drain()
                                s_a.inc(sc.activation(out=rsd[:, :, :], in_=rsd[:, :, :], func=AF.Exp, scale=-0.5))
                                st[f"rsq_0_{gp}"] = s_a.n

                            @block.vector
                            def _(v, gp=gp):
                                s_a.wait_at(v, st[f"rsq_0_{gp}"])
                                v.tensor_scalar(xnb[:, 0, 0, :], xt[:, 0, :], mv[:, 0, 0:1],
                                                rsd[:, 0, :], ALU.subtract, ALU.mult)
                                s_v.inc(v.tensor_scalar(xnb[:, 0, 1, :], xt[:, 1, :], mv[:, 1, 0:1],
                                                        rsd[:, 1, :], ALU.subtract, ALU.mult))
                                st[f"lnap_0_{gp}"] = s_v.n

                            @block.scalar
                            def _(sc, gp=gp, t0=t0, t1=t1):
                                s_v.wait(sc)
                                s_sst.inc(sc.dma_start(out=agxn_til[:, t0, :], in_=xnb[:, 0, 0, :]), 16)
                                s_sst.inc(sc.dma_start(out=agxn_til[:, t1, :], in_=xnb[:, 0, 1, :]), 16)

                        @block.tensor
                        def _(t_, gp=gp, call=call):
                            s_v.wait(t_)
                            if gp >= 1:
                                s_a.wait_at(t_, st[f"xev_{call}_{gp-1}"])
                            inst = None
                            for par in range(2):
                                for dc in range(8):
                                    inst = t_.transpose(ps_xnt[:, par, dc, :],
                                                        xnb[:, 0, par, dc * 128:(dc + 1) * 128],
                                                        idn[:, :])
                            s_t.inc(inst)
                            st[f"xtm_{call}_{gp}"] = s_t.n

                        @block.scalar
                        def _(sc, gp=gp, call=call):
                            s_t.wait_at(sc, st[f"xtm_{call}_{gp}"])
                            if gp >= 1:
                                s_t.wait_at(sc, st[f"rtp_{call}_{gp-1}"])
                            sc.activation(out=xnf[:, 0, 0, :, :], in_=ps_xnt[:, 0, :, :], func=AF.Copy)
                            s_a.inc(sc.activation(out=xnf[:, 0, 1, :, :], in_=ps_xnt[:, 1, :, :], func=AF.Copy))
                            st[f"xev_{call}_{gp}"] = s_a.n

                        @block.tensor
                        def _(t_, gp=gp, call=call):
                            s_a.wait_at(t_, st[f"xev_{call}_{gp}"])
                            if gp == 0:
                                s_wc.wait_at(t_, 16)
                            if gp >= 1:
                                s_v.wait_at(t_, st[f"shp_{call}_{gp-1}"])
                            inst = None
                            for par in range(2):
                                for dc in range(8):
                                    inst = t_.matmul(ps_lg[:, par, :], xnf[:, 0, par, dc, :], wg_sb[:, dc, :],
                                                     start=(dc == 0), stop=(dc == 7))
                            s_t.inc(inst)
                            st[f"rtp_{call}_{gp}"] = s_t.n

                        @block.vector
                        def _(v, gp=gp, t0=t0, t1=t1, call=call):
                            s_t.wait(v)
                            if gp >= 1:
                                s_a.wait_at(v, st[f"expp_{call}_{gp-1}"])
                                s_t.wait_at(v, st[f"cntp_{call}_{gp-1}"])
                            for par, tt in ((0, t0), (1, t1)):
                                v.max(mx8[:, :], ps_lg[:, par, :])
                                v.drain()
                                v.max_index(ix8[:, :], mx8[:, :], ps_lg[:, par, :])
                                v.drain()
                                v.tensor_copy(idxf[:, tt:tt + 1], ix8[:, 0:1])
                                v.tensor_copy(idc[:, :], ix8[:, 0:1])
                                v.drain()
                                v.tensor_scalar(ohtm[:, par, :], iotaef[:, :], idc[:, :], None, ALU.is_equal)
                                inst = v.tensor_scalar(sh8[:, par, :], ps_lg[:, par, :], mx8[:, 0:1], None, ALU.subtract)
                                v.drain()
                            s_v.inc(inst)
                            st[f"shp_{call}_{gp}"] = s_v.n

                        @block.scalar
                        def _(sc, gp=gp, t0=t0, t1=t1, call=call):
                            s_v.wait(sc)
                            sc.activation(out=she[:, 0, :], in_=sh8[:, 0, :], func=AF.Exp,
                                          accum_out=esum[:, t0:t0 + 1])
                            s_a.inc(sc.activation(out=she[:, 1, :], in_=sh8[:, 1, :], func=AF.Exp,
                                                  accum_out=esum[:, t1:t1 + 1]))
                            st[f"expp_{call}_{gp}"] = s_a.n

                        @block.tensor
                        def _(t_, gp=gp, call=call):
                            s_v.wait(t_)
                            inst = None
                            for par in range(2):
                                inst = t_.matmul(ps_cnt[:, :], ohtm[:, par, :], ones128b[:, :],
                                                 start=(gp == 0 and par == 0),
                                                 stop=(gp == NT // 2 - 1 and par == 1))
                            s_t.inc(inst)
                            st[f"cntp_{call}_{gp}"] = s_t.n
                            if gp == NT // 2 - 1:
                                st[f"cntmm_{call}"] = s_t.n

                    @block.vector
                    def _(v, call=call):
                        s_t.wait(v)
                        s_v.inc(v.tensor_copy(cnt8[:, :], ps_cnt[:, :]))
                        st[f"cntev_{call}"] = s_v.n
                        s_a.wait(v)
                        v.reciprocal(comb[:, :], esum[:, :])
                        v.drain()
                        s_v.inc(v.tensor_copy(idxw[:, :], idxf[:, :]))

                    @block.gpsimd
                    def _(g, call=call):
                        g.wait_ge(s_v.h, st[f"cntev_{call}"])
                        s_gd.inc(g.dma_start(out=cnt_dram.ap(), in_=cnt8[:, :]), 16)
                        s_gd.wait(g)
                        cc_cnt.inc(g.collective_compute(
                            "AllGather", ALU.bypass, replica_groups=[list(range(NCORE))],
                            ins=[cnt_dram.ap().opt()], outs=[cnt_ag.ap().opt()]))

                    for q in range(8):
                        @block.tensor
                        def _(t_, q=q):
                            s_v.wait(t_)
                            inst = None
                            for j in range(4):
                                tl = q * 4 + j
                                inst = t_.transpose(ps_xnt[0:1, q % 2, j, :],
                                                    idxf[:, tl:tl + 1], idn[:, :])
                            s_t.inc(inst)

                        @block.vector
                        def _(v, q=q):
                            s_t.wait(v)
                            if q == 0:
                                s_gd.wait(v)
                            s_v.inc(v.tensor_copy(idxrow[:, q * 512:(q + 1) * 512],
                                                  ps_xnt[0:1, q % 2, 0:4, :].rearrange("p a b -> p (a b)")))

                    for q in range(8):
                        @block.tensor
                        def _(t_, q=q):
                            s_v.wait(t_)
                            s_t.inc(t_.matmul(ps_bc[:, q % 2, :], ones18[:, :], idxrow[:, q * 512:(q + 1) * 512],
                                              start=True, stop=True))

                        @block.vector
                        def _(v, q=q):
                            s_t.wait(v)
                            s_v.inc(v.tensor_scalar(ohT[:, q * 512:(q + 1) * 512], ps_bc[:, q % 2, :],
                                                    iota8c[:, :], None, ALU.is_equal))


                @block.sync
                def _(sync):
                    cc_cnt.wait_at(sync, 1)
                    s_ld.inc(sync.dma_start(out=cntall[:, :], in_=cnt_ag.ap()[:, :]), 16)

                @block.tensor
                def _(t_):
                    s_ld.wait(t_)
                    s_t.inc(t_.matmul(ps_b8[:, :], cntall[:, :], premask[:, :], start=True, stop=True))

                @block.vector
                def _(v):
                    s_t.wait(v)
                    v.tensor_copy(base8[:, :], ps_b8[:, :])
                    v.drain()
                    s_v.inc(v.tensor_tensor_scan(rT[:, :], ohT[:, :], ohT[:, :], base8[:, :],
                                                 ALU.add, ALU.max))

                for q in range(8):
                    @block.vector
                    def _(v, q=q):
                        if q >= 2:
                            s_t.wait_at(v, st[f"rsmm_{call}_{q-2}"])
                        v.drain()
                        s_v.inc(v.tensor_tensor(prodc[:, q % 2, :], ohT[:, q * 512:(q + 1) * 512],
                                                rT[:, q * 512:(q + 1) * 512], ALU.mult))

                    @block.tensor
                    def _(t_, q=q, call=call):
                        s_v.wait(t_)
                        inst = None
                        for j in range(4):
                            tl = q * 4 + j
                            inst = t_.matmul(ps_rs[:, tl:tl + 1], prodc[:, q % 2, j * 128:(j + 1) * 128],
                                             ones8c[:, :], start=True, stop=True)
                        s_t.inc(inst)
                        st[f"rsmm_{call}_{q}"] = s_t.n

                @block.vector
                def _(v):
                    s_t.wait(v)
                    v.tensor_copy(rsel[:, :], ps_rs[:, :])
                    v.drain()
                    v.tensor_scalar(vld[:, :], rsel[:, :], float(CAP) + 0.5, None, ALU.is_le)
                    v.drain()
                    v.tensor_tensor(wsc[:, :], comb[:, :], vld[:, :], ALU.mult)
                    v.drain()
                    v.tensor_scalar(wsc[:, :], wsc[:, :], 0.5, None, ALU.mult)
                    v.tensor_scalar(off2[:, :], rsel[:, :], -1.0, 5120.0, ALU.add, ALU.min)
                    v.drain()
                    v.tensor_scalar(offf[:, :], off2[:, :], 3839.5, None, ALU.is_ge)
                    v.tensor_scalar(scc[:, :], off2[:, :], 4607.5, None, ALU.is_ge)
                    v.tensor_scalar(esum[:, :], off2[:, :], 1919.5, None, ALU.is_ge)
                    v.scalar_tensor_tensor(gretf[:, :], idxw[:, :], 1920.0, off2[:, :], ALU.mult, ALU.add)
                    v.tensor_scalar(spp[:, :], idxw[:, :], -128.0, 5376.0, ALU.mult, ALU.add)
                    v.drain()
                    v.scalar_tensor_tensor(gretf[:, :], esum[:, :], float((NCORE - 1) * 1920), gretf[:, :],
                                           ALU.mult, ALU.add)
                    v.drain()
                    v.scalar_tensor_tensor(gretf[:, :], offf[:, :], BIG, gretf[:, :], ALU.mult, ALU.add)
                    v.scalar_tensor_tensor(off2[:, :], idxw[:, :], 768.0, off2[:, :], ALU.mult, ALU.add)
                    v.tensor_tensor(scc[:, :], scc[:, :], spp[:, :], ALU.mult)
                    v.drain()
                    s_v.inc(v.tensor_copy(gretu[:, :], gretf[:, :]))
                    v.tensor_scalar(off2[:, :], off2[:, :], -3840.0, None, ALU.add)
                    v.tensor_scalar(offf[:, :], offf[:, :], -1.0, 1.0, ALU.mult, ALU.add)
                    v.drain()
                    v.tensor_tensor(off2[:, :], off2[:, :], scc[:, :], ALU.add)
                    v.drain()
                    v.scalar_tensor_tensor(off2[:, :], offf[:, :], BIG, off2[:, :], ALU.mult, ALU.add)
                    v.drain()
                    s_v.inc(v.tensor_copy(gretu2[:, :], off2[:, :]))
                    v.tensor_scalar(spp[:, :], rsel[:, :], -1.0, None, ALU.add)
                    v.drain()
                    v.tensor_scalar(scc[:, :], spp[:, :], 1.0 / 128.0, C23, ALU.mult, ALU.add)
                    v.drain()
                    v.tensor_scalar(scc[:, :], scc[:, :], -C23, None, ALU.add)
                    v.drain()
                    v.scalar_tensor_tensor(spp[:, :], scc[:, :], -128.0, spp[:, :], ALU.mult, ALU.add)
                    v.drain()
                    v.tensor_scalar(offf[:, :], spp[:, :], 0.0, None, ALU.is_lt)
                    v.drain()
                    v.tensor_tensor(scc[:, :], scc[:, :], offf[:, :], ALU.subtract)
                    v.scalar_tensor_tensor(spp[:, :], offf[:, :], 128.0, spp[:, :], ALU.mult, ALU.add)
                    v.drain()
                    v.scalar_tensor_tensor(off2[:, :], spp[:, :], 41.0, scc[:, :], ALU.mult, ALU.add)
                    v.drain()
                    v.scalar_tensor_tensor(offf[:, :], idxw[:, :], float(RPAD), off2[:, :], ALU.mult, ALU.add)
                    v.drain()
                    v.tensor_scalar(off2[:, :], vld[:, :], -BIG, BIG, ALU.mult, ALU.add)
                    v.drain()
                    v.tensor_tensor(offf[:, :], offf[:, :], off2[:, :], ALU.add)
                    v.drain()
                    s_v.inc(v.tensor_copy(offu[:, :], offf[:, :]))

                cc_S = s_cc[call * 4 + 2]

                @block.gpsimd
                def _(g, call=call):
                    s_v.wait(g)
                    s_gd.wait(g)
                    for t in range(NT):
                        s_gd.inc(g.indirect_dma_start(
                            out=S_dram.ap(),
                            out_offset=IndirectOffsetOnAxis(offu[:, t:t + 1], 0),
                            in_=locid[:, t:t + 1], in_offset=None,
                            bounds_check=st["rb_slot"], oob_is_err=False,
                        ), 16)
                    g.wait_ge(s_gd.h, s_gd.n)
                    cc_S.inc(g.collective_compute(
                        "AllGather", ALU.bypass, replica_groups=[list(range(NCORE))],
                        ins=[S_dram.ap().opt()], outs=[S_ag.ap().opt()]))
                    cc_S.wait_at(g, 1)
                    for r in range(NCORE):
                        s_gd.inc(g.indirect_dma_start(
                            out=Ssb[:, r, :], out_offset=None,
                            in_=S_rows, in_offset=IndirectOffsetOnAxis(midxu[:, r:r + 1], 0),
                            bounds_check=st["rb_srow"], oob_is_err=False,
                        ), 16)

                @block.vector
                def _(v):
                    s_gd.wait(v)
                    for r in range(NCORE):
                        v.tensor_scalar(mrg_m[:, :], Ssb[:, r, :], -0.5, None, ALU.is_ge)
                        v.tensor_scalar(mrg_t[:, :], Ssb[:, r, :], float(r * 512), None, ALU.add)
                        v.drain()
                        if r == 0:
                            v.tensor_tensor(gacc[:, :], mrg_t[:, :], mrg_m[:, :], ALU.mult)
                        else:
                            v.tensor_tensor(mrg_t[:, :], mrg_t[:, :], mrg_m[:, :], ALU.mult)
                            v.drain()
                            v.tensor_tensor(gacc[:, :], gacc[:, :], mrg_t[:, :], ALU.add)
                        v.drain()
                    s_v.inc(v.tensor_copy(g_u[:, :], gacc[:, :]))
                    st[f"routing_done_{call}"] = s_v.n

        def emit_expert(call):
            with ExitStack() as fc:
                ef = fc.enter_context
                sfx = f"_f{call}"
                xg = ef(nc.sbuf_tensor("xg" + sfx, [128, 2, 3, D], BF16))
                xgf = ef(nc.sbuf_tensor("xgf" + sfx, [128, 2, 3, 8, 128], BF16, is_transpose_dest=True))
                h_sb = ef(nc.sbuf_tensor("h_sb" + sfx, [128, 16, CHT], BF16))
                y_sb = ef(nc.sbuf_tensor("y_sb" + sfx, [128, 3, D], BF16))
                ps_ha = [ef(nc.psum_tensor(f"ps_h{i}" + sfx, [128, CHT], F32)) for i in range(2)]
                ps_y = [[ef(nc.psum_tensor(f"ps_y{stt}{dh}" + sfx, [128, 512], F32)) for dh in range(2)]
                        for stt in range(3)]

                fill0 = s_fill.n
                tr0 = s_tr.n
                m20 = s_m2.n
                ev0 = s_v.n
                st0 = s_sst.n
                st[f"yst_{call}_base"] = st0
                routing_done = st[f"routing_done_{call}"]
                cc_xn = s_cc[call * 4 + 1]
                cc_y = s_cc[call * 4 + 3]

                ntile = NC41
                chunks = [(ch, min(3, ntile - ch * 3)) for ch in range(NCH)]

                for ch, nk in chunks:
                    par = ch % 2

                    @block.gpsimd
                    def _(g, ch=ch, nk=nk, par=par):
                        if ch == 0:
                            s_v.wait_at(g, routing_done)
                            cc_xn.wait_at(g, 8)
                        if ch >= 2:
                            s_tr.wait_at(g, st[f"xtr_{call}_{ch-2}"])
                        for k in range(nk):
                            s_gd.inc(g.indirect_dma_start(
                                out=xg[:, par, k, :], out_offset=None,
                                in_=ag_xn_out.ap(), in_offset=IndirectOffsetOnAxis(
                                    g_u[:, ch * 3 + k:ch * 3 + k + 1], 0),
                                bounds_check=st["rb_xn"], oob_is_err=False,
                            ), 16)

                    @block.sync
                    def _(sync, ch=ch, nk=nk, par=par):
                        s_gd.wait(sync)
                        if ch >= 2:
                            s_fill.wait_at(sync, fill0 + 32 * (ch - 1))
                        inst = None
                        for k in range(nk):
                            s_tr.inc(sync.dma_start(out=xgf[:, par, k, :, :], in_=xg[:, par, k, :],
                                                    transpose=True), 16)
                        st[f"xtr_{call}_{ch}"] = s_tr.n

                    if ch == 6:
                        @block.gpsimd
                        def _(g, call=call):
                            g.wait_ge(s_sst.h, st[f"yst_{call}_base"] + 5 * 16)
                            s_ccy[call * 3].inc(g.collective_compute(
                                "AllGather", ALU.bypass, replica_groups=[list(range(NCORE))],
                                ins=[y_dA.ap()[0:1920, :].opt()],
                                outs=[y_agA.ap()[0:NCORE * 1920, :].opt()]))
                    if ch == 11:
                        @block.gpsimd
                        def _(g, call=call):
                            g.wait_ge(s_sst.h, st[f"yst_{call}_base"] + 10 * 16)
                            s_ccy[call * 3].inc(g.collective_compute(
                                "AllGather", ALU.bypass, replica_groups=[list(range(NCORE))],
                                ins=[y_dA.ap()[1920:3840, :].opt()],
                                outs=[y_agA.ap()[NCORE * 1920:NCORE * 3840, :].opt()]))
                    if ch == 13:
                        @block.gpsimd
                        def _(g, call=call):
                            g.wait_ge(s_sst.h, st[f"yst_{call}_base"] + 12 * 16)
                            s_ccy[call * 3 + 1].inc(g.collective_compute(
                                "AllGather", ALU.bypass, replica_groups=[list(range(NCORE))],
                                ins=[y_dB.ap()[0:768, :].opt()],
                                outs=[y_agB.ap()[0:NCORE * 768, :].opt()]))

                    for fh in range(2):
                        fillv = []
                        abase = s_a.n

                        for ft in range(16):
                            fg = fh * 16 + ft

                            @block.tensor
                            def _(t_, ch=ch, nk=nk, par=par, fh=fh, ft=ft, fg=fg, abase=abase):
                                if fh == 0 and ft == 0:
                                    s_tr.wait(t_)
                                    if ch == 0:
                                        s_wc.wait(t_)
                                if ft >= 2:
                                    s_a.wait_at(t_, abase + ft - 1)
                                elif s_a.n > 0:
                                    s_a.wait_at(t_, s_a.n)
                                inst = None
                                for dc in range(8):
                                    inst = t_.matmul(ps_ha[ft % 2][:, 0:nk * 128],
                                                     w1_sb[:, dc, fg * 128:(fg + 1) * 128],
                                                     xgf[:, par, 0:nk, dc, :],
                                                     start=(dc == 0), stop=(dc == 7))
                                s_fill.inc(inst)
                                fillv.append(s_fill.n)

                        @block.scalar
                        def _(sc, ch=ch, nk=nk, fh=fh, fillv=fillv):
                            for ft in range(16):
                                fg = fh * 16 + ft
                                s_fill.wait_at(sc, fillv[ft])
                                s_a.inc(sc.activation(out=h_sb[:, ft, 0:nk * 128], in_=ps_ha[ft % 2][:, 0:nk * 128],
                                                      func=AF.Gelu, bias=b1_sb[:, fg:fg + 1], scale=1.0))

                        @block.tensor
                        def _(t_, ch=ch, nk=nk, fh=fh):
                            s_a.wait(t_)
                            if fh == 0:
                                s_v.wait_at(t_, ev0 + ch * 6)
                            for stt in range(3 if nk == 3 else nk):
                                for dh in range(2):
                                    inst = None
                                    for fc in range(16):
                                        inst = t_.matmul(ps_y[stt][dh][:, :],
                                                         h_sb[:, fc, stt * 128:(stt + 1) * 128],
                                                         w2_sb[:, fh * 16 + fc, dh * 512:(dh + 1) * 512],
                                                         start=(fh == 0 and fc == 0),
                                                         stop=(fh == 1 and fc == 15))
                                    if fh == 1:
                                        s_m2.inc(inst)

                    @block.vector
                    def _(v, ch=ch, nk=nk, par=par):
                        s_m2.wait(v)
                        s_sst.wait_at(v, st0 + ch * 16)
                        inst = None
                        for stt in range(nk):
                            for dh in range(2):
                                inst = v.tensor_tensor(y_sb[:, stt, dh * 512:(dh + 1) * 512],
                                                       ps_y[stt][dh][:, :],
                                                       b2b[:, dh * 512:(dh + 1) * 512], ALU.add)
                        s_v.inc(inst, 6)

                    @block.scalar
                    def _(sc, ch=ch, nk=nk):
                        s_v.wait(sc)
                        if ch < 10:
                            s_sst.inc(sc.dma_start(out=y_tilA[:, ch * 3:ch * 3 + nk, :],
                                                   in_=y_sb[:, 0:nk, :]), 16)
                        else:
                            s_sst.inc(sc.dma_start(out=y_tilB[:, (ch - 10) * 3:(ch - 10) * 3 + nk, :],
                                                   in_=y_sb[:, 0:nk, :]), 16)



                @block.gpsimd
                def _(g):
                    s_sst.wait(g)
                    s_ccy[call * 3 + 1].inc(g.collective_compute(
                        "AllGather", ALU.bypass, replica_groups=[list(range(NCORE))],
                        ins=[y_dB.ap()[768:1408, :].opt()],
                        outs=[y_agB.ap()[NCORE * 768:NCORE * 1408, :].opt()]))

        def emit_return(call):
            rc = ctx.enter_context(ExitStack())
            ygat = rc.enter_context(nc.sbuf_tensor(f"ygat{call}", [128, 2, 2, D], BF16))
            xtr = rc.enter_context(nc.sbuf_tensor(f"xtr{call}", [128, 2, 2, D], F32))
            xnwr = rc.enter_context(nc.sbuf_tensor(f"xnwr{call}", [128, 2, 2, D], F32))
            if call == 0:
                xnbr = rc.enter_context(nc.sbuf_tensor(f"xnbr{call}", [128, 2, 2, D], BF16))
            for gp in range(NT // 2):
                t0, t1 = 2 * gp, 2 * gp + 1

                @block.gpsimd
                def _(g, gp=gp, t0=t0, t1=t1, call=call):
                    if gp == 0:
                        s_ccy[call * 3].wait_at(g, 2)
                        s_ccy[call * 3 + 1].wait_at(g, 2)
                    if gp >= 2:
                        s_v.wait_at(g, st[f"resp_{call}_{gp-2}"])
                    sl = gp % 2
                    for par, tt in ((0, t0), (1, t1)):
                        s_gd.inc(g.indirect_dma_start(
                            out=ygat[:, sl, par, :], out_offset=None,
                            in_=y_agA.ap(), in_offset=IndirectOffsetOnAxis(gretu[:, tt:tt + 1], 0),
                            bounds_check=st["rb_yA"], oob_is_err=False,
                        ), 16)
                        s_gd.inc(g.indirect_dma_start(
                            out=ygat[:, sl, par, :], out_offset=None,
                            in_=y_agB.ap(), in_offset=IndirectOffsetOnAxis(gretu2[:, tt:tt + 1], 0),
                            bounds_check=st["rb_yB"], oob_is_err=False,
                        ), 16)

                @block.sync
                def _(sync, gp=gp, t0=t0, t1=t1, call=call):
                    if gp < 2:
                        s_ccy[call * 3 + 1].wait_at(sync, 2)
                    else:
                        s_v.wait_at(sync, st[f"resp_{call}_{gp-2}"])
                    sl = gp % 2
                    src = x_til if call == 0 else x2_til
                    s_ld.inc(sync.dma_start(out=xtr[:, sl, 0, :], in_=src[:, t0, :]), 16)
                    s_ld.inc(sync.dma_start(out=xtr[:, sl, 1, :], in_=src[:, t1, :]), 16)

                @block.vector
                def _(v, gp=gp, t0=t0, t1=t1, call=call):
                    s_gd.wait(v)
                    s_ld.wait(v)
                    if gp >= 2:
                        s_sst.wait_at(v, st[f"xstp_{call}_{gp-2}"])
                    sl = gp % 2
                    v.scalar_tensor_tensor(xnwr[:, sl, 0, :], ygat[:, sl, 0, :], wsc[:, t0:t0 + 1],
                                           xtr[:, sl, 0, :], ALU.mult, ALU.add)
                    inst = v.scalar_tensor_tensor(xnwr[:, sl, 1, :], ygat[:, sl, 1, :], wsc[:, t1:t1 + 1],
                                                  xtr[:, sl, 1, :], ALU.mult, ALU.add)
                    s_v.inc(inst)
                    st[f"resp_{call}_{gp}"] = s_v.n
                    if call == 0:
                        v.drain()
                        v.bn_stats(bns[:, 0, 0:6], xnwr[:, sl, 0, 0:512])
                        v.bn_stats(bns[:, 0, 6:12], xnwr[:, sl, 0, 512:1024])
                        v.bn_stats(bns[:, 1, 0:6], xnwr[:, sl, 1, 0:512])
                        v.bn_stats(bns[:, 1, 6:12], xnwr[:, sl, 1, 512:1024])
                        v.drain()
                        v.bn_aggr(mv[:, 0, :], bns[:, 0, :])
                        v.bn_aggr(mv[:, 1, :], bns[:, 1, :])
                        v.drain()
                        s_v.inc(v.tensor_scalar(rsd[:, :, :], mv[:, :, 1:2], EPS, None, ALU.add))

                if call == 0:
                    @block.scalar
                    def _(sc, gp=gp):
                        s_v.wait(sc)
                        sc.activation(out=rsd[:, :, :], in_=rsd[:, :, :], func=AF.Ln)
                        sc.drain()
                        s_a.inc(sc.activation(out=rsd[:, :, :], in_=rsd[:, :, :], func=AF.Exp, scale=-0.5))

                    @block.vector
                    def _(v, gp=gp):
                        s_a.wait(v)
                        if gp >= 2:
                            s_sst.wait_at(v, st[f"xn2p_{gp-2}"])
                        sl = gp % 2
                        v.tensor_scalar(xnbr[:, sl, 0, :], xnwr[:, sl, 0, :], mv[:, 0, 0:1],
                                        rsd[:, 0, :], ALU.subtract, ALU.mult)
                        s_v.inc(v.tensor_scalar(xnbr[:, sl, 1, :], xnwr[:, sl, 1, :], mv[:, 1, 0:1],
                                                rsd[:, 1, :], ALU.subtract, ALU.mult))

                    @block.scalar
                    def _(sc, gp=gp, t0=t0, t1=t1):
                        s_v.wait(sc)
                        sl = gp % 2
                        s_sst.inc(sc.dma_start(out=x1_til[:, t0, :], in_=xnwr[:, sl, 0, :]), 16)
                        s_sst.inc(sc.dma_start(out=x1_til[:, t1, :], in_=xnwr[:, sl, 1, :]), 16)
                        st[f"xstp_0_{gp}"] = s_sst.n
                        s_sst.inc(sc.dma_start(out=xn2_til[:, t0, :], in_=xnbr[:, sl, 0, :]), 16)
                        s_sst.inc(sc.dma_start(out=xn2_til[:, t1, :], in_=xnbr[:, sl, 1, :]), 16)
                        st[f"xn2p_{gp}"] = s_sst.n
                else:
                    @block.scalar
                    def _(sc, gp=gp, t0=t0, t1=t1):
                        s_v.wait(sc)
                        sl = gp % 2
                        s_sst.inc(sc.dma_start(out=xo_til[:, t0, :], in_=xnwr[:, sl, 0, :]), 16)
                        s_sst.inc(sc.dma_start(out=xo_til[:, t1, :], in_=xnwr[:, sl, 1, :]), 16)
                        st[f"xstp_1_{gp}"] = s_sst.n
            rc.close()

        def emit_attention():
            with ExitStack() as ac:
                ea = ac.enter_context
                qkp_sb = ea(nc.sbuf_tensor("qkp_sb", [128, 8, 2 * PD], BF16))
                xnf2 = ea(nc.sbuf_tensor("xnf2", [128, 2, 2, 8, 128], BF16, is_transpose_dest=True))
                lga = ea(nc.sbuf_tensor("lga", [128, 16, 16], F32))
                aw = ea(nc.sbuf_tensor("aw", [128, 2, 2, D], F32))

                def emit_router_mm(b, g):
                    lgb = ps_kv[1][0]

                    @block.tensor
                    def _(t_, g=g, b=b):
                        s_tr.wait_at(t_, st[f"a2trp_{b}_{g}"])
                        if g == 0:
                            s_v.wait_at(t_, st[f"kvz_{b}"])
                        prev = st.get(f"a2sh_{b}_{g-1}", st.get(f"a2sh_{b-1}_7", 0))
                        s_v.wait_at(t_, prev)
                        inst = None
                        for par in range(2):
                            for dc in range(8):
                                inst = t_.matmul(lgb[:, par * 8:(par + 1) * 8],
                                                 xnf2[:, g % 2, par, dc, :], wg_sb[:, dc, :],
                                                 start=(dc == 0), stop=(dc == 7))
                        s_t.inc(inst)
                        st[f"a2mm_{b}_{g}"] = s_t.n

                    @block.vector
                    def _(v, g=g, b=b):
                        s_t.wait_at(v, st[f"a2mm_{b}_{g}"])
                        s_v.inc(v.tensor_copy(lga[:, b * 8 + g, :], lgb[:, 0:16]))
                        st[f"a2sh_{b}_{g}"] = s_v.n
                qe_all = ea(nc.sbuf_tensor("qe_all", [128, 4, PD], BF16))
                qT = ea(nc.sbuf_tensor("qT", [128, 16, 2, 128], BF16, is_transpose_dest=True))
                ke = ea(nc.sbuf_tensor("ke", [128, 2, PD], BF16))
                kv_sb = ea(nc.sbuf_tensor("kv_sb", [128, 2, D], BF16))
                zsum = ea(nc.sbuf_tensor("zsum", [128, 2, 16], F32))
                zinv = ea(nc.sbuf_tensor("zinv", [128, 2, 16], F32))
                ps_qk = [ea(nc.psum_tensor(f"ps_qk{i}", [128, 512], F32)) for i in range(2)]
                ps_kv = [[ea(nc.psum_tensor(f"ps_kv{eh}{dh}", [128, 512], F32)) for dh in range(2)]
                         for eh in range(2)]
                ps_nm = [ea(nc.psum_tensor(f"ps_nm{i}", [128, 512], F32)) for i in range(2)]

                @block.gpsimd
                def _(g):
                    s_sst.wait_at(g, st["xn2p_15"])
                    s_wc.inc(g.dma_start(out=qkp_sb[:, :, :],
                                         in_=qk_in.rearrange("(dc p) e -> p dc e", p=128)), 16)

                qkb = [ps_qk, ps_nm]

                def emit_kv(t_, gp, b):
                    s_a.wait_at(t_, st[f"expkp_{b}_{gp}"])
                    if b > 0 and gp == 0:
                        s_v.wait_at(t_, st[f"aresp_{b-1}_7"])
                        s_v.wait_at(t_, st[f"a2sh_{b-1}_7"])
                    sl = gp % 2
                    inst = None
                    for par in range(2):
                        for eh in range(2):
                            for dh in range(2):
                                inst = t_.matmul(ps_kv[eh][dh][:, :],
                                                 ke[:, par, eh * 128:(eh + 1) * 128],
                                                 xnb[:, sl, par, dh * 512:(dh + 1) * 512],
                                                 start=(gp == 0 and par == 0),
                                                 stop=(gp == 7 and par == 1))
                    s_t.inc(inst)
                    st[f"kvp_{b}_{gp}"] = s_t.n

                for b in (ATT_BATCHES if ATT_BATCHES is not None else range(B)):
                    for gp in range(8):
                        i0, i1 = 2 * gp, 2 * gp + 1
                        t0, t1 = b * 16 + i0, b * 16 + i1
                        sl = gp % 2

                        @block.sync
                        def _(sync, gp=gp, t0=t0, t1=t1, b=b, sl=sl):
                            if b == 0 and gp == 0:
                                s_sst.wait(sync)
                            if b > 0 and gp == 0:
                                s_t.wait_at(sync, st[f"kvp_{b-1}_7"])
                                s_sst.wait_at(sync, st[f"axp_{b-1}_7"])
                                s_tr.wait_at(sync, st[f"a2trp_{b-1}_7"])
                            if gp >= 2:
                                s_t.wait_at(sync, st[f"kvp_{b}_{gp-2}"])
                            s_ld.inc(sync.dma_start(out=xnb[:, sl, 0, :], in_=xn2_til[:, t0, :]), 16)
                            s_ld.inc(sync.dma_start(out=xnb[:, sl, 1, :], in_=xn2_til[:, t1, :]), 16)
                            s_ld.wait(sync)
                            s_tr.inc(sync.dma_start(out=xnf[:, sl, 0, :, :], in_=xnb[:, sl, 0, :], transpose=True), 16)
                            s_tr.inc(sync.dma_start(out=xnf[:, sl, 1, :, :], in_=xnb[:, sl, 1, :], transpose=True), 16)
                            st[f"atr_{b}_{gp}"] = s_tr.n
                            if gp >= 2:
                                qgp = gp - 2
                                j0, j1 = 2 * qgp, 2 * qgp + 1
                                s_a.wait_at(sync, st[f"expkp_{b}_{qgp}"])
                                if b > 0 and qgp <= 1:
                                    s_v.wait_at(sync, st[f"aresp_{b-1}_7"])
                                s_tr.inc(sync.dma_start(out=qT[:, j0, :, :], in_=qe_all[:, j0 % 4, :], transpose=True), 16)
                                s_tr.inc(sync.dma_start(out=qT[:, j1, :, :], in_=qe_all[:, j1 % 4, :], transpose=True), 16)
                                st[f"qtr_{b}_{qgp}"] = s_tr.n

                        @block.tensor
                        def _(t_, gp=gp, b=b, sl=sl):
                            s_tr.wait_at(t_, st[f"atr_{b}_{gp}"])
                            if gp == 0:
                                s_wc.wait(t_)
                            prev = st.get(f"expkp_{b}_{gp-2}", 0)
                            if prev:
                                s_a.wait_at(t_, prev)
                            if b > 0 and gp == 1:
                                s_v.wait_at(t_, st[f"aresp_{b-1}_7"])
                            qk = qkb[sl]
                            inst = None
                            for par in range(2):
                                for dc in range(8):
                                    inst = t_.matmul(qk[par][:, :], xnf[:, sl, par, dc, :], qkp_sb[:, dc, :],
                                                     start=(dc == 0), stop=(dc == 7))
                            s_t.inc(inst)
                            st[f"qkp_{b}_{gp}"] = s_t.n

                        if gp >= 1:
                            @block.tensor
                            def _(t_, gp=gp - 1, b=b):
                                emit_kv(t_, gp, b)

                        @block.scalar
                        def _(sc, gp=gp, i0=i0, i1=i1, b=b, sl=sl):
                            s_t.wait_at(sc, st[f"qkp_{b}_{gp}"])
                            prevk = st.get(f"kvp_{b}_{gp-1}", 0)
                            if prevk:
                                s_t.wait_at(sc, prevk)
                            prev = st.get(f"qtr_{b}_{gp-2}", st.get(f"qtr_{b-1}_{gp+6}", 0))
                            if prev:
                                s_tr.wait_at(sc, prev)
                            qk = qkb[sl]
                            sc.activation(out=qe_all[:, i0 % 4, :], in_=qk[0][:, 0:PD], func=AF.Exp,
                                          accum_out=zsum[:, b, i0:i0 + 1])
                            sc.activation(out=ke[:, 0, :], in_=qk[0][:, PD:2 * PD], func=AF.Exp)
                            sc.activation(out=qe_all[:, i1 % 4, :], in_=qk[1][:, 0:PD], func=AF.Exp,
                                          accum_out=zsum[:, b, i1:i1 + 1])
                            s_a.inc(sc.activation(out=ke[:, 1, :], in_=qk[1][:, PD:2 * PD], func=AF.Exp))
                            st[f"expkp_{b}_{gp}"] = s_a.n

                    @block.sync
                    def _(sync, b=b):
                        for qgp in (6, 7):
                            j0, j1 = 2 * qgp, 2 * qgp + 1
                            s_a.wait_at(sync, st[f"expkp_{b}_{qgp}"])
                            s_tr.inc(sync.dma_start(out=qT[:, j0, :, :], in_=qe_all[:, j0 % 4, :], transpose=True), 16)
                            s_tr.inc(sync.dma_start(out=qT[:, j1, :, :], in_=qe_all[:, j1 % 4, :], transpose=True), 16)
                            st[f"qtr_{b}_{qgp}"] = s_tr.n

                    @block.tensor
                    def _(t_, b=b):
                        emit_kv(t_, 7, b)

                    @block.vector
                    def _(v, b=b):
                        s_t.wait(v)
                        v.drain()
                        for eh in range(2):
                            for dh in range(2):
                                v.tensor_copy(kv_sb[:, eh, dh * 512:(dh + 1) * 512], ps_kv[eh][dh][:, :])
                        v.tensor_scalar(zinv[:, b, :], zsum[:, b, :], 1e-8, None, ALU.add)
                        v.drain()
                        s_v.inc(v.reciprocal(zinv[:, b, :], zinv[:, b, :]))
                        st[f"kvz_{b}"] = s_v.n

                    if DEBUG:
                        @block.scalar
                        def _(sc, b=b):
                            s_v.wait(sc)
                            s_sst.inc(sc.dma_start(out=dbg_z[b, :, :], in_=zinv[:, b, :]), 16)
                            s_sst.inc(sc.dma_start(out=dbg_kv[b, :, :], in_=kv_sb[:, :, :].rearrange("p a b -> p (a b)")), 16)
                            s_sst.inc(sc.dma_start(out=dbg_qe[b, :, :], in_=qe_all[:, :, :].rearrange("p a b -> p (a b)")), 16)

                    nmA = ps_nm
                    nmB = [ps_kv[0][0], ps_kv[0][1]]
                    for gp in range(8):
                        i0, i1 = 2 * gp, 2 * gp + 1
                        t0, t1 = b * 16 + i0, b * 16 + i1

                        if gp == 0:
                            @block.sync
                            def _(sync, b=b):
                                if b > 0:
                                    s_v.wait_at(sync, st[f"aresp_{b-1}_7"])
                                s_ld.inc(sync.dma_start(out=xt[:, 0, :], in_=x1_til[:, b * 16, :]), 16)
                                s_ld.inc(sync.dma_start(out=xt[:, 1, :], in_=x1_til[:, b * 16 + 1, :]), 16)

                        @block.tensor
                        def _(t_, gp=gp, i0=i0, i1=i1, b=b):
                            if gp == 0:
                                s_v.wait_at(t_, st[f"kvz_{b}"])
                                if b > 0:
                                    s_v.wait_at(t_, st[f"aresp_{b-1}_7"])
                            if gp >= 1:
                                s_v.wait_at(t_, st[f"aresp_{b}_{gp-1}"])
                            inst = None
                            for nmp, ii in ((nmA, i0), (nmB, i1)):
                                for dh in range(2):
                                    for eh in range(2):
                                        inst = t_.matmul(nmp[dh][:, :], qT[:, ii, eh, :],
                                                         kv_sb[:, eh, dh * 512:(dh + 1) * 512],
                                                         start=(eh == 0), stop=(eh == 1))
                            s_t.inc(inst)

                        @block.vector
                        def _(v, gp=gp, i0=i0, i1=i1, b=b):
                            s_t.wait(v)
                            s_ld.wait(v)
                            sl = gp % 2
                            prev = st.get(f"axp_{b}_{gp-2}", st.get(f"axp_{b-1}_{gp+6}", 0))
                            if prev:
                                s_sst.wait_at(v, prev)
                            for par, (nmp, ii) in enumerate(((nmA, i0), (nmB, i1))):
                                for dh in range(2):
                                    inst = v.scalar_tensor_tensor(aw[:, sl, par, dh * 512:(dh + 1) * 512],
                                                                  nmp[dh][:, :], zinv[:, b, ii:ii + 1],
                                                                  xt[:, par, dh * 512:(dh + 1) * 512],
                                                                  ALU.mult, ALU.add)
                            s_v.inc(inst)
                            st[f"aresp_{b}_{gp}"] = s_v.n
                            v.drain()
                            v.bn_stats(bns[:, 0, 0:6], aw[:, sl, 0, 0:512])
                            v.bn_stats(bns[:, 0, 6:12], aw[:, sl, 0, 512:1024])
                            v.bn_stats(bns[:, 1, 0:6], aw[:, sl, 1, 0:512])
                            v.bn_stats(bns[:, 1, 6:12], aw[:, sl, 1, 512:1024])
                            v.drain()
                            v.bn_aggr(mv[:, 0, :], bns[:, 0, :])
                            v.bn_aggr(mv[:, 1, :], bns[:, 1, :])
                            v.drain()
                            s_v.inc(v.tensor_scalar(rsd[:, :, :], mv[:, :, 1:2], EPS, None, ALU.add))

                        if gp < 7:
                            @block.sync
                            def _(sync, gp=gp, t0=t0, t1=t1, b=b):
                                s_v.wait_at(sync, st[f"aresp_{b}_{gp}"])
                                s_ld.inc(sync.dma_start(out=xt[:, 0, :], in_=x1_til[:, t0 + 2, :]), 16)
                                s_ld.inc(sync.dma_start(out=xt[:, 1, :], in_=x1_til[:, t1 + 2, :]), 16)

                        @block.scalar
                        def _(sc, gp=gp):
                            s_v.wait(sc)
                            sc.activation(out=rsd[:, :, :], in_=rsd[:, :, :], func=AF.Ln)
                            sc.drain()
                            s_a.inc(sc.activation(out=rsd[:, :, :], in_=rsd[:, :, :], func=AF.Exp, scale=-0.5))

                        @block.vector
                        def _(v, gp=gp, b=b):
                            s_a.wait(v)
                            sl = gp % 2
                            if gp >= 1:
                                s_sst.wait_at(v, st[f"axp_{b}_{gp-1}"])
                                s_tr.wait_at(v, st[f"a2trp_{b}_{gp-1}"])
                            v.tensor_scalar(xnb[:, 0, 0, :], aw[:, sl, 0, :], mv[:, 0, 0:1],
                                            rsd[:, 0, :], ALU.subtract, ALU.mult)
                            s_v.inc(v.tensor_scalar(xnb[:, 0, 1, :], aw[:, sl, 1, :], mv[:, 1, 0:1],
                                                    rsd[:, 1, :], ALU.subtract, ALU.mult))
                            st[f"aply_{b}_{gp}"] = s_v.n

                        @block.scalar
                        def _(sc, gp=gp, t0=t0, t1=t1, b=b):
                            s_v.wait(sc)
                            sl = gp % 2
                            s_sst.inc(sc.dma_start(out=x2_til[:, t0, :], in_=aw[:, sl, 0, :]), 16)
                            s_sst.inc(sc.dma_start(out=agxn_til[:, t0, :], in_=xnb[:, 0, 0, :]), 16)
                            s_sst.inc(sc.dma_start(out=x2_til[:, t1, :], in_=aw[:, sl, 1, :]), 16)
                            s_sst.inc(sc.dma_start(out=agxn_til[:, t1, :], in_=xnb[:, 0, 1, :]), 16)
                            st[f"axp_{b}_{gp}"] = s_sst.n

                        lgb = ps_kv[1][0]

                        @block.sync
                        def _(sync, gp=gp, b=b):
                            s_v.wait_at(sync, st[f"aply_{b}_{gp}"])
                            prev = st.get(f"a2mm_{b}_{gp-2}", st.get(f"a2mm_{b-1}_{gp+6}", 0))
                            s_t.wait_at(sync, prev)
                            s_tr.inc(sync.dma_start(out=xnf2[:, gp % 2, 0, :, :], in_=xnb[:, 0, 0, :], transpose=True), 16)
                            s_tr.inc(sync.dma_start(out=xnf2[:, gp % 2, 1, :, :], in_=xnb[:, 0, 1, :], transpose=True), 16)
                            st[f"a2trp_{b}_{gp}"] = s_tr.n

                        if gp >= 2:
                            emit_router_mm(b, gp - 2)

                    emit_router_mm(b, 6)
                    emit_router_mm(b, 7)
                    st["attn_prev_b"] = s_t.n

                for q in range(16):
                    t0, t1 = 2 * q, 2 * q + 1

                    @block.vector
                    def _(v, q=q, t0=t0, t1=t1):
                        if q == 0:
                            s_v.wait(v)
                        if q >= 1:
                            s_a.wait_at(v, st[f"p2exp_{q-1}"])
                            s_t.wait_at(v, st[f"c2mm_{q-1}"])
                        inst = None
                        for par, tt in ((0, t0), (1, t1)):
                            v.max(mx8[:, :], lga[:, q, par * 8:(par + 1) * 8])
                            v.drain()
                            v.max_index(ix8[:, :], mx8[:, :], lga[:, q, par * 8:(par + 1) * 8])
                            v.drain()
                            v.tensor_copy(idxf[:, tt:tt + 1], ix8[:, 0:1])
                            v.tensor_copy(idc[:, :], ix8[:, 0:1])
                            v.drain()
                            v.tensor_scalar(ohtm[:, par, :], iotaef[:, :], idc[:, :], None, ALU.is_equal)
                            inst = v.tensor_scalar(sh8[:, par, :], lga[:, q, par * 8:(par + 1) * 8],
                                                   mx8[:, 0:1], None, ALU.subtract)
                            v.drain()
                        s_v.inc(inst)
                        st[f"p2sh_{q}"] = s_v.n

                    @block.scalar
                    def _(sc, q=q, t0=t0, t1=t1):
                        s_v.wait_at(sc, st[f"p2sh_{q}"])
                        sc.activation(out=she[:, 0, :], in_=sh8[:, 0, :], func=AF.Exp,
                                      accum_out=esum[:, t0:t0 + 1])
                        s_a.inc(sc.activation(out=she[:, 1, :], in_=sh8[:, 1, :], func=AF.Exp,
                                              accum_out=esum[:, t1:t1 + 1]))
                        st[f"p2exp_{q}"] = s_a.n

                    @block.tensor
                    def _(t_, q=q):
                        s_v.wait_at(t_, st[f"p2sh_{q}"])
                        inst = None
                        for par in range(2):
                            inst = t_.matmul(ps_nm[0][0:E, 0:1], ohtm[:, par, :], ones128b[:, :],
                                             start=(q == 0 and par == 0),
                                             stop=(q == 15 and par == 1))
                        s_t.inc(inst)
                        st[f"c2mm_{q}"] = s_t.n

                @block.vector
                def _(v):
                    s_a.wait(v)
                    v.reciprocal(comb[:, :], esum[:, :])
                    v.drain()
                    s_v.inc(v.tensor_copy(idxw[:, :], idxf[:, :]))

                @block.vector
                def _(v):
                    s_t.wait(v)
                    s_v.inc(v.tensor_copy(cnt8[:, :], ps_nm[0][0:E, 0:1]))
                    st["cntev_1"] = s_v.n

                @block.gpsimd
                def _(g):
                    for k in range(8):
                        g.wait_ge(s_sst.h, st[f"axp_{k // 4}_{2 * (k % 4) + 1}"])
                        s_cc[5].inc(g.collective_compute(
                            "AllGather", ALU.bypass, replica_groups=[list(range(NCORE))],
                            ins=[ag_xn_in.ap()[512 * k:512 * (k + 1), :].opt()],
                            outs=[ag_xn_out.ap()[NCORE * 512 * k:NCORE * 512 * (k + 1), :].opt()]))
                    g.wait_ge(s_v.h, st["cntev_1"])
                    s_gd.inc(g.dma_start(out=cnt_dram.ap(), in_=cnt8[:, :]), 16)
                    s_gd.wait(g)
                    s_cc[4].inc(g.collective_compute(
                        "AllGather", ALU.bypass, replica_groups=[list(range(NCORE))],
                        ins=[cnt_dram.ap().opt()], outs=[cnt_ag.ap().opt()]))

        emit_routing(0)
        emit_expert(0)
        emit_return(0)
        emit_attention()

        @block.gpsimd
        def _(g):
            s_cc[2].wait_at(g, 1)
            s_gd.inc(g.dma_start(out=S_dram.ap().rearrange("(a b) one -> a (b one)", b=128),
                                 in_=S_init.ap().rearrange("(a b) one -> a (b one)", b=128)), 16)

        emit_routing(1)
        emit_expert(1)
        emit_return(1)

        @block.sync
        def _(sync):
            s_sst.wait(sync)
            s_gd.wait(sync)
            if DEBUG:
                s_ld.inc(sync.dma_start(out=dbg_x1[:, :], in_=x1_dram.ap()[:, :]), 16)
                s_ld.inc(sync.dma_start(out=dbg_x2[:, :], in_=x2_dram.ap()[:, :]), 16)
                s_ld.inc(sync.dma_start(out=dbg_xn2[:, :], in_=xn2_dram.ap()[:, :]), 16)
                s_ld.inc(sync.dma_start(out=dbg_xn3[:, :], in_=ag_xn_in.ap()[:, :]), 16)
                s_ld.inc(sync.dma_start(out=dbg_y[:, :], in_=y_dram.ap()[:, :]), 16)
                s_ld.inc(sync.dma_start(out=dbg_r[:, 0:NT], in_=rsel[:, :]), 16)
                s_ld.inc(sync.dma_start(out=dbg_r[:, NT:2 * NT], in_=wsc[:, :]), 16)
                s_ld.inc(sync.dma_start(out=dbg_r[:, 2 * NT:3 * NT], in_=gretf[:, :]), 16)
                s_ld.inc(sync.dma_start(out=dbg_r[:, 3 * NT:4 * NT], in_=comb[:, :]), 16)
                s_ld.inc(sync.dma_start(out=dbg_g[:, :], in_=g_u[:, :]), 16)
                s_ld.wait(sync)

    return nc


_CACHED = {}


def kernel(L0_attn_x, L0_attn_q_proj, L0_attn_k_proj, L0_ffn_wg, L0_ffn_W1,
           L0_ffn_b1, L0_ffn_W2, L0_ffn_b2):
    x = np.ascontiguousarray(np.asarray(L0_attn_x, dtype=np.float32))
    qk = np.ascontiguousarray(np.concatenate([np.asarray(L0_attn_q_proj, np.float32),
                                              np.asarray(L0_attn_k_proj, np.float32)], axis=1))
    wg = np.ascontiguousarray(np.asarray(L0_ffn_wg, np.float32))
    W1 = np.asarray(L0_ffn_W1, np.float32)
    b1 = np.asarray(L0_ffn_b1, np.float32)
    W2 = np.asarray(L0_ffn_W2, np.float32)
    b2 = np.asarray(L0_ffn_b2, np.float32)
    ident = np.eye(128, dtype=ml_dtypes.bfloat16)

    if "nc" not in _CACHED:
        _CACHED["nc"] = build()
    nc = _CACHED["nc"]

    in_maps = []
    for c in range(NCORE):
        pm = (np.arange(E) < c).astype(np.float32).reshape(E, 1)
        in_maps.append({
            "x": np.ascontiguousarray(x[2 * c:2 * c + 2].reshape(T4, D)),
            "qk": qk,
            "wg": wg,
            "w1": np.ascontiguousarray(W1[c]),
            "b1": np.ascontiguousarray(b1[c].reshape(32, 128).T),
            "w2": np.ascontiguousarray(W2[c]),
            "b2": np.ascontiguousarray(b2[c].reshape(1, D)),
            "premask": pm,
            "myc": np.full((128, 1), float(c), np.float32),
            "ident": ident,
        })
    res = run_bass_kernel_spmd(nc, in_maps, core_ids=list(range(NCORE)))
    out = np.concatenate([r["xo"].reshape(2, TSEQ, D) for r in res.results], axis=0)
    return out


# version 49
# speedup vs baseline: 1.0746x; 1.0230x over previous
import numpy as np
import ml_dtypes
import concourse.bass as bass
import concourse.mybir as mybir
from concourse.bass import IndirectOffsetOnAxis
from concourse.bass_utils import run_bass_kernel_spmd
from contextlib import ExitStack

F32 = mybir.dt.float32
BF16 = mybir.dt.bfloat16
U32 = mybir.dt.uint32
I32 = mybir.dt.int32
AF = mybir.ActivationFunctionType
ALU = mybir.AluOpType
AX = mybir.AxisListType

NCORE = 8
B = 2
TSEQ = 2048
T4 = B * TSEQ
NT = 32
D = 1024
E = 8
FF = 4096
PD = 256
CAP = 5121
NC41 = 41
RPAD = NC41 * 128
CHT = 384
NCH = 14
BIG = float(2 ** 30)
C23 = 12582912.0
EPS = 1e-5
DEBUG = False
ATT_BATCHES = None


class Sem:
    def __init__(self, ctx, nc, name):
        self.h = ctx.enter_context(nc.semaphore(name))
        self.n = 0

    def inc(self, inst, k=1):
        inst.then_inc(self.h, k)
        self.n += k

    def wait(self, eng):
        eng.wait_ge(self.h, self.n)

    def wait_at(self, eng, v):
        if v > 0:
            eng.wait_ge(self.h, v)


def build():
    nc = bass.Bass("TRN2", target_bir_lowering=False)
    x_in = nc.dram_tensor("x", [T4, D], F32, kind="ExternalInput").ap()
    qk_in = nc.dram_tensor("qk", [D, 2 * PD], F32, kind="ExternalInput").ap()
    wg_in = nc.dram_tensor("wg", [D, E], F32, kind="ExternalInput").ap()
    w1_in = nc.dram_tensor("w1", [D, FF], F32, kind="ExternalInput").ap()
    b1_in = nc.dram_tensor("b1", [128, 32], F32, kind="ExternalInput").ap()
    w2_in = nc.dram_tensor("w2", [FF, D], F32, kind="ExternalInput").ap()
    b2_in = nc.dram_tensor("b2", [1, D], F32, kind="ExternalInput").ap()
    premask_in = nc.dram_tensor("premask", [E, 1], F32, kind="ExternalInput").ap()
    myc_in = nc.dram_tensor("myc", [128, 1], F32, kind="ExternalInput").ap()
    ident_in = nc.dram_tensor("ident", [128, 128], BF16, kind="ExternalInput").ap()
    xo = nc.dram_tensor("xo", [T4, D], F32, kind="ExternalOutput").ap()
    if DEBUG:
        dbg_x1 = nc.dram_tensor("dbg_x1", [T4, D], F32, kind="ExternalOutput").ap()
        dbg_x2 = nc.dram_tensor("dbg_x2", [T4, D], F32, kind="ExternalOutput").ap()
        dbg_xn2 = nc.dram_tensor("dbg_xn2", [T4, D], BF16, kind="ExternalOutput").ap()
        dbg_xn3 = nc.dram_tensor("dbg_xn3", [T4, D], BF16, kind="ExternalOutput").ap()
        dbg_y = nc.dram_tensor("dbg_y", [RPAD, D], BF16, kind="ExternalOutput").ap()
        dbg_r = nc.dram_tensor("dbg_r", [128, 4 * NT], F32, kind="ExternalOutput").ap()
        dbg_g = nc.dram_tensor("dbg_g", [128, NC41], U32, kind="ExternalOutput").ap()
        dbg_z = nc.dram_tensor("dbg_z", [2, 128, 16], F32, kind="ExternalOutput").ap()
        dbg_kv = nc.dram_tensor("dbg_kv", [2, 128, 2048], BF16, kind="ExternalOutput").ap()
        dbg_qe = nc.dram_tensor("dbg_qe", [2, 128, 16 * PD], BF16, kind="ExternalOutput").ap()
        dbg_nm = nc.dram_tensor("dbg_nm", [2, 128, 1024], F32, kind="ExternalOutput").ap()
        dbg_t0 = nc.dram_tensor("dbg_t0", [3, 128, 1024], BF16, kind="ExternalOutput").ap()

    ag_xn_in = nc.dram_tensor("ag_xn_in", [T4, D], BF16)
    ag_xn_out = nc.dram_tensor("ag_xn_out", [NCORE * T4, D], BF16, addr_space="Shared")
    cnt_dram = nc.dram_tensor("cnt_dram", [1, E], F32)
    cnt_ag = nc.dram_tensor("cnt_ag", [NCORE, E], F32, addr_space="Shared")
    S_dram = nc.dram_tensor("S_dram", [E * RPAD, 1], I32)
    S_init = nc.dram_tensor("S_init", [E * RPAD, 1], I32)
    S_ag = nc.dram_tensor("S_ag", [NCORE * E * RPAD, 1], I32, addr_space="Shared")
    y_d = nc.dram_tensor("y_d", [RPAD, D], BF16)
    y_ag = nc.dram_tensor("y_ag", [NCORE * RPAD, D], BF16, addr_space="Shared")
    x1_dram = nc.dram_tensor("x1_dram", [T4, D], F32)
    x2_dram = nc.dram_tensor("x2_dram", [T4, D], F32)
    xn2_dram = nc.dram_tensor("xn2_dram", [T4, D], BF16)

    x_til = x_in.rearrange("(t p) d -> p t d", p=128)
    x1_til = x1_dram.ap().rearrange("(t p) d -> p t d", p=128)
    x2_til = x2_dram.ap().rearrange("(t p) d -> p t d", p=128)
    xo_til = xo.rearrange("(t p) d -> p t d", p=128)
    agxn_til = ag_xn_in.ap().rearrange("(t p) d -> p t d", p=128)
    xn2_til = xn2_dram.ap().rearrange("(t p) d -> p t d", p=128)
    y_til = y_d.ap().rearrange("(c p) d -> p c d", p=128)
    S_rows = S_ag.ap().rearrange("(row c) one -> row (c one)", c=NC41)

    with ExitStack() as ctx:
        en = ctx.enter_context
        w1_sb = en(nc.sbuf_tensor("w1_sb", [128, 8, FF], BF16))
        w2_sb = en(nc.sbuf_tensor("w2_sb", [128, 32, D], BF16))
        wg_sb = en(nc.sbuf_tensor("wg_sb", [128, 8, E], BF16))
        b1_sb = en(nc.sbuf_tensor("b1_sb", [128, 32], F32))
        b2b = en(nc.sbuf_tensor("b2b", [128, D], F32))
        idn = en(nc.sbuf_tensor("idn", [128, 128], BF16))
        ones18 = en(nc.sbuf_tensor("ones18", [1, E], BF16))
        ones128b = en(nc.sbuf_tensor("ones128b", [128, 1], BF16))
        ohtm = en(nc.sbuf_tensor("ohtm", [128, 2, E], BF16))
        idc = en(nc.sbuf_tensor("idc", [128, 1], F32))
        ones8c = en(nc.sbuf_tensor("ones8c", [E, 1], F32))
        iota8c = en(nc.sbuf_tensor("iota8c", [E, 1], F32))
        iotaef = en(nc.sbuf_tensor("iotaef", [128, E], I32))
        locid = en(nc.sbuf_tensor("locid", [128, NT], I32))
        iota128 = en(nc.sbuf_tensor("iota128", [128, 1], I32))
        premask = en(nc.sbuf_tensor("premask_sb", [E, 1], F32))
        myc = en(nc.sbuf_tensor("myc_sb", [128, 1], F32))
        midxu = en(nc.sbuf_tensor("midxu", [128, E], U32))
        mbase = en(nc.sbuf_tensor("mbase", [128, 1], F32))
        xt = en(nc.sbuf_tensor("xt", [128, 2, D], F32))
        xnb = en(nc.sbuf_tensor("xnb", [128, 2, 2, D], BF16))
        xnf = en(nc.sbuf_tensor("xnf", [128, 2, 2, 8, 128], BF16, is_transpose_dest=True))
        bns = en(nc.sbuf_tensor("bns", [128, 2, 12], F32))
        mv = en(nc.sbuf_tensor("mv", [128, 2, 2], F32))
        rsd = en(nc.sbuf_tensor("rsd", [128, 2, 1], F32))
        nbi = en(nc.sbuf_tensor("nbi", [128, 2, 1], F32))
        g_u = en(nc.sbuf_tensor("g_u", [128, NC41], U32))
        gretu = en(nc.sbuf_tensor("gretu", [128, NT], U32))
        gretu2 = en(nc.sbuf_tensor("gretu2", [128, NT], U32))
        wsc = en(nc.sbuf_tensor("wsc", [128, NT], F32))
        cnt8 = en(nc.sbuf_tensor("cnt8", [E, 1], F32))
        cntall = en(nc.sbuf_tensor("cntall", [E, E], F32))
        base8 = en(nc.sbuf_tensor("base8", [E, 1], F32))
        mx8 = en(nc.sbuf_tensor("mx8", [128, 8], F32))
        ix8 = en(nc.sbuf_tensor("ix8", [128, 8], U32))
        idxf = en(nc.sbuf_tensor("idxf", [128, NT], BF16))
        idxw = en(nc.sbuf_tensor("idxw", [128, NT], F32))
        sh8 = en(nc.sbuf_tensor("sh8", [128, 2, E], F32))
        she = en(nc.sbuf_tensor("she", [128, 2, E], F32))
        esum = en(nc.sbuf_tensor("esum", [128, NT], F32))
        comb = en(nc.sbuf_tensor("comb", [128, NT], F32))
        rsel = en(nc.sbuf_tensor("rsel", [128, NT], F32))
        vld = en(nc.sbuf_tensor("vld", [128, NT], F32))
        scc = en(nc.sbuf_tensor("scc", [128, NT], F32))
        spp = en(nc.sbuf_tensor("spp", [128, NT], F32))
        offf = en(nc.sbuf_tensor("offf", [128, NT], F32))
        off2 = en(nc.sbuf_tensor("off2", [128, NT], F32))
        offu = en(nc.sbuf_tensor("offu", [128, NT], U32))
        gretf = en(nc.sbuf_tensor("gretf", [128, NT], F32))

        s_ld = Sem(ctx, nc, "s_ld")
        s_ld0 = Sem(ctx, nc, "s_ld0")
        s_wc = Sem(ctx, nc, "s_wc")
        s_wg = Sem(ctx, nc, "s_wg")
        s_tr = Sem(ctx, nc, "s_tr")
        s_str = Sem(ctx, nc, "s_str")
        s_sst = Sem(ctx, nc, "s_sst")
        s_gd = Sem(ctx, nc, "s_gd")
        s_gc = Sem(ctx, nc, "s_gc")
        s_v = Sem(ctx, nc, "s_v")
        s_a = Sem(ctx, nc, "s_a")
        s_t = Sem(ctx, nc, "s_t")
        s_fill = Sem(ctx, nc, "s_fill")
        s_m2 = Sem(ctx, nc, "s_m2")
        s_cc = [Sem(ctx, nc, f"s_cc{i}") for i in range(8)]
        s_ccy = [Sem(ctx, nc, f"s_ccy{i}") for i in range(6)]

        block = en(nc.Block())
        st = {"xnst": 0}

        c0 = ctx.enter_context(ExitStack())
        b2r = c0.enter_context(nc.sbuf_tensor("b2r", [1, D], F32))
        neg1 = c0.enter_context(nc.sbuf_tensor("neg1", [128, E * NC41], I32))
        ones1 = c0.enter_context(nc.sbuf_tensor("ones1", [1, 128], F32))

        @block.sync
        def _(sync):
            s_ld0.inc(sync.dma_start(out=b1_sb[:, :], in_=b1_in[:, :]), 16)
            s_ld0.inc(sync.dma_start(out=b2r[:, :], in_=b2_in[:, :]), 16)
            s_ld0.inc(sync.dma_start(out=idn[:, :], in_=ident_in[:, :]), 16)
            s_ld0.inc(sync.dma_start(out=premask[:, :], in_=premask_in[:, :]), 16)
            s_ld0.inc(sync.dma_start(out=myc[:, :], in_=myc_in[:, :]), 16)

        @block.gpsimd
        def _(g):
            g.iota(iotaef[:, :], pattern=[[1, E]], base=0, channel_multiplier=0)
            g.iota(iota8c[:, :], pattern=[[1, 1]], base=0, channel_multiplier=1,
                   allow_small_or_imprecise_dtypes=True)
            g.iota(locid[:, :], pattern=[[NCORE * 512, NT // 4], [128, 4]], base=0,
                   channel_multiplier=1)
            g.iota(iota128[:, :], pattern=[[1, 1]], base=0, channel_multiplier=1)
            g.memset(neg1[:, :], -1)
            g.memset(ones1[:, :], 1.0)
            g.memset(ones18[:, :], 1.0)
            g.memset(ones8c[:, :], 1.0)
            g.memset(ones128b[:, :], 1.0)
            st["rb_slot"] = g.to_reg(E * RPAD - 1)
            st["rb_srow"] = g.to_reg(NCORE * E * 128 - 1)
            st["rb_xn"] = g.to_reg(NCORE * T4 - 1)
            st["rb_y"] = g.to_reg(NCORE * RPAD - 1)
            s_gc.inc(g.drain())
            s_wg.inc(g.dma_start(out=wg_sb[:, :, :], in_=wg_in.rearrange("(dc p) e -> p dc e", p=128)), 16)
            s_wc.inc(g.dma_start(out=w1_sb[:, :, :], in_=w1_in.rearrange("(dc p) f -> p dc f", p=128)), 16)
            s_wc.inc(g.dma_start(out=w2_sb[:, :, :], in_=w2_in.rearrange("(fc p) d -> p fc d", p=128)), 16)
            s_gd.inc(g.dma_start(out=S_dram.ap().rearrange("(e p c) one -> p e (c one)", e=E, p=128),
                                 in_=neg1[:, :].rearrange("p (e c) -> p e c", e=E)), 16)
            s_gd.inc(g.dma_start(out=S_init.ap().rearrange("(e p c) one -> p e (c one)", e=E, p=128),
                                 in_=neg1[:, :].rearrange("p (e c) -> p e c", e=E)), 16)

        @block.vector
        def _(v):
            s_ld0.wait(v)
            s_gc.wait_at(v, 1)
            v.scalar_tensor_tensor(mbase[:, :], myc[:, :], 128.0, iota128[:, :], ALU.mult, ALU.add)
            v.drain()
            for r in range(NCORE):
                v.tensor_scalar(offf[:, r:r + 1], mbase[:, :], float(r * 1024), None, ALU.add)
            v.drain()
            s_v.inc(v.tensor_copy(midxu[:, :], offf[:, 0:E]))

        ps_tmp = c0.enter_context(nc.psum_tensor("ps_b2", [128, 512], F32))
        for dh in range(2):
            @block.tensor
            def _(t_, dh=dh):
                if dh == 0:
                    s_ld0.wait(t_)
                    s_v.wait(t_)
                else:
                    s_v.wait(t_)
                s_t.inc(t_.matmul(ps_tmp[:, :], ones1[:, :], b2r[:, dh * 512:(dh + 1) * 512],
                                  start=True, stop=True))

            @block.vector
            def _(v, dh=dh):
                s_t.wait(v)
                s_v.inc(v.tensor_copy(b2b[:, dh * 512:(dh + 1) * 512], ps_tmp[:, :]))
        c0.close()

        def emit_routing(call):
            with ExitStack() as rc:
                er = rc.enter_context
                sfx = f"_r{call}"
                idxrow = er(nc.sbuf_tensor("idxrow" + sfx, [1, T4], BF16))
                ohT = er(nc.sbuf_tensor("ohT" + sfx, [E, T4], BF16))
                rT = er(nc.sbuf_tensor("rT" + sfx, [E, T4], F32))
                prodc = er(nc.sbuf_tensor("prodc" + sfx, [E, 2, 512], F32))
                Ssb = er(nc.sbuf_tensor("Ssb" + sfx, [128, E, NC41], I32))
                mrg_m = er(nc.sbuf_tensor("mrg_m" + sfx, [128, NC41], F32))
                mrg_t = er(nc.sbuf_tensor("mrg_t" + sfx, [128, NC41], F32))
                gacc = er(nc.sbuf_tensor("gacc" + sfx, [128, NC41], F32))
                ps_lg = er(nc.psum_tensor("ps_lg" + sfx, [128, 2, E], F32))
                ps_xnt = er(nc.psum_tensor("ps_xnt" + sfx, [128, 2, 8, 128], BF16))
                ps_bc = er(nc.psum_tensor("ps_bc" + sfx, [E, 2, 512], F32))
                ps_b8 = er(nc.psum_tensor("ps_b8" + sfx, [E, 1], F32))
                ps_cnt = er(nc.psum_tensor("ps_cnt" + sfx, [E, 1], F32))
                ps_rs = er(nc.psum_tensor("ps_rs" + sfx, [128, NT], F32))

                xnst0 = s_sst.n
                cc_cnt = s_cc[call * 4 + 0]
                cc_xn = s_cc[call * 4 + 1]
                if call == 0:
                    @block.gpsimd
                    def _(g):
                        for k in range(8):
                            g.wait_ge(s_sst.h, xnst0 + 32 * (2 * k + 2))
                            cc_xn.inc(g.collective_compute(
                                "AllGather", ALU.bypass, replica_groups=[list(range(NCORE))],
                                ins=[ag_xn_in.ap()[512 * k:512 * (k + 1), :].opt()],
                                outs=[ag_xn_out.ap()[NCORE * 512 * k:NCORE * 512 * (k + 1), :].opt()]))

                    for gp in range(NT // 2):
                        t0, t1 = 2 * gp, 2 * gp + 1

                        @block.sync
                        def _(sync, gp=gp, t0=t0, t1=t1, call=call):
                            if gp == 0:
                                if call == 1:
                                    s_t.wait(sync)
                                    s_v.wait(sync)
                                    s_a.wait(sync)
                                    s_sst.wait(sync)
                            if call == 0:
                                if gp >= 1:
                                    s_v.wait_at(sync, st[f"lnap_{call}_{gp-1}"])
                                s_ld.inc(sync.dma_start(out=xt[:, 0, :], in_=x_til[:, t0, :]), 16)
                                s_ld.inc(sync.dma_start(out=xt[:, 1, :], in_=x_til[:, t1, :]), 16)
                            else:
                                if gp >= 1:
                                    s_tr.wait_at(sync, st[f"trpp_{call}_{gp-1}"])
                                s_ld.inc(sync.dma_start(out=xnb[:, 0, 0, :], in_=agxn_til[:, t0, :]), 16)
                                s_ld.inc(sync.dma_start(out=xnb[:, 0, 1, :], in_=agxn_til[:, t1, :]), 16)

                        if call == 0:
                            @block.vector
                            def _(v, gp=gp, t0=t0, t1=t1):
                                s_ld.wait(v)
                                if gp >= 1:
                                    s_t.wait_at(v, st[f"xtm_0_{gp-1}"])
                                    s_sst.wait_at(v, xnst0 + 32 * gp)
                                v.bn_stats(bns[:, 0, 0:6], xt[:, 0, 0:512])
                                v.bn_stats(bns[:, 0, 6:12], xt[:, 0, 512:1024])
                                v.bn_stats(bns[:, 1, 0:6], xt[:, 1, 0:512])
                                v.bn_stats(bns[:, 1, 6:12], xt[:, 1, 512:1024])
                                v.drain()
                                v.bn_aggr(mv[:, 0, :], bns[:, 0, :])
                                v.bn_aggr(mv[:, 1, :], bns[:, 1, :])
                                v.drain()
                                s_v.inc(v.tensor_scalar(rsd[:, :, :], mv[:, :, 1:2], EPS, None, ALU.add))

                            @block.scalar
                            def _(sc, gp=gp):
                                s_v.wait(sc)
                                sc.activation(out=rsd[:, :, :], in_=rsd[:, :, :], func=AF.Ln)
                                sc.drain()
                                s_a.inc(sc.activation(out=rsd[:, :, :], in_=rsd[:, :, :], func=AF.Exp, scale=-0.5))
                                st[f"rsq_0_{gp}"] = s_a.n

                            @block.vector
                            def _(v, gp=gp):
                                s_a.wait_at(v, st[f"rsq_0_{gp}"])
                                v.tensor_scalar(xnb[:, 0, 0, :], xt[:, 0, :], mv[:, 0, 0:1],
                                                rsd[:, 0, :], ALU.subtract, ALU.mult)
                                s_v.inc(v.tensor_scalar(xnb[:, 0, 1, :], xt[:, 1, :], mv[:, 1, 0:1],
                                                        rsd[:, 1, :], ALU.subtract, ALU.mult))
                                st[f"lnap_0_{gp}"] = s_v.n

                            @block.scalar
                            def _(sc, gp=gp, t0=t0, t1=t1):
                                s_v.wait(sc)
                                s_sst.inc(sc.dma_start(out=agxn_til[:, t0, :], in_=xnb[:, 0, 0, :]), 16)
                                s_sst.inc(sc.dma_start(out=agxn_til[:, t1, :], in_=xnb[:, 0, 1, :]), 16)

                        @block.tensor
                        def _(t_, gp=gp, call=call):
                            s_v.wait(t_)
                            if gp >= 1:
                                s_a.wait_at(t_, st[f"xev_{call}_{gp-1}"])
                            inst = None
                            for par in range(2):
                                for dc in range(8):
                                    inst = t_.transpose(ps_xnt[:, par, dc, :],
                                                        xnb[:, 0, par, dc * 128:(dc + 1) * 128],
                                                        idn[:, :])
                            s_t.inc(inst)
                            st[f"xtm_{call}_{gp}"] = s_t.n

                        @block.scalar
                        def _(sc, gp=gp, call=call):
                            s_t.wait_at(sc, st[f"xtm_{call}_{gp}"])
                            if gp >= 1:
                                s_t.wait_at(sc, st[f"rtp_{call}_{gp-1}"])
                            sc.activation(out=xnf[:, 0, 0, :, :], in_=ps_xnt[:, 0, :, :], func=AF.Copy)
                            s_a.inc(sc.activation(out=xnf[:, 0, 1, :, :], in_=ps_xnt[:, 1, :, :], func=AF.Copy))
                            st[f"xev_{call}_{gp}"] = s_a.n

                        @block.tensor
                        def _(t_, gp=gp, call=call):
                            s_a.wait_at(t_, st[f"xev_{call}_{gp}"])
                            if gp == 0:
                                s_wg.wait(t_)
                            if gp >= 1:
                                s_v.wait_at(t_, st[f"shp_{call}_{gp-1}"])
                            inst = None
                            for par in range(2):
                                for dc in range(8):
                                    inst = t_.matmul(ps_lg[:, par, :], xnf[:, 0, par, dc, :], wg_sb[:, dc, :],
                                                     start=(dc == 0), stop=(dc == 7))
                            s_t.inc(inst)
                            st[f"rtp_{call}_{gp}"] = s_t.n

                        @block.vector
                        def _(v, gp=gp, t0=t0, t1=t1, call=call):
                            s_t.wait(v)
                            if gp >= 1:
                                s_a.wait_at(v, st[f"expp_{call}_{gp-1}"])
                                s_t.wait_at(v, st[f"cntp_{call}_{gp-1}"])
                            for par, tt in ((0, t0), (1, t1)):
                                v.max(mx8[:, :], ps_lg[:, par, :])
                                v.drain()
                                v.max_index(ix8[:, :], mx8[:, :], ps_lg[:, par, :])
                                v.drain()
                                v.tensor_copy(idxf[:, tt:tt + 1], ix8[:, 0:1])
                                v.tensor_copy(idc[:, :], ix8[:, 0:1])
                                v.drain()
                                v.tensor_scalar(ohtm[:, par, :], iotaef[:, :], idc[:, :], None, ALU.is_equal)
                                inst = v.tensor_scalar(sh8[:, par, :], ps_lg[:, par, :], mx8[:, 0:1], None, ALU.subtract)
                                v.drain()
                            s_v.inc(inst)
                            st[f"shp_{call}_{gp}"] = s_v.n

                        @block.scalar
                        def _(sc, gp=gp, t0=t0, t1=t1, call=call):
                            s_v.wait(sc)
                            sc.activation(out=she[:, 0, :], in_=sh8[:, 0, :], func=AF.Exp,
                                          accum_out=esum[:, t0:t0 + 1])
                            s_a.inc(sc.activation(out=she[:, 1, :], in_=sh8[:, 1, :], func=AF.Exp,
                                                  accum_out=esum[:, t1:t1 + 1]))
                            st[f"expp_{call}_{gp}"] = s_a.n

                        @block.tensor
                        def _(t_, gp=gp, call=call):
                            s_v.wait(t_)
                            inst = None
                            for par in range(2):
                                inst = t_.matmul(ps_cnt[:, :], ohtm[:, par, :], ones128b[:, :],
                                                 start=(gp == 0 and par == 0),
                                                 stop=(gp == NT // 2 - 1 and par == 1))
                            s_t.inc(inst)
                            st[f"cntp_{call}_{gp}"] = s_t.n
                            if gp == NT // 2 - 1:
                                st[f"cntmm_{call}"] = s_t.n

                    @block.vector
                    def _(v, call=call):
                        s_t.wait(v)
                        s_v.inc(v.tensor_copy(cnt8[:, :], ps_cnt[:, :]))
                        st[f"cntev_{call}"] = s_v.n
                        s_a.wait(v)
                        v.reciprocal(comb[:, :], esum[:, :])
                        v.drain()
                        s_v.inc(v.tensor_copy(idxw[:, :], idxf[:, :]))

                    @block.gpsimd
                    def _(g, call=call):
                        g.wait_ge(s_v.h, st[f"cntev_{call}"])
                        s_gd.inc(g.dma_start(out=cnt_dram.ap(), in_=cnt8[:, :]), 16)
                        s_gd.wait(g)
                        cc_cnt.inc(g.collective_compute(
                            "AllGather", ALU.bypass, replica_groups=[list(range(NCORE))],
                            ins=[cnt_dram.ap().opt()], outs=[cnt_ag.ap().opt()]))

                    for q in range(8):
                        @block.tensor
                        def _(t_, q=q):
                            s_v.wait(t_)
                            inst = None
                            for j in range(4):
                                tl = q * 4 + j
                                inst = t_.transpose(ps_xnt[0:1, q % 2, j, :],
                                                    idxf[:, tl:tl + 1], idn[:, :])
                            s_t.inc(inst)

                        @block.vector
                        def _(v, q=q):
                            s_t.wait(v)
                            if q == 0:
                                s_gd.wait(v)
                            s_v.inc(v.tensor_copy(idxrow[:, q * 512:(q + 1) * 512],
                                                  ps_xnt[0:1, q % 2, 0:4, :].rearrange("p a b -> p (a b)")))

                    for q in range(8):
                        @block.tensor
                        def _(t_, q=q):
                            s_v.wait(t_)
                            s_t.inc(t_.matmul(ps_bc[:, q % 2, :], ones18[:, :], idxrow[:, q * 512:(q + 1) * 512],
                                              start=True, stop=True))

                        @block.vector
                        def _(v, q=q):
                            s_t.wait(v)
                            s_v.inc(v.tensor_scalar(ohT[:, q * 512:(q + 1) * 512], ps_bc[:, q % 2, :],
                                                    iota8c[:, :], None, ALU.is_equal))


                @block.sync
                def _(sync):
                    cc_cnt.wait_at(sync, 1)
                    s_ld.inc(sync.dma_start(out=cntall[:, :], in_=cnt_ag.ap()[:, :]), 16)

                @block.tensor
                def _(t_):
                    s_ld.wait(t_)
                    s_t.inc(t_.matmul(ps_b8[:, :], cntall[:, :], premask[:, :], start=True, stop=True))

                @block.vector
                def _(v):
                    s_t.wait(v)
                    v.tensor_copy(base8[:, :], ps_b8[:, :])
                    v.drain()
                    s_v.inc(v.tensor_tensor_scan(rT[:, :], ohT[:, :], ohT[:, :], base8[:, :],
                                                 ALU.add, ALU.max))

                for q in range(8):
                    @block.vector
                    def _(v, q=q):
                        if q >= 2:
                            s_t.wait_at(v, st[f"rsmm_{call}_{q-2}"])
                        v.drain()
                        s_v.inc(v.tensor_tensor(prodc[:, q % 2, :], ohT[:, q * 512:(q + 1) * 512],
                                                rT[:, q * 512:(q + 1) * 512], ALU.mult))

                    @block.tensor
                    def _(t_, q=q, call=call):
                        s_v.wait(t_)
                        inst = None
                        for j in range(4):
                            tl = q * 4 + j
                            inst = t_.matmul(ps_rs[:, tl:tl + 1], prodc[:, q % 2, j * 128:(j + 1) * 128],
                                             ones8c[:, :], start=True, stop=True)
                        s_t.inc(inst)
                        st[f"rsmm_{call}_{q}"] = s_t.n

                @block.vector
                def _(v):
                    s_t.wait(v)
                    v.tensor_copy(rsel[:, :], ps_rs[:, :])
                    v.drain()
                    v.tensor_scalar(vld[:, :], rsel[:, :], float(CAP) + 0.5, None, ALU.is_le)
                    v.drain()
                    v.tensor_tensor(wsc[:, :], comb[:, :], vld[:, :], ALU.mult)
                    v.drain()
                    v.tensor_scalar(wsc[:, :], wsc[:, :], 0.5, None, ALU.mult)
                    v.tensor_scalar(off2[:, :], rsel[:, :], -1.0, 5120.0, ALU.add, ALU.min)
                    v.drain()
                    v.tensor_scalar(offf[:, :], off2[:, :], 3839.5, None, ALU.is_ge)
                    v.tensor_scalar(scc[:, :], off2[:, :], 4607.5, None, ALU.is_ge)
                    v.tensor_scalar(esum[:, :], off2[:, :], 1919.5, None, ALU.is_ge)
                    v.scalar_tensor_tensor(gretf[:, :], idxw[:, :], 1920.0, off2[:, :], ALU.mult, ALU.add)
                    v.tensor_scalar(spp[:, :], idxw[:, :], -128.0, 5376.0, ALU.mult, ALU.add)
                    v.drain()
                    v.scalar_tensor_tensor(gretf[:, :], esum[:, :], float((NCORE - 1) * 1920), gretf[:, :],
                                           ALU.mult, ALU.add)
                    v.scalar_tensor_tensor(off2[:, :], idxw[:, :], 768.0, off2[:, :], ALU.mult, ALU.add)
                    v.tensor_tensor(scc[:, :], scc[:, :], spp[:, :], ALU.mult)
                    v.drain()
                    v.tensor_scalar(off2[:, :], off2[:, :], float(NCORE * 3840 - 3840), None, ALU.add)
                    v.drain()
                    v.tensor_tensor(off2[:, :], off2[:, :], scc[:, :], ALU.add)
                    v.drain()
                    v.tensor_tensor(off2[:, :], off2[:, :], gretf[:, :], ALU.subtract)
                    v.drain()
                    v.tensor_tensor(off2[:, :], off2[:, :], offf[:, :], ALU.mult)
                    v.drain()
                    v.tensor_tensor(gretf[:, :], gretf[:, :], off2[:, :], ALU.add)
                    v.drain()
                    s_v.inc(v.tensor_copy(gretu[:, :], gretf[:, :]))
                    v.tensor_scalar(spp[:, :], rsel[:, :], -1.0, None, ALU.add)
                    v.drain()
                    v.tensor_scalar(scc[:, :], spp[:, :], 1.0 / 128.0, C23, ALU.mult, ALU.add)
                    v.drain()
                    v.tensor_scalar(scc[:, :], scc[:, :], -C23, None, ALU.add)
                    v.drain()
                    v.scalar_tensor_tensor(spp[:, :], scc[:, :], -128.0, spp[:, :], ALU.mult, ALU.add)
                    v.drain()
                    v.tensor_scalar(offf[:, :], spp[:, :], 0.0, None, ALU.is_lt)
                    v.drain()
                    v.tensor_tensor(scc[:, :], scc[:, :], offf[:, :], ALU.subtract)
                    v.scalar_tensor_tensor(spp[:, :], offf[:, :], 128.0, spp[:, :], ALU.mult, ALU.add)
                    v.drain()
                    v.scalar_tensor_tensor(off2[:, :], spp[:, :], 41.0, scc[:, :], ALU.mult, ALU.add)
                    v.drain()
                    v.scalar_tensor_tensor(offf[:, :], idxw[:, :], float(RPAD), off2[:, :], ALU.mult, ALU.add)
                    v.drain()
                    v.tensor_scalar(off2[:, :], vld[:, :], -BIG, BIG, ALU.mult, ALU.add)
                    v.drain()
                    v.tensor_tensor(offf[:, :], offf[:, :], off2[:, :], ALU.add)
                    v.drain()
                    s_v.inc(v.tensor_copy(offu[:, :], offf[:, :]))

                cc_S = s_cc[call * 4 + 2]

                @block.gpsimd
                def _(g, call=call):
                    s_v.wait(g)
                    s_gd.wait(g)
                    for t in range(NT):
                        s_gd.inc(g.indirect_dma_start(
                            out=S_dram.ap(),
                            out_offset=IndirectOffsetOnAxis(offu[:, t:t + 1], 0),
                            in_=locid[:, t:t + 1], in_offset=None,
                            bounds_check=st["rb_slot"], oob_is_err=False,
                        ), 16)
                    g.wait_ge(s_gd.h, s_gd.n)
                    cc_S.inc(g.collective_compute(
                        "AllGather", ALU.bypass, replica_groups=[list(range(NCORE))],
                        ins=[S_dram.ap().opt()], outs=[S_ag.ap().opt()]))
                    cc_S.wait_at(g, 1)
                    for r in range(NCORE):
                        s_gd.inc(g.indirect_dma_start(
                            out=Ssb[:, r, :], out_offset=None,
                            in_=S_rows, in_offset=IndirectOffsetOnAxis(midxu[:, r:r + 1], 0),
                            bounds_check=st["rb_srow"], oob_is_err=False,
                        ), 16)

                @block.vector
                def _(v):
                    s_gd.wait(v)
                    for r in range(NCORE):
                        v.tensor_scalar(mrg_m[:, :], Ssb[:, r, :], -0.5, None, ALU.is_ge)
                        v.tensor_scalar(mrg_t[:, :], Ssb[:, r, :], float(r * 512), None, ALU.add)
                        v.drain()
                        if r == 0:
                            v.tensor_tensor(gacc[:, :], mrg_t[:, :], mrg_m[:, :], ALU.mult)
                        else:
                            v.tensor_tensor(mrg_t[:, :], mrg_t[:, :], mrg_m[:, :], ALU.mult)
                            v.drain()
                            v.tensor_tensor(gacc[:, :], gacc[:, :], mrg_t[:, :], ALU.add)
                        v.drain()
                    s_v.inc(v.tensor_copy(g_u[:, :], gacc[:, :]))
                    st[f"routing_done_{call}"] = s_v.n

        def emit_expert(call):
            with ExitStack() as fc:
                ef = fc.enter_context
                sfx = f"_f{call}"
                xg = ef(nc.sbuf_tensor("xg" + sfx, [128, 2, 3, D], BF16))
                xgf = ef(nc.sbuf_tensor("xgf" + sfx, [128, 2, 3, 8, 128], BF16, is_transpose_dest=True))
                h_sb = ef(nc.sbuf_tensor("h_sb" + sfx, [128, 16, CHT], BF16))
                y_sb = ef(nc.sbuf_tensor("y_sb" + sfx, [128, 3, D], BF16))
                ps_ha = [ef(nc.psum_tensor(f"ps_h{i}" + sfx, [128, CHT], F32)) for i in range(2)]
                ps_y = [[ef(nc.psum_tensor(f"ps_y{stt}{dh}" + sfx, [128, 512], F32)) for dh in range(2)]
                        for stt in range(3)]

                fill0 = s_fill.n
                tr0 = s_tr.n
                m20 = s_m2.n
                ev0 = s_v.n
                st0 = s_sst.n
                st[f"yst_{call}_base"] = st0
                routing_done = st[f"routing_done_{call}"]
                cc_xn = s_cc[call * 4 + 1]
                cc_y = s_cc[call * 4 + 3]

                ntile = NC41
                chunks = [(ch, min(3, ntile - ch * 3)) for ch in range(NCH)]

                for ch, nk in chunks:
                    par = ch % 2

                    @block.gpsimd
                    def _(g, ch=ch, nk=nk, par=par):
                        if ch == 0:
                            s_v.wait_at(g, routing_done)
                            cc_xn.wait_at(g, 8)
                        if ch >= 2:
                            s_tr.wait_at(g, st[f"xtr_{call}_{ch-2}"])
                        for k in range(nk):
                            s_gd.inc(g.indirect_dma_start(
                                out=xg[:, par, k, :], out_offset=None,
                                in_=ag_xn_out.ap(), in_offset=IndirectOffsetOnAxis(
                                    g_u[:, ch * 3 + k:ch * 3 + k + 1], 0),
                                bounds_check=st["rb_xn"], oob_is_err=False,
                            ), 16)

                    @block.sync
                    def _(sync, ch=ch, nk=nk, par=par):
                        s_gd.wait(sync)
                        if ch >= 2:
                            s_fill.wait_at(sync, fill0 + 32 * (ch - 1))
                        inst = None
                        for k in range(nk):
                            s_tr.inc(sync.dma_start(out=xgf[:, par, k, :, :], in_=xg[:, par, k, :],
                                                    transpose=True), 16)
                        st[f"xtr_{call}_{ch}"] = s_tr.n

                    if ch == 6:
                        @block.gpsimd
                        def _(g, call=call):
                            g.wait_ge(s_sst.h, st[f"yst_{call}_base"] + 5 * 16)
                            s_ccy[call * 3].inc(g.collective_compute(
                                "AllGather", ALU.bypass, replica_groups=[list(range(NCORE))],
                                ins=[y_d.ap()[0:1920, :].opt()],
                                outs=[y_ag.ap()[0:NCORE * 1920, :].opt()]))
                    if ch == 11:
                        @block.gpsimd
                        def _(g, call=call):
                            g.wait_ge(s_sst.h, st[f"yst_{call}_base"] + 10 * 16)
                            s_ccy[call * 3].inc(g.collective_compute(
                                "AllGather", ALU.bypass, replica_groups=[list(range(NCORE))],
                                ins=[y_d.ap()[1920:3840, :].opt()],
                                outs=[y_ag.ap()[NCORE * 1920:NCORE * 3840, :].opt()]))
                    if ch == 13:
                        @block.gpsimd
                        def _(g, call=call):
                            g.wait_ge(s_sst.h, st[f"yst_{call}_base"] + 12 * 16)
                            s_ccy[call * 3 + 1].inc(g.collective_compute(
                                "AllGather", ALU.bypass, replica_groups=[list(range(NCORE))],
                                ins=[y_d.ap()[3840:4608, :].opt()],
                                outs=[y_ag.ap()[NCORE * 3840:NCORE * 3840 + NCORE * 768, :].opt()]))

                    for fh in range(2):
                        fillv = []
                        abase = s_a.n

                        for ft in range(16):
                            fg = fh * 16 + ft

                            @block.tensor
                            def _(t_, ch=ch, nk=nk, par=par, fh=fh, ft=ft, fg=fg, abase=abase):
                                if fh == 0 and ft == 0:
                                    s_tr.wait(t_)
                                    if ch == 0:
                                        s_wc.wait(t_)
                                if ft >= 2:
                                    s_a.wait_at(t_, abase + ft - 1)
                                elif s_a.n > 0:
                                    s_a.wait_at(t_, s_a.n)
                                inst = None
                                for dc in range(8):
                                    inst = t_.matmul(ps_ha[ft % 2][:, 0:nk * 128],
                                                     w1_sb[:, dc, fg * 128:(fg + 1) * 128],
                                                     xgf[:, par, 0:nk, dc, :],
                                                     start=(dc == 0), stop=(dc == 7))
                                s_fill.inc(inst)
                                fillv.append(s_fill.n)

                        @block.scalar
                        def _(sc, ch=ch, nk=nk, fh=fh, fillv=fillv):
                            for ft in range(16):
                                fg = fh * 16 + ft
                                s_fill.wait_at(sc, fillv[ft])
                                s_a.inc(sc.activation(out=h_sb[:, ft, 0:nk * 128], in_=ps_ha[ft % 2][:, 0:nk * 128],
                                                      func=AF.Gelu, bias=b1_sb[:, fg:fg + 1], scale=1.0))

                        @block.tensor
                        def _(t_, ch=ch, nk=nk, fh=fh):
                            s_a.wait(t_)
                            if fh == 0:
                                s_v.wait_at(t_, ev0 + ch * 6)
                            for stt in range(3 if nk == 3 else nk):
                                for dh in range(2):
                                    inst = None
                                    for fc in range(16):
                                        inst = t_.matmul(ps_y[stt][dh][:, :],
                                                         h_sb[:, fc, stt * 128:(stt + 1) * 128],
                                                         w2_sb[:, fh * 16 + fc, dh * 512:(dh + 1) * 512],
                                                         start=(fh == 0 and fc == 0),
                                                         stop=(fh == 1 and fc == 15))
                                    if fh == 1:
                                        s_m2.inc(inst)

                    @block.vector
                    def _(v, ch=ch, nk=nk, par=par):
                        s_m2.wait(v)
                        s_sst.wait_at(v, st0 + ch * 16)
                        inst = None
                        for stt in range(nk):
                            for dh in range(2):
                                inst = v.tensor_tensor(y_sb[:, stt, dh * 512:(dh + 1) * 512],
                                                       ps_y[stt][dh][:, :],
                                                       b2b[:, dh * 512:(dh + 1) * 512], ALU.add)
                        s_v.inc(inst, 6)

                    @block.scalar
                    def _(sc, ch=ch, nk=nk):
                        s_v.wait(sc)
                        s_sst.inc(sc.dma_start(out=y_til[:, ch * 3:ch * 3 + nk, :],
                                               in_=y_sb[:, 0:nk, :]), 16)



                @block.gpsimd
                def _(g):
                    s_sst.wait(g)
                    s_ccy[call * 3 + 1].inc(g.collective_compute(
                        "AllGather", ALU.bypass, replica_groups=[list(range(NCORE))],
                        ins=[y_d.ap()[4608:5248, :].opt()],
                        outs=[y_ag.ap()[NCORE * 3840 + NCORE * 768:NCORE * RPAD, :].opt()]))

        def emit_return(call):
            rc = ctx.enter_context(ExitStack())
            ygat = rc.enter_context(nc.sbuf_tensor(f"ygat{call}", [128, 2, 2, D], BF16))
            xtr = rc.enter_context(nc.sbuf_tensor(f"xtr{call}", [128, 2, 2, D], F32))
            xnwr = rc.enter_context(nc.sbuf_tensor(f"xnwr{call}", [128, 2, 2, D], F32))
            if call == 0:
                xnbr = rc.enter_context(nc.sbuf_tensor(f"xnbr{call}", [128, 2, 2, D], BF16))
            for gp in range(NT // 2):
                t0, t1 = 2 * gp, 2 * gp + 1

                @block.gpsimd
                def _(g, gp=gp, t0=t0, t1=t1, call=call):
                    if gp == 0:
                        s_ccy[call * 3].wait_at(g, 2)
                        s_ccy[call * 3 + 1].wait_at(g, 2)
                    if gp >= 2:
                        s_v.wait_at(g, st[f"resp_{call}_{gp-2}"])
                    sl = gp % 2
                    for par, tt in ((0, t0), (1, t1)):
                        s_gd.inc(g.indirect_dma_start(
                            out=ygat[:, sl, par, :], out_offset=None,
                            in_=y_ag.ap(), in_offset=IndirectOffsetOnAxis(gretu[:, tt:tt + 1], 0),
                            bounds_check=st["rb_y"], oob_is_err=False,
                        ), 16)

                @block.sync
                def _(sync, gp=gp, t0=t0, t1=t1, call=call):
                    if gp < 2:
                        s_ccy[call * 3 + 1].wait_at(sync, 2)
                    else:
                        s_v.wait_at(sync, st[f"resp_{call}_{gp-2}"])
                    sl = gp % 2
                    src = x_til if call == 0 else x2_til
                    s_ld.inc(sync.dma_start(out=xtr[:, sl, 0, :], in_=src[:, t0, :]), 16)
                    s_ld.inc(sync.dma_start(out=xtr[:, sl, 1, :], in_=src[:, t1, :]), 16)

                @block.vector
                def _(v, gp=gp, t0=t0, t1=t1, call=call):
                    s_gd.wait(v)
                    s_ld.wait(v)
                    if gp >= 2:
                        s_sst.wait_at(v, st[f"xstp_{call}_{gp-2}"])
                    sl = gp % 2
                    v.scalar_tensor_tensor(xnwr[:, sl, 0, :], ygat[:, sl, 0, :], wsc[:, t0:t0 + 1],
                                           xtr[:, sl, 0, :], ALU.mult, ALU.add)
                    inst = v.scalar_tensor_tensor(xnwr[:, sl, 1, :], ygat[:, sl, 1, :], wsc[:, t1:t1 + 1],
                                                  xtr[:, sl, 1, :], ALU.mult, ALU.add)
                    s_v.inc(inst)
                    st[f"resp_{call}_{gp}"] = s_v.n
                    if call == 0:
                        v.drain()
                        v.bn_stats(bns[:, 0, 0:6], xnwr[:, sl, 0, 0:512])
                        v.bn_stats(bns[:, 0, 6:12], xnwr[:, sl, 0, 512:1024])
                        v.bn_stats(bns[:, 1, 0:6], xnwr[:, sl, 1, 0:512])
                        v.bn_stats(bns[:, 1, 6:12], xnwr[:, sl, 1, 512:1024])
                        v.drain()
                        v.bn_aggr(mv[:, 0, :], bns[:, 0, :])
                        v.bn_aggr(mv[:, 1, :], bns[:, 1, :])
                        v.drain()
                        s_v.inc(v.tensor_scalar(rsd[:, :, :], mv[:, :, 1:2], EPS, None, ALU.add))

                if call == 0:
                    @block.scalar
                    def _(sc, gp=gp):
                        s_v.wait(sc)
                        sc.activation(out=rsd[:, :, :], in_=rsd[:, :, :], func=AF.Ln)
                        sc.drain()
                        s_a.inc(sc.activation(out=rsd[:, :, :], in_=rsd[:, :, :], func=AF.Exp, scale=-0.5))

                    @block.vector
                    def _(v, gp=gp):
                        s_a.wait(v)
                        if gp >= 2:
                            s_sst.wait_at(v, st[f"xn2p_{gp-2}"])
                        sl = gp % 2
                        v.tensor_scalar(xnbr[:, sl, 0, :], xnwr[:, sl, 0, :], mv[:, 0, 0:1],
                                        rsd[:, 0, :], ALU.subtract, ALU.mult)
                        s_v.inc(v.tensor_scalar(xnbr[:, sl, 1, :], xnwr[:, sl, 1, :], mv[:, 1, 0:1],
                                                rsd[:, 1, :], ALU.subtract, ALU.mult))

                    @block.scalar
                    def _(sc, gp=gp, t0=t0, t1=t1):
                        s_v.wait(sc)
                        sl = gp % 2
                        s_sst.inc(sc.dma_start(out=x1_til[:, t0, :], in_=xnwr[:, sl, 0, :]), 16)
                        s_sst.inc(sc.dma_start(out=x1_til[:, t1, :], in_=xnwr[:, sl, 1, :]), 16)
                        st[f"xstp_0_{gp}"] = s_sst.n
                        s_sst.inc(sc.dma_start(out=xn2_til[:, t0, :], in_=xnbr[:, sl, 0, :]), 16)
                        s_sst.inc(sc.dma_start(out=xn2_til[:, t1, :], in_=xnbr[:, sl, 1, :]), 16)
                        st[f"xn2p_{gp}"] = s_sst.n
                else:
                    @block.scalar
                    def _(sc, gp=gp, t0=t0, t1=t1):
                        s_v.wait(sc)
                        sl = gp % 2
                        s_sst.inc(sc.dma_start(out=xo_til[:, t0, :], in_=xnwr[:, sl, 0, :]), 16)
                        s_sst.inc(sc.dma_start(out=xo_til[:, t1, :], in_=xnwr[:, sl, 1, :]), 16)
                        st[f"xstp_1_{gp}"] = s_sst.n
            rc.close()

        def emit_attention():
            with ExitStack() as ac:
                ea = ac.enter_context
                qkp_sb = ea(nc.sbuf_tensor("qkp_sb", [128, 8, 2 * PD], BF16))
                xnf2 = ea(nc.sbuf_tensor("xnf2", [128, 2, 2, 8, 128], BF16, is_transpose_dest=True))
                lga = ea(nc.sbuf_tensor("lga", [128, 16, 16], F32))
                aw = ea(nc.sbuf_tensor("aw", [128, 2, 2, D], F32))

                def emit_router_mm(b, g):
                    lgb = ps_kv[1][0]

                    @block.tensor
                    def _(t_, g=g, b=b):
                        s_tr.wait_at(t_, st[f"a2trp_{b}_{g}"])
                        if g == 0:
                            s_v.wait_at(t_, st[f"kvz_{b}"])
                        prev = st.get(f"a2sh_{b}_{g-1}", st.get(f"a2sh_{b-1}_7", 0))
                        s_v.wait_at(t_, prev)
                        inst = None
                        for par in range(2):
                            for dc in range(8):
                                inst = t_.matmul(lgb[:, par * 8:(par + 1) * 8],
                                                 xnf2[:, g % 2, par, dc, :], wg_sb[:, dc, :],
                                                 start=(dc == 0), stop=(dc == 7))
                        s_t.inc(inst)
                        st[f"a2mm_{b}_{g}"] = s_t.n

                    @block.vector
                    def _(v, g=g, b=b):
                        s_t.wait_at(v, st[f"a2mm_{b}_{g}"])
                        s_v.inc(v.tensor_copy(lga[:, b * 8 + g, :], lgb[:, 0:16]))
                        st[f"a2sh_{b}_{g}"] = s_v.n
                qe_all = ea(nc.sbuf_tensor("qe_all", [128, 4, PD], BF16))
                qT = ea(nc.sbuf_tensor("qT", [128, 16, 2, 128], BF16, is_transpose_dest=True))
                ke = ea(nc.sbuf_tensor("ke", [128, 2, PD], BF16))
                kv_sb = ea(nc.sbuf_tensor("kv_sb", [128, 2, D], BF16))
                zsum = ea(nc.sbuf_tensor("zsum", [128, 2, 16], F32))
                zinv = ea(nc.sbuf_tensor("zinv", [128, 2, 16], F32))
                ps_qk = [ea(nc.psum_tensor(f"ps_qk{i}", [128, 512], F32)) for i in range(2)]
                ps_kv = [[ea(nc.psum_tensor(f"ps_kv{eh}{dh}", [128, 512], F32)) for dh in range(2)]
                         for eh in range(2)]
                ps_nm = [ea(nc.psum_tensor(f"ps_nm{i}", [128, 512], F32)) for i in range(2)]

                @block.gpsimd
                def _(g):
                    s_sst.wait_at(g, st["xn2p_15"])
                    s_wc.inc(g.dma_start(out=qkp_sb[:, :, :],
                                         in_=qk_in.rearrange("(dc p) e -> p dc e", p=128)), 16)

                qkb = [ps_qk, ps_nm]

                def emit_kv(t_, gp, b):
                    s_a.wait_at(t_, st[f"expkp_{b}_{gp}"])
                    if b > 0 and gp == 0:
                        s_v.wait_at(t_, st[f"aresp_{b-1}_7"])
                        s_v.wait_at(t_, st[f"a2sh_{b-1}_7"])
                    sl = gp % 2
                    inst = None
                    for par in range(2):
                        for eh in range(2):
                            for dh in range(2):
                                inst = t_.matmul(ps_kv[eh][dh][:, :],
                                                 ke[:, par, eh * 128:(eh + 1) * 128],
                                                 xnb[:, sl, par, dh * 512:(dh + 1) * 512],
                                                 start=(gp == 0 and par == 0),
                                                 stop=(gp == 7 and par == 1))
                    s_t.inc(inst)
                    st[f"kvp_{b}_{gp}"] = s_t.n

                for b in (ATT_BATCHES if ATT_BATCHES is not None else range(B)):
                    for gp in range(8):
                        i0, i1 = 2 * gp, 2 * gp + 1
                        t0, t1 = b * 16 + i0, b * 16 + i1
                        sl = gp % 2

                        @block.sync
                        def _(sync, gp=gp, t0=t0, t1=t1, b=b, sl=sl):
                            if b == 0 and gp == 0:
                                s_sst.wait(sync)
                            if b > 0 and gp == 0:
                                s_t.wait_at(sync, st[f"kvp_{b-1}_7"])
                                s_sst.wait_at(sync, st[f"axp_{b-1}_7"])
                                s_tr.wait_at(sync, st[f"a2trp_{b-1}_7"])
                            if gp >= 2:
                                s_t.wait_at(sync, st[f"kvp_{b}_{gp-2}"])
                            s_ld.inc(sync.dma_start(out=xnb[:, sl, 0, :], in_=xn2_til[:, t0, :]), 16)
                            s_ld.inc(sync.dma_start(out=xnb[:, sl, 1, :], in_=xn2_til[:, t1, :]), 16)
                            s_ld.wait(sync)
                            s_tr.inc(sync.dma_start(out=xnf[:, sl, 0, :, :], in_=xnb[:, sl, 0, :], transpose=True), 16)
                            s_tr.inc(sync.dma_start(out=xnf[:, sl, 1, :, :], in_=xnb[:, sl, 1, :], transpose=True), 16)
                            st[f"atr_{b}_{gp}"] = s_tr.n
                            if gp >= 2:
                                qgp = gp - 2
                                j0, j1 = 2 * qgp, 2 * qgp + 1
                                s_a.wait_at(sync, st[f"expkp_{b}_{qgp}"])
                                if b > 0 and qgp <= 1:
                                    s_v.wait_at(sync, st[f"aresp_{b-1}_7"])
                                s_tr.inc(sync.dma_start(out=qT[:, j0, :, :], in_=qe_all[:, j0 % 4, :], transpose=True), 16)
                                s_tr.inc(sync.dma_start(out=qT[:, j1, :, :], in_=qe_all[:, j1 % 4, :], transpose=True), 16)
                                st[f"qtr_{b}_{qgp}"] = s_tr.n

                        @block.tensor
                        def _(t_, gp=gp, b=b, sl=sl):
                            s_tr.wait_at(t_, st[f"atr_{b}_{gp}"])
                            if gp == 0:
                                s_wc.wait(t_)
                            prev = st.get(f"expkp_{b}_{gp-2}", 0)
                            if prev:
                                s_a.wait_at(t_, prev)
                            if b > 0 and gp == 1:
                                s_v.wait_at(t_, st[f"aresp_{b-1}_7"])
                            qk = qkb[sl]
                            inst = None
                            for par in range(2):
                                for dc in range(8):
                                    inst = t_.matmul(qk[par][:, :], xnf[:, sl, par, dc, :], qkp_sb[:, dc, :],
                                                     start=(dc == 0), stop=(dc == 7))
                            s_t.inc(inst)
                            st[f"qkp_{b}_{gp}"] = s_t.n

                        if gp >= 1:
                            @block.tensor
                            def _(t_, gp=gp - 1, b=b):
                                emit_kv(t_, gp, b)

                        @block.scalar
                        def _(sc, gp=gp, i0=i0, i1=i1, b=b, sl=sl):
                            s_t.wait_at(sc, st[f"qkp_{b}_{gp}"])
                            prevk = st.get(f"kvp_{b}_{gp-1}", 0)
                            if prevk:
                                s_t.wait_at(sc, prevk)
                            prev = st.get(f"qtr_{b}_{gp-2}", st.get(f"qtr_{b-1}_{gp+6}", 0))
                            if prev:
                                s_tr.wait_at(sc, prev)
                            qk = qkb[sl]
                            sc.activation(out=qe_all[:, i0 % 4, :], in_=qk[0][:, 0:PD], func=AF.Exp,
                                          accum_out=zsum[:, b, i0:i0 + 1])
                            sc.activation(out=ke[:, 0, :], in_=qk[0][:, PD:2 * PD], func=AF.Exp)
                            sc.activation(out=qe_all[:, i1 % 4, :], in_=qk[1][:, 0:PD], func=AF.Exp,
                                          accum_out=zsum[:, b, i1:i1 + 1])
                            s_a.inc(sc.activation(out=ke[:, 1, :], in_=qk[1][:, PD:2 * PD], func=AF.Exp))
                            st[f"expkp_{b}_{gp}"] = s_a.n

                    @block.sync
                    def _(sync, b=b):
                        for qgp in (6, 7):
                            j0, j1 = 2 * qgp, 2 * qgp + 1
                            s_a.wait_at(sync, st[f"expkp_{b}_{qgp}"])
                            s_tr.inc(sync.dma_start(out=qT[:, j0, :, :], in_=qe_all[:, j0 % 4, :], transpose=True), 16)
                            s_tr.inc(sync.dma_start(out=qT[:, j1, :, :], in_=qe_all[:, j1 % 4, :], transpose=True), 16)
                            st[f"qtr_{b}_{qgp}"] = s_tr.n

                    @block.tensor
                    def _(t_, b=b):
                        emit_kv(t_, 7, b)

                    @block.vector
                    def _(v, b=b):
                        s_t.wait(v)
                        v.drain()
                        for eh in range(2):
                            for dh in range(2):
                                v.tensor_copy(kv_sb[:, eh, dh * 512:(dh + 1) * 512], ps_kv[eh][dh][:, :])
                        v.tensor_scalar(zinv[:, b, :], zsum[:, b, :], 1e-8, None, ALU.add)
                        v.drain()
                        s_v.inc(v.reciprocal(zinv[:, b, :], zinv[:, b, :]))
                        st[f"kvz_{b}"] = s_v.n

                    if DEBUG:
                        @block.scalar
                        def _(sc, b=b):
                            s_v.wait(sc)
                            s_sst.inc(sc.dma_start(out=dbg_z[b, :, :], in_=zinv[:, b, :]), 16)
                            s_sst.inc(sc.dma_start(out=dbg_kv[b, :, :], in_=kv_sb[:, :, :].rearrange("p a b -> p (a b)")), 16)
                            s_sst.inc(sc.dma_start(out=dbg_qe[b, :, :], in_=qe_all[:, :, :].rearrange("p a b -> p (a b)")), 16)

                    nmA = ps_nm
                    nmB = [ps_kv[0][0], ps_kv[0][1]]
                    for gp in range(8):
                        i0, i1 = 2 * gp, 2 * gp + 1
                        t0, t1 = b * 16 + i0, b * 16 + i1

                        if gp == 0:
                            @block.sync
                            def _(sync, b=b):
                                if b > 0:
                                    s_v.wait_at(sync, st[f"aresp_{b-1}_7"])
                                s_ld.inc(sync.dma_start(out=xt[:, 0, :], in_=x1_til[:, b * 16, :]), 16)
                                s_ld.inc(sync.dma_start(out=xt[:, 1, :], in_=x1_til[:, b * 16 + 1, :]), 16)

                        @block.tensor
                        def _(t_, gp=gp, i0=i0, i1=i1, b=b):
                            if gp == 0:
                                s_v.wait_at(t_, st[f"kvz_{b}"])
                                if b > 0:
                                    s_v.wait_at(t_, st[f"aresp_{b-1}_7"])
                            if gp >= 1:
                                s_v.wait_at(t_, st[f"aresp_{b}_{gp-1}"])
                            inst = None
                            for nmp, ii in ((nmA, i0), (nmB, i1)):
                                for dh in range(2):
                                    for eh in range(2):
                                        inst = t_.matmul(nmp[dh][:, :], qT[:, ii, eh, :],
                                                         kv_sb[:, eh, dh * 512:(dh + 1) * 512],
                                                         start=(eh == 0), stop=(eh == 1))
                            s_t.inc(inst)

                        @block.vector
                        def _(v, gp=gp, i0=i0, i1=i1, b=b):
                            s_t.wait(v)
                            s_ld.wait(v)
                            sl = gp % 2
                            prev = st.get(f"axp_{b}_{gp-2}", st.get(f"axp_{b-1}_{gp+6}", 0))
                            if prev:
                                s_sst.wait_at(v, prev)
                            for par, (nmp, ii) in enumerate(((nmA, i0), (nmB, i1))):
                                for dh in range(2):
                                    inst = v.scalar_tensor_tensor(aw[:, sl, par, dh * 512:(dh + 1) * 512],
                                                                  nmp[dh][:, :], zinv[:, b, ii:ii + 1],
                                                                  xt[:, par, dh * 512:(dh + 1) * 512],
                                                                  ALU.mult, ALU.add)
                            s_v.inc(inst)
                            st[f"aresp_{b}_{gp}"] = s_v.n
                            v.drain()
                            v.bn_stats(bns[:, 0, 0:6], aw[:, sl, 0, 0:512])
                            v.bn_stats(bns[:, 0, 6:12], aw[:, sl, 0, 512:1024])
                            v.bn_stats(bns[:, 1, 0:6], aw[:, sl, 1, 0:512])
                            v.bn_stats(bns[:, 1, 6:12], aw[:, sl, 1, 512:1024])
                            v.drain()
                            v.bn_aggr(mv[:, 0, :], bns[:, 0, :])
                            v.bn_aggr(mv[:, 1, :], bns[:, 1, :])
                            v.drain()
                            s_v.inc(v.tensor_scalar(rsd[:, :, :], mv[:, :, 1:2], EPS, None, ALU.add))

                        if gp < 7:
                            @block.sync
                            def _(sync, gp=gp, t0=t0, t1=t1, b=b):
                                s_v.wait_at(sync, st[f"aresp_{b}_{gp}"])
                                s_ld.inc(sync.dma_start(out=xt[:, 0, :], in_=x1_til[:, t0 + 2, :]), 16)
                                s_ld.inc(sync.dma_start(out=xt[:, 1, :], in_=x1_til[:, t1 + 2, :]), 16)

                        @block.scalar
                        def _(sc, gp=gp):
                            s_v.wait(sc)
                            sc.activation(out=rsd[:, :, :], in_=rsd[:, :, :], func=AF.Ln)
                            sc.drain()
                            s_a.inc(sc.activation(out=rsd[:, :, :], in_=rsd[:, :, :], func=AF.Exp, scale=-0.5))

                        @block.vector
                        def _(v, gp=gp, b=b):
                            s_a.wait(v)
                            sl = gp % 2
                            if gp >= 1:
                                s_sst.wait_at(v, st[f"axp_{b}_{gp-1}"])
                                s_tr.wait_at(v, st[f"a2trp_{b}_{gp-1}"])
                            v.tensor_scalar(xnb[:, 0, 0, :], aw[:, sl, 0, :], mv[:, 0, 0:1],
                                            rsd[:, 0, :], ALU.subtract, ALU.mult)
                            s_v.inc(v.tensor_scalar(xnb[:, 0, 1, :], aw[:, sl, 1, :], mv[:, 1, 0:1],
                                                    rsd[:, 1, :], ALU.subtract, ALU.mult))
                            st[f"aply_{b}_{gp}"] = s_v.n

                        @block.scalar
                        def _(sc, gp=gp, t0=t0, t1=t1, b=b):
                            s_v.wait(sc)
                            sl = gp % 2
                            s_sst.inc(sc.dma_start(out=x2_til[:, t0, :], in_=aw[:, sl, 0, :]), 16)
                            s_sst.inc(sc.dma_start(out=agxn_til[:, t0, :], in_=xnb[:, 0, 0, :]), 16)
                            s_sst.inc(sc.dma_start(out=x2_til[:, t1, :], in_=aw[:, sl, 1, :]), 16)
                            s_sst.inc(sc.dma_start(out=agxn_til[:, t1, :], in_=xnb[:, 0, 1, :]), 16)
                            st[f"axp_{b}_{gp}"] = s_sst.n

                        lgb = ps_kv[1][0]

                        @block.sync
                        def _(sync, gp=gp, b=b):
                            s_v.wait_at(sync, st[f"aply_{b}_{gp}"])
                            prev = st.get(f"a2mm_{b}_{gp-2}", st.get(f"a2mm_{b-1}_{gp+6}", 0))
                            s_t.wait_at(sync, prev)
                            s_tr.inc(sync.dma_start(out=xnf2[:, gp % 2, 0, :, :], in_=xnb[:, 0, 0, :], transpose=True), 16)
                            s_tr.inc(sync.dma_start(out=xnf2[:, gp % 2, 1, :, :], in_=xnb[:, 0, 1, :], transpose=True), 16)
                            st[f"a2trp_{b}_{gp}"] = s_tr.n

                        if gp >= 2:
                            emit_router_mm(b, gp - 2)

                    emit_router_mm(b, 6)
                    emit_router_mm(b, 7)
                    st["attn_prev_b"] = s_t.n

                for q in range(16):
                    t0, t1 = 2 * q, 2 * q + 1

                    @block.vector
                    def _(v, q=q, t0=t0, t1=t1):
                        if q == 0:
                            s_v.wait(v)
                        if q >= 1:
                            s_a.wait_at(v, st[f"p2exp_{q-1}"])
                            s_t.wait_at(v, st[f"c2mm_{q-1}"])
                        inst = None
                        for par, tt in ((0, t0), (1, t1)):
                            v.max(mx8[:, :], lga[:, q, par * 8:(par + 1) * 8])
                            v.drain()
                            v.max_index(ix8[:, :], mx8[:, :], lga[:, q, par * 8:(par + 1) * 8])
                            v.drain()
                            v.tensor_copy(idxf[:, tt:tt + 1], ix8[:, 0:1])
                            v.tensor_copy(idc[:, :], ix8[:, 0:1])
                            v.drain()
                            v.tensor_scalar(ohtm[:, par, :], iotaef[:, :], idc[:, :], None, ALU.is_equal)
                            inst = v.tensor_scalar(sh8[:, par, :], lga[:, q, par * 8:(par + 1) * 8],
                                                   mx8[:, 0:1], None, ALU.subtract)
                            v.drain()
                        s_v.inc(inst)
                        st[f"p2sh_{q}"] = s_v.n

                    @block.scalar
                    def _(sc, q=q, t0=t0, t1=t1):
                        s_v.wait_at(sc, st[f"p2sh_{q}"])
                        sc.activation(out=she[:, 0, :], in_=sh8[:, 0, :], func=AF.Exp,
                                      accum_out=esum[:, t0:t0 + 1])
                        s_a.inc(sc.activation(out=she[:, 1, :], in_=sh8[:, 1, :], func=AF.Exp,
                                              accum_out=esum[:, t1:t1 + 1]))
                        st[f"p2exp_{q}"] = s_a.n

                    @block.tensor
                    def _(t_, q=q):
                        s_v.wait_at(t_, st[f"p2sh_{q}"])
                        inst = None
                        for par in range(2):
                            inst = t_.matmul(ps_nm[0][0:E, 0:1], ohtm[:, par, :], ones128b[:, :],
                                             start=(q == 0 and par == 0),
                                             stop=(q == 15 and par == 1))
                        s_t.inc(inst)
                        st[f"c2mm_{q}"] = s_t.n

                @block.vector
                def _(v):
                    s_a.wait(v)
                    v.reciprocal(comb[:, :], esum[:, :])
                    v.drain()
                    s_v.inc(v.tensor_copy(idxw[:, :], idxf[:, :]))

                @block.vector
                def _(v):
                    s_t.wait(v)
                    s_v.inc(v.tensor_copy(cnt8[:, :], ps_nm[0][0:E, 0:1]))
                    st["cntev_1"] = s_v.n

                @block.gpsimd
                def _(g):
                    for k in range(8):
                        g.wait_ge(s_sst.h, st[f"axp_{k // 4}_{2 * (k % 4) + 1}"])
                        s_cc[5].inc(g.collective_compute(
                            "AllGather", ALU.bypass, replica_groups=[list(range(NCORE))],
                            ins=[ag_xn_in.ap()[512 * k:512 * (k + 1), :].opt()],
                            outs=[ag_xn_out.ap()[NCORE * 512 * k:NCORE * 512 * (k + 1), :].opt()]))
                    g.wait_ge(s_v.h, st["cntev_1"])
                    s_gd.inc(g.dma_start(out=cnt_dram.ap(), in_=cnt8[:, :]), 16)
                    s_gd.wait(g)
                    s_cc[4].inc(g.collective_compute(
                        "AllGather", ALU.bypass, replica_groups=[list(range(NCORE))],
                        ins=[cnt_dram.ap().opt()], outs=[cnt_ag.ap().opt()]))

        emit_routing(0)
        emit_expert(0)
        emit_return(0)
        emit_attention()

        @block.gpsimd
        def _(g):
            s_cc[2].wait_at(g, 1)
            s_gd.inc(g.dma_start(out=S_dram.ap().rearrange("(a b) one -> a (b one)", b=128),
                                 in_=S_init.ap().rearrange("(a b) one -> a (b one)", b=128)), 16)

        emit_routing(1)
        emit_expert(1)
        emit_return(1)

        @block.sync
        def _(sync):
            s_sst.wait(sync)
            s_gd.wait(sync)
            if DEBUG:
                s_ld.inc(sync.dma_start(out=dbg_x1[:, :], in_=x1_dram.ap()[:, :]), 16)
                s_ld.inc(sync.dma_start(out=dbg_x2[:, :], in_=x2_dram.ap()[:, :]), 16)
                s_ld.inc(sync.dma_start(out=dbg_xn2[:, :], in_=xn2_dram.ap()[:, :]), 16)
                s_ld.inc(sync.dma_start(out=dbg_xn3[:, :], in_=ag_xn_in.ap()[:, :]), 16)
                s_ld.inc(sync.dma_start(out=dbg_y[:, :], in_=y_dram.ap()[:, :]), 16)
                s_ld.inc(sync.dma_start(out=dbg_r[:, 0:NT], in_=rsel[:, :]), 16)
                s_ld.inc(sync.dma_start(out=dbg_r[:, NT:2 * NT], in_=wsc[:, :]), 16)
                s_ld.inc(sync.dma_start(out=dbg_r[:, 2 * NT:3 * NT], in_=gretf[:, :]), 16)
                s_ld.inc(sync.dma_start(out=dbg_r[:, 3 * NT:4 * NT], in_=comb[:, :]), 16)
                s_ld.inc(sync.dma_start(out=dbg_g[:, :], in_=g_u[:, :]), 16)
                s_ld.wait(sync)

    return nc


_CACHED = {}


def kernel(L0_attn_x, L0_attn_q_proj, L0_attn_k_proj, L0_ffn_wg, L0_ffn_W1,
           L0_ffn_b1, L0_ffn_W2, L0_ffn_b2):
    x = np.ascontiguousarray(np.asarray(L0_attn_x, dtype=np.float32))
    qk = np.ascontiguousarray(np.concatenate([np.asarray(L0_attn_q_proj, np.float32),
                                              np.asarray(L0_attn_k_proj, np.float32)], axis=1))
    wg = np.ascontiguousarray(np.asarray(L0_ffn_wg, np.float32))
    W1 = np.asarray(L0_ffn_W1, np.float32)
    b1 = np.asarray(L0_ffn_b1, np.float32)
    W2 = np.asarray(L0_ffn_W2, np.float32)
    b2 = np.asarray(L0_ffn_b2, np.float32)
    ident = np.eye(128, dtype=ml_dtypes.bfloat16)

    if "nc" not in _CACHED:
        _CACHED["nc"] = build()
    nc = _CACHED["nc"]

    in_maps = []
    for c in range(NCORE):
        pm = (np.arange(E) < c).astype(np.float32).reshape(E, 1)
        in_maps.append({
            "x": np.ascontiguousarray(x[2 * c:2 * c + 2].reshape(T4, D)),
            "qk": qk,
            "wg": wg,
            "w1": np.ascontiguousarray(W1[c]),
            "b1": np.ascontiguousarray(b1[c].reshape(32, 128).T),
            "w2": np.ascontiguousarray(W2[c]),
            "b2": np.ascontiguousarray(b2[c].reshape(1, D)),
            "premask": pm,
            "myc": np.full((128, 1), float(c), np.float32),
            "ident": ident,
        })
    res = run_bass_kernel_spmd(nc, in_maps, core_ids=list(range(NCORE)))
    out = np.concatenate([r["xo"].reshape(2, TSEQ, D) for r in res.results], axis=0)
    return out
